# version 29 (final)
# baseline (speedup 1.0000x reference)
import numpy as np
import ml_dtypes
import concourse.bass as bass
import concourse.mybir as mybir
from concourse.bass_utils import run_bass_kernel_spmd
from contextlib import ExitStack

F32 = mybir.dt.float32
BF16 = mybir.dt.bfloat16
AF = mybir.ActivationFunctionType
ALU = mybir.AluOpType
AX = mybir.AxisListType

D = 1024
CH = 8
NTILE = 32
NSLOT = 8
SCALE = np.float32(1.0 / 32.0)
EBIAS = np.float32(-44.0)
f32 = np.float32


class Sched:

    def __init__(self):
        self.ops = {"sync": [], "gpsimd": [], "scalar": [], "vector": [], "tensor": []}
        self.cnt = {}

    def inc(self, name, by=1):
        self.cnt[name] = self.cnt.get(name, 0) + by
        return self.cnt[name]

    def val(self, name):
        return self.cnt.get(name, 0)

    def add(self, eng, waits, body, inc=None, incby=1):
        waits = [w for w in waits if w is not None]
        post = None
        if inc is not None:
            post = (inc, self.inc(inc, incby))

        def fn(e, sems, _w=waits, _b=body, _p=post, _by=incby):
            for s, v in _w:
                if v > 0:
                    e.wait_ge(sems[s], v)
            inst = _b(e, sems)
            if _p is not None:
                assert inst is not None
                inst.then_inc(sems[_p[0]], _by)

        self.ops[eng].append(fn)


def build(taps=()):
    nc = bass.Bass("TRN2", target_bir_lowering=False)

    x_d = nc.dram_tensor("x", [4096, D], F32, kind="ExternalInput")
    xo_d = nc.dram_tensor("x_own", [2048, D], F32, kind="ExternalInput")
    xp_d = nc.dram_tensor("x_prev", [8, D], F32, kind="ExternalInput")
    wq_d = nc.dram_tensor("wq", [D, D], F32, kind="ExternalInput")
    wk_d = nc.dram_tensor("wk", [D, D], F32, kind="ExternalInput")
    wvT_d = nc.dram_tensor("wvT", [D, D], F32, kind="ExternalInput")
    wo_d = nc.dram_tensor("wo", [D, D], F32, kind="ExternalInput")
    wrT_d = nc.dram_tensor("wrT", [D, D], F32, kind="ExternalInput")
    wkTf_d = nc.dram_tensor("wkTf", [D, D], F32, kind="ExternalInput")
    wvTf_d = nc.dram_tensor("wvTf", [D, D], F32, kind="ExternalInput")
    mur_d = nc.dram_tensor("mur", [128, CH], F32, kind="ExternalInput")
    muk_d = nc.dram_tensor("muk", [128, CH], F32, kind="ExternalInput")
    maskt_d = nc.dram_tensor("maskt", [128, 640], F32, kind="ExternalInput")
    out_d = nc.dram_tensor("out", [2048, D], F32, kind="ExternalOutput")
    wvp_d = nc.dram_tensor("wvp_stage", [128, CH * D], BF16, kind="Internal")
    xof_d = nc.dram_tensor("xof_stage", [128, NSLOT * CH * 256], BF16, kind="Internal")
    wrs_d = nc.dram_tensor("wrs_stage", [D, D], BF16, kind="Internal")
    wks_d = nc.dram_tensor("wks_stage", [D, D], BF16, kind="Internal")
    wvs_d = nc.dram_tensor("wvs_stage", [D, D], BF16, kind="Internal")

    tap_d = {}
    for t in taps:
        shp = {"xnfm": [128, CH * 4096], "xntm": [128, NTILE * D],
               "tall": [128, NSLOT * CH * 256], "m": [128, CH * D],
               "wvp": [128, CH * D]}[t]
        tap_d[t] = nc.dram_tensor("tap_" + t, shp, F32, kind="ExternalOutput")

    S = Sched()
    sem_names = ["dx", "dw", "dvs", "dob", "dci0", "dci1", "doc0", "doc1", "dcp", "dxa", "dxb", "dwh0", "dmk", "dci2", "dtx", "vka", "gfn", "dxf", "dnf0", "dnf1", "dwv0", "dmu", "dk0", "dk1", "dwr2", "dvv", "dwv1",
                 "tm0", "tt1", "tq1", "tt2", "tsc", "tav", "tsu", "to", "tdn",
                 "ttc", "tk", "tr", "tf",
                 "se0", "st1", "sa1", "sa2", "sex", "soe", "slc", "srk", "ssg",
                 "ve1", "vl1", "ve2", "vl2", "vmk", "vav", "vrc", "vcst",
                 "vec", "vlc", "vmx", "vfn", "gio"]

    with ExitStack() as ctx:
        sb = lambda name, shape, dt: ctx.enter_context(nc.sbuf_tensor(name, shape, dt))
        ident = sb("ident", [128, 128], BF16)
        pidx = sb("pidx", [128, 1], F32)
        ones_bf = sb("onesbf", [128, 1], BF16)
        maskt = sb("maskt_s", [128, 640], BF16)
        t_all = sb("t_all", [128, NSLOT * CH * 256], BF16)

        e_m = ExitStack()
        m_sb = e_m.enter_context(nc.sbuf_tensor("m_sb", [128, CH * D], BF16))
        xo1 = [e_m.enter_context(nc.sbuf_tensor(f"xo1_{i}", [128, 2 * D], F32)) for i in range(2)]
        col1 = {n: e_m.enter_context(nc.sbuf_tensor("c1" + n, [128, 16], F32))
                for n in ("s1", "s2", "c6", "cv", "u", "rs", "nm")}
        xot = e_m.enter_context(nc.sbuf_tensor("xot", [128, 16 * D], BF16))
        xof = e_m.enter_context(nc.sbuf_tensor("xof", [128, NSLOT * CH * 256], BF16))
        e_a0 = ExitStack()
        wa_sb = e_a0.enter_context(nc.sbuf_tensor("wa", [128, CH * D], BF16))
        wb_sb = e_a0.enter_context(nc.sbuf_tensor("wb", [128, CH * D], BF16))
        wvpe = e_a0.enter_context(nc.sbuf_tensor("wvpe", [128, 2 * D], BF16))
        wc_sb = e_a0.enter_context(nc.sbuf_tensor("wc", [128, CH * D], BF16))
        wd_sb = e_a0.enter_context(nc.sbuf_tensor("wd", [128, CH * D], BF16))
        iota_f = e_a0.enter_context(nc.sbuf_tensor("iotaf", [128, 128], F32))
        e_a0.close()
        e_m.close()

        e_b = ExitStack()
        xn_fm = e_b.enter_context(nc.sbuf_tensor("xn_fm", [128, CH * 4096], BF16))
        xn_tm = e_b.enter_context(nc.sbuf_tensor("xn_tm", [128, NTILE * D], BF16))
        wv_p = e_b.enter_context(nc.sbuf_tensor("wv_p", [128, CH * D], BF16))
        x_grp = [e_b.enter_context(nc.sbuf_tensor(f"x_grp{i}", [128, 2 * D], F32)) for i in range(2)]
        col2 = {n: e_b.enter_context(nc.sbuf_tensor("c2" + n, [128, NTILE], F32))
                for n in ("s1", "s2", "cv", "u", "rs")}
        col2["c6"] = col2["u"]
        col2["nm"] = col2["cv"]
        ptbuf = e_b.enter_context(nc.sbuf_tensor("ptbuf", [128, 3 * 256], BF16))
        avxt = e_b.enter_context(nc.sbuf_tensor("avxt", [128, CH * 256], BF16))
        outb = e_b.enter_context(nc.sbuf_tensor("outb", [128, 2 * D], F32))
        recip = e_b.enter_context(nc.sbuf_tensor("recip", [128, 16], F32))
        e_b.close()

        e_c = ExitStack()
        warr = e_c.enter_context(nc.sbuf_tensor("warr", [128, CH * D], BF16))
        wark = e_c.enter_context(nc.sbuf_tensor("wark", [128, CH * D], BF16))
        warv = e_c.enter_context(nc.sbuf_tensor("warv", [128, CH * D], BF16))
        mur = e_c.enter_context(nc.sbuf_tensor("mur_s", [128, CH], F32))
        muk = e_c.enter_context(nc.sbuf_tensor("muk_s", [128, CH], F32))
        xprev = e_c.enter_context(nc.sbuf_tensor("xprev", [8, D], F32))
        xprevt = e_c.enter_context(nc.sbuf_tensor("xprevt", [8, D], BF16))
        xprevf = e_c.enter_context(nc.sbuf_tensor("xprevf", [128, CH * 8], BF16))
        xoc = [e_c.enter_context(nc.sbuf_tensor(f"xoc{i}", [128, 2 * D], F32)) for i in range(3)]
        parts = e_c.enter_context(nc.sbuf_tensor("parts", [128, 2 * D], F32))
        colc = {n: e_c.enter_context(nc.sbuf_tensor("cc" + n, [128, 20], F32))
                for n in ("s1", "s2", "c6", "cv", "u", "rs", "nm")}
        xnopf_b = [e_c.enter_context(nc.sbuf_tensor(f"xnopf{i}", [128, CH * 257], BF16))
                   for i in range(2)]
        diffb = e_c.enter_context(nc.sbuf_tensor("diffb", [128, CH * 256], BF16))
        xr_sb = [e_c.enter_context(nc.sbuf_tensor(f"xr_sb{i}", [128, CH * 256], BF16)) for i in range(2)]
        xk_sb = [e_c.enter_context(nc.sbuf_tensor(f"xk_sb{i}", [128, CH * 256], BF16)) for i in range(2)]
        rk_sb = e_c.enter_context(nc.sbuf_tensor("rk_sb", [128, CH * 256], BF16))
        kact = [e_c.enter_context(nc.sbuf_tensor(f"kact{i}", [128, CH * 256], BF16)) for i in range(2)]
        r_sb = e_c.enter_context(nc.sbuf_tensor("r_sb", [128, 2 * D], F32))
        xpart = e_c.enter_context(nc.sbuf_tensor("xpart", [128, 2 * D], F32))
        fbuf = e_c.enter_context(nc.sbuf_tensor("fbuf", [128, D], F32))
        outc = [e_c.enter_context(nc.sbuf_tensor(f"outc{i}", [128, 2 * D], F32)) for i in range(2)]
        e_c.close()

        pool = ctx.enter_context(nc.psum_tensor("pool", [128, 4096], F32))
        pSC = [pool[:, 0:512], pool[:, 512:1024]]
        pSCb = [pool[:, 512 * i:512 * (i + 1)].bitcast(BF16) for i in range(6)]
        pAV = pool[:, 1024:3072]
        pAVlo = pool[:, 1024:2048]
        pAVhi = pool[:, 2048:3072]
        pOS = pool[:, 3072:4096]

        sems = {}
        for n in sem_names:
            sems[n] = ctx.enter_context(nc.semaphore("sem_" + n))

        gate = {"sc0": None, "sc1": None, "sc2": None, "sc3": None, "sc4": None, "sc5": None}

        def gwait(key):
            g = gate[key]
            return [g] if g else []

        def ln_group(xslices, xnslices, x3d, cols, k0, xgate, sA, vL, pd=128,
                     apply_extra=None):
            G = len(xslices)
            sA0 = S.val(sA)
            vL0 = S.val(vL)
            cs = lambda n: cols[n][0:pd, k0:k0 + G]
            n = 1024
            for g in range(G):
                S.add("scalar",
                      ([xgate] + ([apply_extra] if apply_extra else [])) if g == 0 else [],
                      lambda e, s, o=xnslices[g], i=xslices[g],
                      a=cols["s2"][0:pd, k0 + g:k0 + g + 1]:
                      e.activation(o, i, AF.Square, accum_out=a), inc=sA)
            S.add("vector", [xgate],
                  lambda e, s, o=cs("s1"), i=x3d: e.reduce_sum(o, i, axis=AX.X), inc=vL)
            S.add("vector", [(vL, vL0 + 1)],
                  lambda e, s, o=cs("c6"), i=cs("s1"): e.tensor_tensor(o, i, i, ALU.mult),
                  inc=vL)
            S.add("vector", [(sA, sA0 + G), (vL, vL0 + 2)],
                  lambda e, s, o=cs("cv"), i=cs("c6"), j2=cs("s2"), sc=-1.0 / n:
                  e.scalar_tensor_tensor(o, i, sc, j2, ALU.mult, ALU.add), inc=vL)
            S.add("scalar", [(vL, vL0 + 3)],
                  lambda e, s, o=cs("u"), i=cs("cv"), sc=f32(1.0 / n):
                  e.activation(o, i, AF.Ln, bias=f32(1e-5), scale=sc), inc=sA)
            S.add("scalar", [(sA, sA0 + G + 1)],
                  lambda e, s, o=cs("rs"), i=cs("u"): e.activation(o, i, AF.Exp, scale=f32(-0.5)),
                  inc=sA)
            S.add("vector", [(sA, sA0 + G + 2)],
                  lambda e, s, o=cs("nm"), i=cs("s1"), r=cs("rs"), sc=-1.0 / n:
                  e.scalar_tensor_tensor(o, i, sc, r, ALU.mult, ALU.mult), inc=vL)
            for g in range(G):
                S.add("scalar", [(vL, vL0 + 4)] if g == 0 else [],
                      lambda e, s, o=xnslices[g], i=xslices[g],
                      b=cols["nm"][0:pd, k0 + g:k0 + g + 1],
                      r=cols["rs"][0:pd, k0 + g:k0 + g + 1]:
                      e.activation(o, i, AF.Identity, bias=b, scale=r), inc=sA)

        def transpose_tile(src, dst_ap, bank, readygate, tsem, vsem, evac_extra=None):
            def tbody(e, s, b=bank, src_=src):
                inst = None
                for cc in range(CH):
                    inst = e.matmul(pSCb[b][:, cc * 128:(cc + 1) * 128],
                                    src_[:, cc * 128:(cc + 1) * 128], ident[:, :],
                                    is_transpose=True, start=(cc == 0), stop=(cc == CH - 1))
                return inst
            S.add("tensor", [readygate, ("vcst", 1)] + gwait("sc%d" % bank), tbody, inc=tsem)
            tv = S.val(tsem)
            S.add("vector", [(tsem, tv)] + ([evac_extra] if evac_extra else []),
                  lambda e, s, o=dst_ap, b=bank:
                  e.tensor_copy(o, pSCb[b][:, 0:CH * 128]), inc=vsem)
            gate["sc%d" % bank] = (vsem, S.val(vsem))

        for hh in range(2):
            for (dst, dr) in ((wa_sb, wq_d), (wb_sb, wk_d)):
                S.add("gpsimd", [],
                      lambda e, s, d=dst, r=dr, h=hh:
                      e.dma_start(d[:, :].rearrange("p (c k) -> p c k", k=D)[:, :, h * 512:(h + 1) * 512],
                                  r[:, :].rearrange("(c p) k -> p c k", p=128)[:, :, h * 512:(h + 1) * 512]),
                      inc="dwh0" if hh == 0 else "dw", incby=16)
        for hh in range(2):
            for (dst, dr) in ((wc_sb, wvT_d), (wd_sb, wo_d)):
                S.add("gpsimd", [],
                      lambda e, s, d=dst, r=dr, h=hh:
                      e.dma_start(d[:, :].rearrange("p (c k) -> p c k", k=D)[:, :, h * 512:(h + 1) * 512],
                                  r[:, :].rearrange("(c p) k -> p c k", p=128)[:, :, h * 512:(h + 1) * 512]),
                      inc="dwv0" if hh == 0 else "dwv1", incby=16)
        S.add("gpsimd", [], lambda e, s: e.dma_start(maskt[:, :], maskt_d[:, :]),
              inc="dmk", incby=16)
        S.add("gpsimd", [], lambda e, s: e.iota(
            iota_f[:, :], [[1, 128]], channel_multiplier=0,
            allow_small_or_imprecise_dtypes=True), inc=None)
        S.add("gpsimd", [], lambda e, s: e.iota(
            pidx[:, :], [[0, 1]], channel_multiplier=1,
            allow_small_or_imprecise_dtypes=True), inc="gio")
        S.add("vector", [("gio", 1)],
              lambda e, s: e.tensor_scalar(ident[:, :], iota_f[:, :], pidx[:, :],
                                           None, ALU.is_equal), inc="vcst")
        S.add("vector", [], lambda e, s: e.memset(ones_bf[:, :], 1.0), inc="vcst")

        ast = {}
        tt1_after = {}

        def a1_load(j):
            sem = "dxa" if j % 2 == 0 else "dxb"
            wt = []
            if j >= 2:
                wt = [("sa1", ast[j - 2]["apply"]), ("vl1", ast[j - 2]["reduce"])]
            S.add("sync", wt,
                  lambda e, s, jj=j:
                  e.dma_start(xo1[jj % 2][:, :].rearrange("p (t k) -> p t k", k=D),
                              xo_d[256 * jj:256 * (jj + 1), :].rearrange("(t p) k -> p t k", p=128)),
                  inc=sem, incby=16)
            ast[j] = {"xg": (sem, S.val(sem))}

        def a1_pieces(j):
            st = ast[j]
            k0 = 2 * j
            cs = lambda n: col1[n][:, k0:k0 + 2]
            bufx = xo1[j % 2]
            for g in range(2):
                S.add("scalar",
                      [st["xg"]] if g == 0 else [],
                      lambda e, s, o=xot[:, (k0 + g) * D:(k0 + g + 1) * D],
                      ii=bufx[:, g * D:(g + 1) * D],
                      a=col1["s2"][:, k0 + g:k0 + g + 1]:
                      e.activation(o, ii, AF.Square, accum_out=a), inc="sa1")
            st["sq"] = S.val("sa1")
            vL0 = S.val("vl1")
            S.add("vector", [st["xg"]],
                  lambda e, s, o=cs("s1"), i=bufx[:, :].rearrange("p (t k) -> p t k", k=D):
                  e.reduce_sum(o, i, axis=AX.X), inc="vl1")
            st["reduce"] = S.val("vl1")
            S.add("vector", [("vl1", vL0 + 1)],
                  lambda e, s, o=cs("c6"), i=cs("s1"): e.tensor_tensor(o, i, i, ALU.mult),
                  inc="vl1")
            S.add("vector", [("sa1", st["sq"]), ("vl1", vL0 + 2)],
                  lambda e, s, o=cs("cv"), i=cs("c6"), j2=cs("s2"):
                  e.scalar_tensor_tensor(o, i, -1.0 / 1024, j2, ALU.mult, ALU.add), inc="vl1")
            st["cvar"] = S.val("vl1")
            S.add("scalar", [("vl1", st["cvar"])],
                  lambda e, s, o=cs("u"), i=cs("cv"):
                  e.activation(o, i, AF.Ln, bias=f32(1e-5), scale=f32(1.0 / 1024)), inc="sa1")
            st["ln"] = S.val("sa1")
            S.add("scalar", [("sa1", st["ln"])],
                  lambda e, s, o=cs("rs"), i=cs("u"): e.activation(o, i, AF.Exp, scale=f32(-0.5)),
                  inc="sa1")
            st["exp"] = S.val("sa1")
            S.add("vector", [("sa1", st["exp"])],
                  lambda e, s, o=cs("nm"), i=cs("s1"), r=cs("rs"):
                  e.scalar_tensor_tensor(o, i, -1.0 / 1024, r, ALU.mult, ALU.mult), inc="vl1")
            st["negm"] = S.val("vl1")
            for g in range(2):
                S.add("scalar", [("vl1", st["negm"])] if g == 0 else [],
                      lambda e, s, o=xot[:, (k0 + g) * D:(k0 + g + 1) * D],
                      ii=bufx[:, g * D:(g + 1) * D],
                      b=col1["nm"][:, k0 + g:k0 + g + 1], r=col1["rs"][:, k0 + g:k0 + g + 1]:
                      e.activation(o, ii, AF.Identity, bias=b, scale=r), inc="sa1")
            st["apply"] = S.val("sa1")
            if j + 2 < NSLOT:
                a1_load(j + 2)

        a1_load(0)
        a1_load(1)

        g = 0
        group_emitters = []
        for ccc in range(CH):
            for hc in range(2):
                def mk_m(c=ccc, h=hc):
                    def em():
                        global_bank = emctx["g"] % 2

                        def mbody(e, s, b=global_bank, c2=c, h2=h):
                            inst = None
                            for er in range(CH):
                                inst = e.matmul(pSC[b][:, 0:512],
                                                wa_sb[:, er * D + c2 * 128: er * D + (c2 + 1) * 128],
                                                wb_sb[:, er * D + h2 * 512: er * D + h2 * 512 + 512],
                                                start=(er == 0), stop=(er == CH - 1))
                            return inst
                        need = [("dwh0", 32)] if (c < 4 and h == 0) else [("dwh0", 32), ("dw", 32)]
                        S.add("tensor", need + gwait("sc%d" % global_bank), mbody, inc="tm0")
                        tv = S.val("tm0")
                        S.add("scalar", [("tm0", tv)],
                              lambda e, s, b=global_bank, c2=c, h2=h:
                              e.copy(m_sb[:, c2 * D + h2 * 512: c2 * D + h2 * 512 + 512],
                                     pSC[b][:, 0:512]), inc="se0")
                        gate["sc%d" % global_bank] = ("se0", S.val("se0"))
                        emctx["g"] += 1
                    return em
                group_emitters.append(mk_m())
        wvT_marker = len(group_emitters)
        for ec in range(CH):
            for oh in range(2):
                def mk_v(c=ec, h=oh):
                    def em():
                        global_bank = emctx["g"] % 2

                        def vbody(e, s, b=global_bank, c2=c, h2=h):
                            inst = None
                            for vr in range(CH):
                                inst = e.matmul(pSC[b][:, 0:512],
                                                wc_sb[:, vr * D + c2 * 128: vr * D + (c2 + 1) * 128],
                                                wd_sb[:, vr * D + h2 * 512: vr * D + h2 * 512 + 512],
                                                start=(vr == 0), stop=(vr == CH - 1))
                            return inst
                        vneed = [("dwv0", 32)] if (c < 4 and h == 0) else \
                            [("dwv0", 32), ("dwv1", 32)]
                        S.add("tensor", vneed + gwait("sc%d" % global_bank),
                              vbody, inc="tm0")
                        tv = S.val("tm0")
                        wt = [("tm0", tv)]
                        if c >= 2 and h == 0:
                            wt.append(("dvs", 16 * c))
                        S.add("scalar", wt,
                              lambda e, s, b=global_bank, sl=c % 2, h2=h:
                              e.copy(wvpe[:, sl * D + h2 * 512: sl * D + h2 * 512 + 512],
                                     pSC[b][:, 0:512]), inc="se0")
                        gate["sc%d" % global_bank] = ("se0", S.val("se0"))
                        emctx["g"] += 1
                        if h == 1:
                            S.add("gpsimd", [("se0", S.val("se0"))],
                                  lambda e, s, c2=c, sl=c % 2:
                                  e.dma_start(wvp_d[:, c2 * D:(c2 + 1) * D],
                                              wvpe[:, sl * D:(sl + 1) * D]),
                                  inc="dvs", incby=16)
                    return em
                group_emitters.append(mk_v())

        def emit_a1_transposes(j):
            st = ast[j]
            for tau in range(2):
                i = 2 * j + tau
                apply_done = ("sa1", st["apply"] - 1 + tau)
                dst_ap = xof[:, 2048 * j:2048 * (j + 1)].rearrange(
                    "p (c q) -> p c q", q=256)[:, :, tau * 128:tau * 128 + 128]
                transpose_tile(xot[:, i * D:(i + 1) * D], dst_ap, i % 2,
                               apply_done, "tt1", "ve1")
            st["evac"] = ("ve1", S.val("ve1"))
            S.add("sync", [st["evac"]],
                  lambda e, s, jj=j:
                  e.dma_start(xof_d[:, 2048 * jj:2048 * (jj + 1)],
                              xof[:, 2048 * jj:2048 * (jj + 1)]),
                  inc="dxf", incby=16)

        def emit_a1_tensor(j):
            st = ast[j]
            evac_done = st["evac"]
            for ccp in range(CH):
                bank = ccp % 2

                def tqb(e, s, b=bank, c=ccp, jj=j):
                    inst = None
                    for cr in range(CH):
                        inst = e.matmul(pSC[b][:, 0:256],
                                        m_sb[:, cr * D + c * 128: cr * D + (c + 1) * 128],
                                        xof[:, 2048 * jj + cr * 256: 2048 * jj + (cr + 1) * 256],
                                        start=(cr == 0), stop=(cr == CH - 1))
                    return inst
                S.add("tensor", [evac_done] + gwait("sc%d" % bank), tqb, inc="tq1")
                tv = S.val("tq1")
                S.add("scalar", [("tq1", tv)],
                      lambda e, s, b=bank, jj=j, c=ccp:
                      e.copy(t_all[:, jj * 2048 + c * 256: jj * 2048 + (c + 1) * 256],
                             pSC[b][:, 0:256]), inc="st1")
                gate["sc%d" % bank] = ("st1", S.val("st1"))

        a1_pieces(0)
        a1_pieces(1)
        emit_a1_transposes(0)
        emit_a1_transposes(1)
        emctx = {"g": 0, "dw_a0": 0}
        for idx, em in enumerate(group_emitters):
            em()
            if idx % 4 == 3 and idx // 4 + 2 < NSLOT:
                a1_pieces(idx // 4 + 2)

        for _j in range(2, NSLOT):
            emit_a1_transposes(_j)
            emit_a1_tensor(_j - 2)
        emit_a1_tensor(NSLOT - 2)
        emit_a1_tensor(NSLOT - 1)
        a1_t_done = ("tq1", S.val("tq1"))
        a1_s_done = [("sa1", S.val("sa1")), ("st1", S.val("st1"))]
        S.add("gpsimd", a1_s_done + [a1_t_done, ("dvs", S.val("dvs")), ("dxf", 16 * NSLOT)],
              lambda e, s: e.dma_start(wv_p[:, :], wvp_d[:, :]), inc="dvs", incby=16)
        wvp_ready = ("dvs", S.val("dvs"))

        exp_of_block = {}
        av_of_block = {}
        pt_gate = [None, None, None]

        ln_state = {}
        sa2_apply = {}

        def emit_load(gi):
            if gi == 0:
                wt = a1_s_done + [a1_t_done, ("vl1", 4 * NSLOT), ("dxf", 16 * NSLOT)]
            elif gi == 1:
                wt = [("dxf", 16 * NSLOT)]
            else:
                st = ln_state[gi - 2]
                wt = [("sa2", st["apply"]), ("vl2", st["reduce"])]
            S.add("sync", wt,
                  lambda e, s, g=gi:
                  e.dma_start(x_grp[g % 2][:, :].rearrange("p (t k) -> p t k", k=D),
                              x_d[256 * g:256 * (g + 1), :].rearrange("(t p) k -> p t k", p=128)),
                  inc="dxa" if gi % 2 == 0 else "dxb", incby=16)
            ln_state[gi] = {"xg": ("dxa" if gi % 2 == 0 else "dxb",
                                   S.val("dxa" if gi % 2 == 0 else "dxb"))}

        def emit_ln_vec(gi):
            st = ln_state[gi]
            buf = x_grp[gi % 2]
            k0 = 2 * gi
            cs = lambda n: col2[n][:, k0:k0 + 2]
            vL0 = S.val("vl2")
            S.add("vector", [st["xg"]],
                  lambda e, s, o=cs("s1"), i=buf[:, :].rearrange("p (t k) -> p t k", k=D):
                  e.reduce_sum(o, i, axis=AX.X), inc="vl2")
            st["reduce"] = S.val("vl2")
            S.add("vector", [("vl2", vL0 + 1)],
                  lambda e, s, o=cs("c6"), i=cs("s1"): e.tensor_tensor(o, i, i, ALU.mult),
                  inc="vl2")
            S.add("vector", [("sa2", st["sq"]), ("vl2", vL0 + 2)],
                  lambda e, s, o=cs("cv"), i=cs("c6"), j2=cs("s2"):
                  e.scalar_tensor_tensor(o, i, -1.0 / 1024, j2, ALU.mult, ALU.add), inc="vl2")
            st["cvar"] = S.val("vl2")

        def emit_ln_sq(gi):
            st = ln_state[gi]
            buf = x_grp[gi % 2]
            k0 = 2 * gi
            for g in range(2):
                i = k0 + g
                S.add("scalar", [st["xg"], ("tt2", S.val("tt2"))] if g == 0 else [],
                      lambda e, s, o=xn_tm[:, i * D:(i + 1) * D], ii=buf[:, g * D:(g + 1) * D],
                      a=col2["s2"][:, i:i + 1]:
                      e.activation(o, ii, AF.Square, accum_out=a), inc="sa2")
            st["sq"] = S.val("sa2")

        def emit_ln_s_ln(gi):
            st = ln_state[gi]
            k0 = 2 * gi
            cs = lambda n: col2[n][:, k0:k0 + 2]
            S.add("scalar", [("vl2", st["cvar"])],
                  lambda e, s, o=cs("u"), i=cs("cv"):
                  e.activation(o, i, AF.Ln, bias=f32(1e-5), scale=f32(1.0 / 1024)), inc="sa2")
            st["ln"] = S.val("sa2")

        def emit_ln_s_exp(gi):
            st = ln_state[gi]
            k0 = 2 * gi
            cs = lambda n: col2[n][:, k0:k0 + 2]
            S.add("scalar", [("sa2", st["ln"])],
                  lambda e, s, o=cs("rs"), i=cs("u"): e.activation(o, i, AF.Exp, scale=f32(-0.5)),
                  inc="sa2")

        def emit_ln_negm(gi):
            st = ln_state[gi]
            k0 = 2 * gi
            cs = lambda n: col2[n][:, k0:k0 + 2]
            S.add("vector", [("sa2", st["exp"])],
                  lambda e, s, o=cs("nm"), i=cs("s1"), r=cs("rs"):
                  e.scalar_tensor_tensor(o, i, -1.0 / 1024, r, ALU.mult, ALU.mult), inc="vl2")
            st["negm"] = S.val("vl2")

        def emit_ln_apply(gi):
            st = ln_state[gi]
            buf = x_grp[gi % 2]
            k0 = 2 * gi
            for g in range(2):
                i = k0 + g
                S.add("scalar", [("vl2", st["negm"])] if g == 0 else [],
                      lambda e, s, o=xn_tm[:, i * D:(i + 1) * D], ii=buf[:, g * D:(g + 1) * D],
                      b=col2["nm"][:, i:i + 1], r=col2["rs"][:, i:i + 1]:
                      e.activation(o, ii, AF.Identity, bias=b, scale=r), inc="sa2")
            st["apply"] = S.val("sa2")
            sa2_apply[gi] = S.val("sa2")

        def emit_transposes(j):
            gate["sc4"] = ("vav", S.val("vav"))
            gate["sc5"] = ("vav", S.val("vav"))
            for i in range(4 * j, 4 * j + 4):
                gi = i // 2
                apply_done = ("sa2", sa2_apply[gi] - 1 + (i % 2))
                dst_ap = xn_fm[:, :].rearrange("p (c t) -> p c t", t=4096)[:, :, i * 128:(i + 1) * 128]
                transpose_tile(xn_tm[:, i * D:(i + 1) * D], dst_ap, 4 + i % 2,
                               apply_done, "tt2", "ve2")

        def emit_B(j, inject):
            E = 4 * j + 4

            def emit_scores(b):
                bank = b % 2

                def scb(e, s, bk=bank, jj=j, bb=b):
                    inst = None
                    for cc in range(CH):
                        inst = e.matmul(pSC[bk][:, 0:256],
                                        xn_fm[:, cc * 4096 + 128 * bb: cc * 4096 + 128 * (bb + 1)],
                                        t_all[:, jj * 2048 + cc * 256: jj * 2048 + (cc + 1) * 256],
                                        start=(cc == 0), stop=(cc == CH - 1))
                    return inst
                S.add("tensor", [("ve2", ve2_of[j])] + gwait("sc%d" % bank), scb, inc="tsc")
                tscv = S.val("tsc")
                k = b - 4 * j
                if k >= 0:
                    S.add("vector", [("tsc", tscv), ("dmk", 16)],
                          lambda e, s, bk=bank, kk=k:
                          e.tensor_tensor(pSC[bk][:, 0:256], pSC[bk][:, 0:256],
                                          maskt[:, 384 - 128 * kk: 640 - 128 * kk], ALU.add),
                          inc="vmk")
                    exp_w = [("vmk", S.val("vmk"))]
                else:
                    exp_w = [("tsc", tscv)]
                pslot = b % 3
                if pt_gate[pslot] is not None:
                    exp_w = exp_w + [pt_gate[pslot]]
                S.add("scalar", exp_w,
                      lambda e, s, bk=bank, ps=pslot:
                      e.activation(ptbuf[:, ps * 256:(ps + 1) * 256], pSC[bk][:, 0:256],
                                   AF.Exp, bias=EBIAS, scale=SCALE), inc="sex")
                exp_of_block[(j, b)] = S.val("sex")
                gate["sc%d" % bank] = ("sex", S.val("sex"))

            def emit_sums_avxt(b):
                pslot = b % 3
                w0 = [("sex", exp_of_block[(j, b)])]
                if b == 0 and j > 0:
                    w0 += [("soe", 2 * j), ("vav", 2 * j)]

                def sums_avxt(e, s, jj=j, bb=b, ps=pslot, first=(b == 0), last=(b == E - 1)):
                    for qt in range(2):
                        e.matmul(pOS[:, qt:qt + 1],
                                 ptbuf[:, ps * 256 + qt * 128: ps * 256 + (qt + 1) * 128],
                                 ones_bf[:, :],
                                 start=(first and qt == 0), stop=(last and qt == 1))
                    inst = None
                    for cc in range(CH):
                        inst = e.matmul(pAV[:, cc * 256:(cc + 1) * 256],
                                        xn_tm[:, bb * D + cc * 128: bb * D + (cc + 1) * 128],
                                        ptbuf[:, ps * 256:(ps + 1) * 256],
                                        start=(first and (cc % 2 == 0)),
                                        stop=(last and (cc % 2 == 1)))
                    return inst
                S.add("tensor", w0, sums_avxt, inc="tav")
                av_of_block[(j, b)] = S.val("tav")
                pt_gate[pslot] = ("tav", S.val("tav"))

            for b in range(E):
                emit_scores(b)
                if b < len(inject):
                    inject[b]()
                if b >= 2:
                    emit_sums_avxt(b - 2)
            for fn in inject[E:]:
                fn()
            emit_sums_avxt(E - 2)
            emit_sums_avxt(E - 1)

            S.add("tensor", [], lambda e, s: e.nop(), inc="tsu")
            for qt in range(2):
                S.add("vector", [("tav", av_of_block[(j, E - 1)])] if qt == 0 else [],
                      lambda e, s, q=qt:
                      e.tensor_copy(
                          avxt[:, :].rearrange("p (c q) -> p c q", q=256)[:, :, q * 128:(q + 1) * 128],
                          pAV[:, :].rearrange("p (c q) -> p c q", q=256)[:, :, q * 128:(q + 1) * 128]),
                      inc="vav")
            vav_q = S.val("vav")
            S.add("vector", [("tsu", S.val("tsu"))],
                  lambda e, s, jj=j: e.reciprocal(recip[:, 2 * jj:2 * jj + 2], pOS[:, 0:2]),
                  inc="vrc")
            for qt in range(2):
                tgt = pOS if qt == 0 else pAVlo
                w = [("vav", vav_q - (1 - qt)), ("vrc", S.val("vrc"))]
                if j == 0 and qt == 0:
                    w.append(wvp_ready)

                def ob(e, s, t=tgt, q=qt):
                    inst = None
                    for oh in range(2):
                        for cc in range(CH):
                            inst = e.matmul(t[:, oh * 512:(oh + 1) * 512],
                                            avxt[:, cc * 256 + q * 128: cc * 256 + (q + 1) * 128],
                                            wv_p[:, cc * D + oh * 512: cc * D + oh * 512 + 512],
                                            start=(cc == 0), stop=(cc == CH - 1))
                    return inst
                S.add("tensor", w, ob, inc="to")
                tov = S.val("to")
                wse = [("to", tov), ("vrc", S.val("vrc")), ("dob", 16 * j)]
                S.add("scalar", wse,
                      lambda e, s, t=tgt, q=qt, jj=j:
                      e.activation(outb[:, q * D:(q + 1) * D],
                                   t[:, 0:D], AF.Identity, scale=recip[:, 2 * jj + q: 2 * jj + q + 1]),
                      inc="soe")
            S.add("gpsimd", [("soe", S.val("soe"))],
                  lambda e, s, jj=j:
                  e.dma_start(out_d[256 * jj:256 * (jj + 1), :].rearrange("(t p) k -> p t k", p=128),
                              outb[:, :].rearrange("p (t k) -> p t k", k=D)),
                  inc="dob", incby=16)

        ve2_of = {}
        emit_load(0)
        emit_load(1)
        for j in range(NSLOT):
            ga, gb = 2 * j, 2 * j + 1
            emit_ln_sq(ga)
            emit_ln_sq(gb)
            ln_state[ga]["exp"] = S.val("sa2") + 2
            ln_state[gb]["exp"] = S.val("sa2") + 4
            emit_ln_vec(ga)
            emit_ln_vec(gb)
            if j + 1 < NSLOT:
                pass
            inject = [lambda a=ga: emit_ln_s_ln(a),
                      lambda a=ga: emit_ln_s_exp(a),
                      lambda a=ga: emit_ln_negm(a),
                      lambda a=gb: emit_ln_s_ln(a),
                      lambda a=gb: emit_ln_s_exp(a),
                      lambda a=gb: emit_ln_negm(a),
                      lambda a=ga: emit_ln_apply(a),
                      lambda a=gb: emit_ln_apply(a)]
            if j == 0:
                for fn in inject:
                    fn()
                emit_transposes(0)
                ve2_of[0] = S.val("ve2")
                if NSLOT > 1:
                    emit_load(2)
                    emit_load(3)
                continue
            emit_B(j - 1, inject)
            if j == 1:
                for (sd, rd) in ((wrs_d, wrT_d), (wks_d, wkTf_d), (wvs_d, wvTf_d)):
                    S.add("gpsimd", [], lambda e, s, d=sd, r=rd: e.dma_start(d[:, :], r[:, :]),
                          inc="dtx", incby=16)
            emit_transposes(j)
            ve2_of[j] = S.val("ve2")
            if j + 1 < NSLOT:
                emit_load(2 * (j + 1))
                emit_load(2 * (j + 1) + 1)
        emit_B(NSLOT - 1, [])

        sc_all = ("tsc", S.val("tsc"))
        av_all = ("tav", S.val("tav"))
        S.add("tensor", [], lambda e, s: e.nop(), inc="tdn")
        b_t_done = ("tdn", S.val("tdn"))
        b_s_done = ("soe", S.val("soe"))
        b_v_done = ("vav", S.val("vav"))

        for t in taps:
            src = {"xnfm": xn_fm, "xntm": xn_tm, "tall": t_all, "m": m_sb, "wvp": wv_p}[t]
            S.add("gpsimd", [b_t_done, b_s_done],
                  lambda e, s, d=tap_d[t], i=src: e.dma_start(d[:, :], i[:, :]),
                  inc="dw", incby=16)

        tap_dw = ("dw", S.val("dw"))
        S.add("gpsimd", [sc_all, tap_dw], lambda e, s: e.dma_start(mur[:, :], mur_d[:, :]),
              inc="dmu", incby=16)
        S.add("gpsimd", [sc_all, tap_dw], lambda e, s: e.dma_start(muk[:, :], muk_d[:, :]),
              inc="dmu", incby=16)
        cmu = ("dmu", 32)
        cwk = []
        for hh in range(2):
            S.add("gpsimd", [sc_all, tap_dw, ("dtx", 48)],
                  lambda e, s, h=hh:
                  e.dma_start(wark[:, :].rearrange("p (c k) -> p c k", k=D)[:, :, h * 512:(h + 1) * 512],
                              wks_d[:, :].rearrange("(c p) k -> p c k", p=128)[:, :, h * 512:(h + 1) * 512]),
                  inc="dk%d" % hh, incby=16)
            cwk.append(("dk%d" % hh, 16))
        for (dst, dr, sm) in ((warr, wrs_d, "dwr2"), (warv, wvs_d, "dvv")):
            S.add("gpsimd", [sc_all, tap_dw, ("dtx", 48)],
                  lambda e, s, d=dst, r=dr:
                  e.dma_start(d[:, :].rearrange("p (c k) -> p c k", k=D),
                              r[:, :].rearrange("(c p) k -> p c k", p=128)),
                  inc=sm, incby=16)
        cwr = ("dwr2", 16)
        cwv = ("dvv", 16)
        S.add("sync", [sc_all, tap_dw],
              lambda e, s: e.dma_start(xprev[:, :], xp_d[:, :]), inc="dx", incby=16)
        cw_done = ("dw", S.val("dw"))

        xpg = ("dx", S.val("dx"))
        ln_group([xprev[:, :]], [xprevt[:, :]], xprev[:, :].rearrange("p (t k) -> p t k", k=D), colc, 16, xpg, "slc", "vlc", pd=8)
        prev_ap = xprevf[:, :].rearrange("p (c q) -> p c q", q=8)

        def prevtb(e, s):
            inst = None
            for cc in range(CH):
                inst = e.matmul(pSCb[0][:, cc * 8:(cc + 1) * 8],
                                xprevt[:, cc * 128:(cc + 1) * 128], ident[0:8, 0:8],
                                is_transpose=True, start=(cc == 0), stop=(cc == CH - 1))
            return inst
        S.add("tensor", [("slc", S.val("slc"))] + gwait("sc0"), prevtb, inc="ttc")
        S.add("vector", [("ttc", S.val("ttc"))],
              lambda e, s: e.tensor_copy(xprevf[:, :], pSCb[0][:, 0:CH * 8]), inc="vec")
        prevf_done = ("vec", S.val("vec"))
        gate["sc0"] = ("vec", S.val("vec"))

        vfn_after = {}
        tf_after = {}
        vka_of = {}
        xpart_of = {}
        cst = {}
        mix_done_of = {}
        ld_of = {}
        vec_after = {-1: ("vec", S.val("vec"))}
        ttc_after = {-1: ("ttc", S.val("ttc"))}

        def c_load(j):
            buf = j % 3
            if j < 3:
                wl = [av_all, tap_dw]
            else:
                wl = [("vfn", xpart_of[j - 3])]
            S.add("sync", wl,
                  lambda e, s, jj=j, b=buf:
                  e.dma_start(xoc[b][:, :].rearrange("p (t k) -> p t k", k=D),
                              xo_d[256 * jj:256 * (jj + 1), :].rearrange("(t p) k -> p t k", p=128)),
                  inc="dci%d" % (j % 3), incby=16)
            ld_of[j] = ("dci%d" % (j % 3), S.val("dci%d" % (j % 3)))
            cst[j] = {"xg": ld_of[j]}

        def c_feed(j):
            st = cst[j]
            buf = j % 2
            if j < 2:
                fw = [av_all, tap_dw]
            else:
                fw = [mix_done_of[j - 2]]
            S.add("sync", fw,
                  lambda e, s, jj=j, b=buf:
                  e.dma_start(
                      xnopf_b[b][:, :].rearrange("p (c q) -> p c q", q=257)[:, :, 1:257],
                      xof_d[:, 2048 * jj:2048 * (jj + 1)].rearrange("p (c q) -> p c q", q=256)),
                  inc="dnf%d" % buf, incby=16)
            st["feed"] = ("dnf%d" % buf, S.val("dnf%d" % buf))
            S.add("vector", [prevf_done],
                  lambda e, s, jj=j, b=buf:
                  e.tensor_copy(xnopf_b[b][:, :].rearrange("p (c q) -> p c q", q=257)[:, :, 0:1],
                                prev_ap[:, :, jj:jj + 1]), inc="vec")
            st["pcol"] = ("vec", S.val("vec"))

        def c_mixes(j):
            st = cst[j]
            buf = j % 2
            xnopf = xnopf_b[buf]
            mix_w = [cmu, st["feed"], st["pcol"]]
            if j >= 1 and mix_done_of.get(j - 1):
                mix_w.append(mix_done_of[j - 1])
            if j >= 2 and (j - 2) in tf_after:
                mix_w.append(tf_after[j - 2])
            cur = xnopf[:, :].rearrange("p (c q) -> p c q", q=257)[:, :, 1:257]
            prv = xnopf[:, :].rearrange("p (c q) -> p c q", q=257)[:, :, 0:256]
            d3 = diffb[:, :].rearrange("p (c q) -> p c q", q=256)
            xk3 = xk_sb[buf][:, :].rearrange("p (c q) -> p c q", q=256)
            xr3 = xr_sb[buf][:, :].rearrange("p (c q) -> p c q", q=256)
            mukb = muk[:, :].rearrange("p (c one) -> p c one", one=1).broadcast_to([128, CH, 256])
            murb = mur[:, :].rearrange("p (c one) -> p c one", one=1).broadcast_to([128, CH, 256])
            S.add("vector", mix_w,
                  lambda e, s: e.tensor_tensor(d3, cur, prv, ALU.subtract), inc="vmx")
            S.add("vector", [("vmx", S.val("vmx"))],
                  lambda e, s: e.tensor_tensor(xk3, d3, mukb, ALU.mult), inc="vmx")
            S.add("vector", [("vmx", S.val("vmx"))],
                  lambda e, s: e.tensor_tensor(xk3, xk3, prv, ALU.add), inc="vmx")
            xk_done_of = ("vmx", S.val("vmx"))
            S.add("vector", [("vmx", S.val("vmx") - 1)],
                  lambda e, s: e.tensor_tensor(xr3, d3, murb, ALU.mult), inc="vmx")
            S.add("vector", [("vmx", S.val("vmx"))],
                  lambda e, s: e.tensor_tensor(xr3, xr3, prv, ALU.add), inc="vmx")
            mix_done_of[j] = ("vmx", S.val("vmx"))
            cst[j]["xk_done"] = xk_done_of

        def c_stage_b(j, inject, midfns=()):
            buf = j % 2
            mix_done = mix_done_of[j]
            ld = ld_of[j]
            pw = ([("vfn", xpart_of[j - 1])] if j >= 1 else [av_all, tap_dw]) + [("dob", 16 * (j + 1))]
            S.add("sync", pw,
                  lambda e, s, jj=j:
                  e.dma_start(parts[:, :].rearrange("p (t k) -> p t k", k=D),
                              out_d[256 * jj:256 * (jj + 1), :].rearrange("(t p) k -> p t k", p=128)),
                  inc="dcp", incby=16)
            pld = ("dcp", S.val("dcp"))
            ip = 0

            def inj():
                nonlocal ip
                if ip < len(inject):
                    inject[ip]()
                    ip += 1

            if j == 0:
                gate["sc2"] = b_s_done
                gate["sc3"] = b_s_done
            for dc in range(CH):
                bank = dc % 4
                pb = pool[:, 512 * bank:512 * bank + 512]

                def kb(e, s, b=pb, d=dc, bu=buf):
                    inst = None
                    for cr in range(CH):
                        inst = e.matmul(b[:, 0:256],
                                        wark[:, cr * D + d * 128: cr * D + (d + 1) * 128],
                                        xk_sb[bu][:, cr * 256:(cr + 1) * 256],
                                        start=(cr == 0), stop=(cr == CH - 1))
                    return inst
                S.add("tensor", [cst[j]["xk_done"], cwk[0] if dc < 4 else cwk[1]] + gwait("sc%d" % bank), kb, inc="tk")
                S.add("scalar", [("tk", S.val("tk"))],
                      lambda e, s, b=pb, d=dc:
                      e.activation(rk_sb[:, d * 256:(d + 1) * 256], b[:, 0:256], AF.Relu),
                      inc="srk")
                gate["sc%d" % bank] = ("srk", S.val("srk"))
                inj()
            sq_w = [("srk", S.val("srk"))]
            if j >= 2:
                sq_w.append(tf_after[j - 2])
            S.add("scalar", sq_w,
                  lambda e, s, bu=buf: e.activation(kact[bu][:, :], rk_sb[:, :], AF.Square),
                  inc="srk")
            kact_done = ("srk", S.val("srk"))
            for qt in range(2):
                tgt = pOS if qt == 0 else pAVlo
                w = [mix_done, cwr]
                if j > 0:
                    w.append(("vfn", vfn_after[j - 1] - 1) if qt == 0 else ("ssg", 2 * j))
                elif qt == 0:
                    w.append(b_s_done)
                if qt == 1:
                    w += gwait("sc2") + gwait("sc3")

                def rb(e, s, t=tgt, q=qt, bu=buf):
                    inst = None
                    for oh in range(2):
                        for cr in range(CH):
                            inst = e.matmul(t[:, oh * 512:(oh + 1) * 512],
                                            xr_sb[bu][:, cr * 256 + q * 128: cr * 256 + (q + 1) * 128],
                                            warr[:, cr * D + oh * 512: cr * D + oh * 512 + 512],
                                            start=(cr == 0), stop=(cr == CH - 1))
                    return inst
                S.add("tensor", w, rb, inc="tr")
                if qt == 0:
                    while ip < len(inject):
                        inj()
                    for fn in midfns:
                        fn()
                sg_w = [("tr", S.val("tr"))]
                if j >= 1:
                    sg_w.append(("vfn", vfn_after[j - 1] - 1))
                S.add("scalar", sg_w,
                      lambda e, s, t=tgt, q=qt:
                      e.activation(r_sb[:, q * D:(q + 1) * D], t[:, 0:D], AF.Sigmoid), inc="ssg")
                if qt == 1:
                    gate["sc2"] = ("ssg", S.val("ssg"))
                    gate["sc3"] = ("ssg", S.val("ssg"))
            for qt in range(2):
                tgt = pAVhi if qt == 0 else pOS
                w = [kact_done, cwv]
                if qt == 0 and j > 0:
                    w.append(("vfn", vfn_after[j - 1] - 3))
                if qt == 0 and j == 0:
                    w.append(b_v_done)
                if qt == 1:
                    w.append(("ssg", S.val("ssg") - 1))

                def fb3(e, s, t=tgt, q=qt, bu=buf):
                    inst = None
                    for oh in range(2):
                        for dc in range(CH):
                            inst = e.matmul(t[:, oh * 512:(oh + 1) * 512],
                                            kact[bu][:, dc * 256 + q * 128: dc * 256 + (q + 1) * 128],
                                            warv[:, dc * D + oh * 512: dc * D + oh * 512 + 512],
                                            start=(dc == 0), stop=(dc == CH - 1))
                    return inst
                S.add("tensor", w, fb3, inc="tf")
            tf_q0 = S.val("tf") - 1
            tf_q1 = S.val("tf")
            tf_after[j] = ("tf", tf_q1)
            S.add("vector", [ld, pld, ("vfn", S.val("vfn"))],
                  lambda e, s, b=j % 3:
                  e.tensor_tensor(xpart[:, :], parts[:, :], xoc[b][:, :], ALU.add),
                  inc="vfn")
            xpart_of[j] = S.val("vfn")
            for qt in range(2):
                src = pAVhi if qt == 0 else pOS
                w = [("tf", tf_q0 if qt == 0 else tf_q1), ("ssg", S.val("ssg") - (1 - qt)),
                     ("vfn", S.val("vfn"))]
                if j >= 2:
                    w.append(("doc%d" % (j % 2), 16 * (j // 2)))
                S.add("vector", w,
                      lambda e, s, t=src, q=qt:
                      e.tensor_tensor(fbuf[:, :], t[:, 0:D], r_sb[:, q * D:(q + 1) * D],
                                      ALU.mult), inc="vfn")
                S.add("vector", [("vfn", S.val("vfn"))],
                      lambda e, s, q=qt, jj=j:
                      e.tensor_tensor(outc[jj % 2][:, q * D:(q + 1) * D], fbuf[:, :],
                                      xpart[:, q * D:(q + 1) * D], ALU.add), inc="vfn")
            vfn_after[j] = S.val("vfn")
            S.add("gpsimd", [("vfn", S.val("vfn"))],
                  lambda e, s, jj=j:
                  e.dma_start(out_d[256 * jj:256 * (jj + 1), :].rearrange("(t p) k -> p t k", p=128),
                              outc[jj % 2][:, :].rearrange("p (t k) -> p t k", k=D)),
                  inc="doc%d" % (j % 2), incby=16)

        c_load(0)
        c_load(1)
        c_feed(0)
        c_feed(1)
        c_mixes(0)
        c_mixes(1)
        for j in range(NSLOT):
            nx = j + 2
            if nx < NSLOT:
                c_load(nx)
                c_feed(nx)
            if j + 1 < NSLOT and j + 1 >= 2:
                midfns = [lambda a=j + 1: c_mixes(a)]
            else:
                midfns = []
            c_stage_b(j, [], midfns)

        S.add("sync", [("doc0", 64), ("doc1", 64), ("dob", 16 * NSLOT)],
              lambda e, s: e.nop(), inc=None)

        with nc.Block() as block:
            def runner(oplist):
                def f(eng):
                    for op in oplist:
                        op(eng, sems)
                return f
            block.sync(runner(S.ops["sync"]))
            block.gpsimd(runner(S.ops["gpsimd"]))
            block.scalar(runner(S.ops["scalar"]))
            block.vector(runner(S.ops["vector"]))
            block.tensor(runner(S.ops["tensor"]))

    return nc


def _host_inputs(x, Wq, Wk, Wv, Wo, W_r, W_k, W_v, mu_r, mu_k):
    x = np.asarray(x, dtype=np.float32)
    c = np.ascontiguousarray
    shared = {
        "wq": c(np.asarray(Wq, np.float32).T),
        "wk": c(np.asarray(Wk, np.float32).T),
        "wvT": c(np.asarray(Wv, np.float32).T),
        "wo": c(np.asarray(Wo, np.float32)),
        "wrT": c(np.asarray(W_r, np.float32).T),
        "wkTf": c(np.asarray(W_k, np.float32).T),
        "wvTf": c(np.asarray(W_v, np.float32).T),
        "mur": c(np.asarray(mu_r, np.float32).reshape(8, 128).T),
        "muk": c(np.asarray(mu_k, np.float32).reshape(8, 128).T),
    }
    in_maps = []
    own_rows_all = []
    for core in range(8):
        b, par = core // 2, core % 2
        xb = c(x[b])
        rows = np.concatenate([np.arange(512 * j + 256 * par, 512 * j + 256 * par + 256)
                               for j in range(8)])
        own_rows_all.append(rows)
        x_own = c(xb[rows])
        xprev = np.zeros((8, 1024), np.float32)
        for j in range(8):
            t = 512 * j + 256 * par - 1
            if t >= 0:
                xprev[j] = xb[t]
        pp = np.arange(128)[:, None]
        cc_ = np.arange(640)[None, :]
        mask = np.where(cc_ < pp + 384 - 256 * par, -1e9, 0.0).astype(np.float32)
        im = dict(shared)
        im["x"] = xb
        im["x_own"] = x_own
        im["x_prev"] = c(xprev)
        im["maskt"] = c(mask)
        in_maps.append(im)
    return in_maps, own_rows_all


def kernel(L0_attn_x, L0_attn_Wq, L0_attn_Wk, L0_attn_Wv, L0_attn_Wo,
           L0_ffn_W_r, L0_ffn_W_k, L0_ffn_W_v, L0_ffn_mu_r, L0_ffn_mu_k):
    in_maps, own_rows = _host_inputs(L0_attn_x, L0_attn_Wq, L0_attn_Wk, L0_attn_Wv,
                                     L0_attn_Wo, L0_ffn_W_r, L0_ffn_W_k, L0_ffn_W_v,
                                     L0_ffn_mu_r, L0_ffn_mu_k)
    nc = build()
    res = run_bass_kernel_spmd(nc, in_maps, core_ids=list(range(8)))
    out = np.empty((4, 4096, 1024), np.float32)
    for core in range(8):
        b = core // 2
        out[b][own_rows[core]] = res.results[core]["out"]
    return out


# version 30
# speedup vs baseline: 1.1389x; 1.1389x over previous
import numpy as np
import ml_dtypes
import concourse.bass as bass
import concourse.mybir as mybir
from concourse.bass_utils import run_bass_kernel_spmd
from contextlib import ExitStack

F32 = mybir.dt.float32
BF16 = mybir.dt.bfloat16
AF = mybir.ActivationFunctionType
ALU = mybir.AluOpType
AX = mybir.AxisListType

D = 1024
CH = 8
NTILE = 32
NSLOT = 8
SCALE = np.float32(1.0 / 32.0)
EBIAS = np.float32(-44.0)
f32 = np.float32


class Sched:

    def __init__(self):
        self.ops = {"sync": [], "gpsimd": [], "scalar": [], "vector": [], "tensor": []}
        self.cnt = {}

    def inc(self, name, by=1):
        self.cnt[name] = self.cnt.get(name, 0) + by
        return self.cnt[name]

    def val(self, name):
        return self.cnt.get(name, 0)

    def add(self, eng, waits, body, inc=None, incby=1):
        waits = [w for w in waits if w is not None]
        post = None
        if inc is not None:
            post = (inc, self.inc(inc, incby))

        def fn(e, sems, _w=waits, _b=body, _p=post, _by=incby):
            for s, v in _w:
                if v > 0:
                    e.wait_ge(sems[s], v)
            inst = _b(e, sems)
            if _p is not None:
                assert inst is not None
                inst.then_inc(sems[_p[0]], _by)

        self.ops[eng].append(fn)


def build(taps=()):
    nc = bass.Bass("TRN2", target_bir_lowering=False)

    x_d = nc.dram_tensor("x", [4096, D], F32, kind="ExternalInput")
    xo_d = nc.dram_tensor("x_own", [2048, D], F32, kind="ExternalInput")
    xp_d = nc.dram_tensor("x_prev", [8, D], F32, kind="ExternalInput")
    wq_d = nc.dram_tensor("wq", [D, D], F32, kind="ExternalInput")
    wk_d = nc.dram_tensor("wk", [D, D], F32, kind="ExternalInput")
    wvT_d = nc.dram_tensor("wvT", [D, D], F32, kind="ExternalInput")
    wo_d = nc.dram_tensor("wo", [D, D], F32, kind="ExternalInput")
    wrT_d = nc.dram_tensor("wrT", [D, D], F32, kind="ExternalInput")
    wkTf_d = nc.dram_tensor("wkTf", [D, D], F32, kind="ExternalInput")
    wvTf_d = nc.dram_tensor("wvTf", [D, D], F32, kind="ExternalInput")
    mur_d = nc.dram_tensor("mur", [128, CH], F32, kind="ExternalInput")
    muk_d = nc.dram_tensor("muk", [128, CH], F32, kind="ExternalInput")
    maskt_d = nc.dram_tensor("maskt", [128, 640], F32, kind="ExternalInput")
    out_d = nc.dram_tensor("out", [2048, D], F32, kind="ExternalOutput")
    wvp_d = nc.dram_tensor("wvp_stage", [128, CH * D], BF16, kind="Internal")
    xof_d = nc.dram_tensor("xof_stage", [128, NSLOT * CH * 256], BF16, kind="Internal")
    wrs_d = nc.dram_tensor("wrs_stage", [D, D], BF16, kind="Internal")
    wks_d = nc.dram_tensor("wks_stage", [D, D], BF16, kind="Internal")
    wvs_d = nc.dram_tensor("wvs_stage", [D, D], BF16, kind="Internal")

    tap_d = {}
    for t in taps:
        shp = {"xnfm": [128, CH * 4096], "xntm": [128, NTILE * D],
               "tall": [128, NSLOT * CH * 256], "m": [128, CH * D],
               "wvp": [128, CH * D]}[t]
        tap_d[t] = nc.dram_tensor("tap_" + t, shp, F32, kind="ExternalOutput")

    S = Sched()
    sem_names = ["dx", "dw", "dvs", "dob", "dci0", "dci1", "doc0", "doc1", "dcp", "dxa", "dxb", "dwh0", "dmk", "dci2", "dtx", "vka", "gfn", "dxf", "dnf0", "dnf1", "dwv0", "dmu", "dk0", "dk1", "dwr2", "dvv", "dwv1",
                 "tm0", "tt1", "tq1", "tt2", "tsc", "tav", "tsu", "to", "tdn",
                 "ttc", "tk", "tr", "tf",
                 "se0", "st1", "sa1", "sa2", "sex", "soe", "slc", "srk", "ssg",
                 "ve1", "vl1", "ve2", "vl2", "vmk", "vav", "vrc", "vcst",
                 "vec", "vlc", "vmx", "vfn", "gio"]

    with ExitStack() as ctx:
        sb = lambda name, shape, dt: ctx.enter_context(nc.sbuf_tensor(name, shape, dt))
        ident = sb("ident", [128, 128], BF16)
        pidx = sb("pidx", [128, 1], F32)
        ones_bf = sb("onesbf", [128, 1], BF16)
        maskt = sb("maskt_s", [128, 640], BF16)
        t_all = sb("t_all", [128, NSLOT * CH * 256], BF16)

        e_m = ExitStack()
        m_sb = e_m.enter_context(nc.sbuf_tensor("m_sb", [128, CH * D], BF16))
        xo1 = [e_m.enter_context(nc.sbuf_tensor(f"xo1_{i}", [128, 2 * D], F32)) for i in range(2)]
        col1 = {n: e_m.enter_context(nc.sbuf_tensor("c1" + n, [128, 16], F32))
                for n in ("s1", "s2", "c6", "cv", "u", "rs", "nm")}
        xot = e_m.enter_context(nc.sbuf_tensor("xot", [128, 16 * D], BF16))
        xof = e_m.enter_context(nc.sbuf_tensor("xof", [128, NSLOT * CH * 256], BF16))
        e_a0 = ExitStack()
        wa_sb = e_a0.enter_context(nc.sbuf_tensor("wa", [128, CH * D], BF16))
        wb_sb = e_a0.enter_context(nc.sbuf_tensor("wb", [128, CH * D], BF16))
        wvpe = e_a0.enter_context(nc.sbuf_tensor("wvpe", [128, 2 * D], BF16))
        wc_sb = e_a0.enter_context(nc.sbuf_tensor("wc", [128, CH * D], BF16))
        wd_sb = e_a0.enter_context(nc.sbuf_tensor("wd", [128, CH * D], BF16))
        iota_f = e_a0.enter_context(nc.sbuf_tensor("iotaf", [128, 128], F32))
        e_a0.close()
        e_m.close()

        e_b = ExitStack()
        xn_fm = e_b.enter_context(nc.sbuf_tensor("xn_fm", [128, CH * 4096], BF16))
        xn_tm = e_b.enter_context(nc.sbuf_tensor("xn_tm", [128, NTILE * D], BF16))
        wv_p = e_b.enter_context(nc.sbuf_tensor("wv_p", [128, CH * D], BF16))
        x_grp = [e_b.enter_context(nc.sbuf_tensor(f"x_grp{i}", [128, 2 * D], F32)) for i in range(2)]
        col2 = {n: e_b.enter_context(nc.sbuf_tensor("c2" + n, [128, NTILE], F32))
                for n in ("s1", "s2", "cv", "u", "rs")}
        col2["c6"] = col2["u"]
        col2["nm"] = col2["cv"]
        ptbuf = e_b.enter_context(nc.sbuf_tensor("ptbuf", [128, 3 * 256], BF16))
        avxt = e_b.enter_context(nc.sbuf_tensor("avxt", [128, CH * 256], BF16))
        outb = e_b.enter_context(nc.sbuf_tensor("outb", [128, 2 * D], F32))
        recip = e_b.enter_context(nc.sbuf_tensor("recip", [128, 16], F32))
        e_b.close()

        e_c = ExitStack()
        warr = e_c.enter_context(nc.sbuf_tensor("warr", [128, CH * D], BF16))
        wark = e_c.enter_context(nc.sbuf_tensor("wark", [128, CH * D], BF16))
        warv = e_c.enter_context(nc.sbuf_tensor("warv", [128, CH * D], BF16))
        mur = e_c.enter_context(nc.sbuf_tensor("mur_s", [128, CH], F32))
        muk = e_c.enter_context(nc.sbuf_tensor("muk_s", [128, CH], F32))
        xprev = e_c.enter_context(nc.sbuf_tensor("xprev", [8, D], F32))
        xprevt = e_c.enter_context(nc.sbuf_tensor("xprevt", [8, D], BF16))
        xprevf = e_c.enter_context(nc.sbuf_tensor("xprevf", [128, CH * 8], BF16))
        xoc = [e_c.enter_context(nc.sbuf_tensor(f"xoc{i}", [128, 2 * D], F32)) for i in range(3)]
        parts = e_c.enter_context(nc.sbuf_tensor("parts", [128, 2 * D], F32))
        colc = {n: e_c.enter_context(nc.sbuf_tensor("cc" + n, [128, 20], F32))
                for n in ("s1", "s2", "c6", "cv", "u", "rs", "nm")}
        xnopf_b = [e_c.enter_context(nc.sbuf_tensor(f"xnopf{i}", [128, CH * 257], BF16))
                   for i in range(2)]
        diffb = e_c.enter_context(nc.sbuf_tensor("diffb", [128, CH * 256], BF16))
        xr_sb = [e_c.enter_context(nc.sbuf_tensor(f"xr_sb{i}", [128, CH * 256], BF16)) for i in range(2)]
        xk_sb = [e_c.enter_context(nc.sbuf_tensor(f"xk_sb{i}", [128, CH * 256], BF16)) for i in range(2)]
        rk_sb = e_c.enter_context(nc.sbuf_tensor("rk_sb", [128, CH * 256], BF16))
        kact = [e_c.enter_context(nc.sbuf_tensor(f"kact{i}", [128, CH * 256], BF16)) for i in range(2)]
        r_sb = e_c.enter_context(nc.sbuf_tensor("r_sb", [128, 2 * D], F32))
        xpart = e_c.enter_context(nc.sbuf_tensor("xpart", [128, 2 * D], F32))
        fbuf = e_c.enter_context(nc.sbuf_tensor("fbuf", [128, D], F32))
        outc = [e_c.enter_context(nc.sbuf_tensor(f"outc{i}", [128, 2 * D], F32)) for i in range(2)]
        e_c.close()

        pool = ctx.enter_context(nc.psum_tensor("pool", [128, 4096], F32))
        pSC = [pool[:, 0:512], pool[:, 512:1024]]
        pSCb = [pool[:, 512 * i:512 * (i + 1)].bitcast(BF16) for i in range(6)]
        pAV = pool[:, 1024:3072]
        pAVlo = pool[:, 1024:2048]
        pAVhi = pool[:, 2048:3072]
        pOS = pool[:, 3072:4096]

        sems = {}
        for n in sem_names:
            sems[n] = ctx.enter_context(nc.semaphore("sem_" + n))

        gate = {"sc0": None, "sc1": None, "sc2": None, "sc3": None, "sc4": None, "sc5": None}

        def gwait(key):
            g = gate[key]
            return [g] if g else []

        def ln_group(xslices, xnslices, x3d, cols, k0, xgate, sA, vL, pd=128,
                     apply_extra=None):
            G = len(xslices)
            sA0 = S.val(sA)
            vL0 = S.val(vL)
            cs = lambda n: cols[n][0:pd, k0:k0 + G]
            n = 1024
            for g in range(G):
                S.add("scalar",
                      ([xgate] + ([apply_extra] if apply_extra else [])) if g == 0 else [],
                      lambda e, s, o=xnslices[g], i=xslices[g],
                      a=cols["s2"][0:pd, k0 + g:k0 + g + 1]:
                      e.activation(o, i, AF.Square, accum_out=a), inc=sA)
            S.add("vector", [xgate],
                  lambda e, s, o=cs("s1"), i=x3d: e.reduce_sum(o, i, axis=AX.X), inc=vL)
            S.add("vector", [(vL, vL0 + 1)],
                  lambda e, s, o=cs("c6"), i=cs("s1"): e.tensor_tensor(o, i, i, ALU.mult),
                  inc=vL)
            S.add("vector", [(sA, sA0 + G), (vL, vL0 + 2)],
                  lambda e, s, o=cs("cv"), i=cs("c6"), j2=cs("s2"), sc=-1.0 / n:
                  e.scalar_tensor_tensor(o, i, sc, j2, ALU.mult, ALU.add), inc=vL)
            S.add("scalar", [(vL, vL0 + 3)],
                  lambda e, s, o=cs("u"), i=cs("cv"), sc=f32(1.0 / n):
                  e.activation(o, i, AF.Ln, bias=f32(1e-5), scale=sc), inc=sA)
            S.add("scalar", [(sA, sA0 + G + 1)],
                  lambda e, s, o=cs("rs"), i=cs("u"): e.activation(o, i, AF.Exp, scale=f32(-0.5)),
                  inc=sA)
            S.add("vector", [(sA, sA0 + G + 2)],
                  lambda e, s, o=cs("nm"), i=cs("s1"), r=cs("rs"), sc=-1.0 / n:
                  e.scalar_tensor_tensor(o, i, sc, r, ALU.mult, ALU.mult), inc=vL)
            for g in range(G):
                S.add("scalar", [(vL, vL0 + 4)] if g == 0 else [],
                      lambda e, s, o=xnslices[g], i=xslices[g],
                      b=cols["nm"][0:pd, k0 + g:k0 + g + 1],
                      r=cols["rs"][0:pd, k0 + g:k0 + g + 1]:
                      e.activation(o, i, AF.Identity, bias=b, scale=r), inc=sA)

        def transpose_tile(src, dst_ap, bank, readygate, tsem, vsem, evac_extra=None):
            def tbody(e, s, b=bank, src_=src):
                inst = None
                for cc in range(CH):
                    inst = e.matmul(pSCb[b][:, cc * 128:(cc + 1) * 128],
                                    src_[:, cc * 128:(cc + 1) * 128], ident[:, :],
                                    is_transpose=True, start=(cc == 0), stop=(cc == CH - 1))
                return inst
            S.add("tensor", [readygate, ("vcst", 1)] + gwait("sc%d" % bank), tbody, inc=tsem)
            tv = S.val(tsem)
            S.add("vector", [(tsem, tv)] + ([evac_extra] if evac_extra else []),
                  lambda e, s, o=dst_ap, b=bank:
                  e.tensor_copy(o, pSCb[b][:, 0:CH * 128]), inc=vsem)
            gate["sc%d" % bank] = (vsem, S.val(vsem))

        for hh in range(2):
            for (dst, dr) in ((wa_sb, wq_d), (wb_sb, wk_d)):
                S.add("gpsimd", [],
                      lambda e, s, d=dst, r=dr, h=hh:
                      e.dma_start(d[:, :].rearrange("p (c k) -> p c k", k=D)[:, :, h * 512:(h + 1) * 512],
                                  r[:, :].rearrange("(c p) k -> p c k", p=128)[:, :, h * 512:(h + 1) * 512]),
                      inc="dwh0" if hh == 0 else "dw", incby=16)
        for hh in range(2):
            for (dst, dr) in ((wc_sb, wvT_d), (wd_sb, wo_d)):
                S.add("gpsimd", [("dw", 32)] if hh == 0 else [("dwv0", 32)],
                      lambda e, s, d=dst, r=dr, h=hh:
                      e.dma_start(d[:, :].rearrange("p (c k) -> p c k", k=D)[:, :, h * 512:(h + 1) * 512],
                                  r[:, :].rearrange("(c p) k -> p c k", p=128)[:, :, h * 512:(h + 1) * 512]),
                      inc="dwv0" if hh == 0 else "dwv1", incby=16)
        S.add("gpsimd", [], lambda e, s: e.dma_start(maskt[:, :], maskt_d[:, :]),
              inc="dmk", incby=16)
        S.add("gpsimd", [], lambda e, s: e.iota(
            iota_f[:, :], [[1, 128]], channel_multiplier=0,
            allow_small_or_imprecise_dtypes=True), inc=None)
        S.add("gpsimd", [], lambda e, s: e.iota(
            pidx[:, :], [[0, 1]], channel_multiplier=1,
            allow_small_or_imprecise_dtypes=True), inc="gio")
        S.add("vector", [("gio", 1)],
              lambda e, s: e.tensor_scalar(ident[:, :], iota_f[:, :], pidx[:, :],
                                           None, ALU.is_equal), inc="vcst")
        S.add("vector", [], lambda e, s: e.memset(ones_bf[:, :], 1.0), inc="vcst")

        ast = {}
        tt1_after = {}

        def a1_load(j):
            sem = "dxa" if j % 2 == 0 else "dxb"
            wt = []
            if j >= 2:
                wt = [("sa1", ast[j - 2]["apply"]), ("vl1", ast[j - 2]["reduce"])]
            S.add("sync", wt,
                  lambda e, s, jj=j:
                  e.dma_start(xo1[jj % 2][:, :].rearrange("p (t k) -> p t k", k=D),
                              xo_d[256 * jj:256 * (jj + 1), :].rearrange("(t p) k -> p t k", p=128)),
                  inc=sem, incby=16)
            ast[j] = {"xg": (sem, S.val(sem))}

        def a1_pieces(j):
            st = ast[j]
            k0 = 2 * j
            cs = lambda n: col1[n][:, k0:k0 + 2]
            bufx = xo1[j % 2]
            for g in range(2):
                S.add("scalar",
                      [st["xg"]] if g == 0 else [],
                      lambda e, s, o=xot[:, (k0 + g) * D:(k0 + g + 1) * D],
                      ii=bufx[:, g * D:(g + 1) * D],
                      a=col1["s2"][:, k0 + g:k0 + g + 1]:
                      e.activation(o, ii, AF.Square, accum_out=a), inc="sa1")
            st["sq"] = S.val("sa1")
            vL0 = S.val("vl1")
            S.add("vector", [st["xg"]],
                  lambda e, s, o=cs("s1"), i=bufx[:, :].rearrange("p (t k) -> p t k", k=D):
                  e.reduce_sum(o, i, axis=AX.X), inc="vl1")
            st["reduce"] = S.val("vl1")
            S.add("vector", [("vl1", vL0 + 1)],
                  lambda e, s, o=cs("c6"), i=cs("s1"): e.tensor_tensor(o, i, i, ALU.mult),
                  inc="vl1")
            S.add("vector", [("sa1", st["sq"]), ("vl1", vL0 + 2)],
                  lambda e, s, o=cs("cv"), i=cs("c6"), j2=cs("s2"):
                  e.scalar_tensor_tensor(o, i, -1.0 / 1024, j2, ALU.mult, ALU.add), inc="vl1")
            st["cvar"] = S.val("vl1")
            S.add("scalar", [("vl1", st["cvar"])],
                  lambda e, s, o=cs("u"), i=cs("cv"):
                  e.activation(o, i, AF.Ln, bias=f32(1e-5), scale=f32(1.0 / 1024)), inc="sa1")
            st["ln"] = S.val("sa1")
            S.add("scalar", [("sa1", st["ln"])],
                  lambda e, s, o=cs("rs"), i=cs("u"): e.activation(o, i, AF.Exp, scale=f32(-0.5)),
                  inc="sa1")
            st["exp"] = S.val("sa1")
            S.add("vector", [("sa1", st["exp"])],
                  lambda e, s, o=cs("nm"), i=cs("s1"), r=cs("rs"):
                  e.scalar_tensor_tensor(o, i, -1.0 / 1024, r, ALU.mult, ALU.mult), inc="vl1")
            st["negm"] = S.val("vl1")
            for g in range(2):
                S.add("scalar", [("vl1", st["negm"])] if g == 0 else [],
                      lambda e, s, o=xot[:, (k0 + g) * D:(k0 + g + 1) * D],
                      ii=bufx[:, g * D:(g + 1) * D],
                      b=col1["nm"][:, k0 + g:k0 + g + 1], r=col1["rs"][:, k0 + g:k0 + g + 1]:
                      e.activation(o, ii, AF.Identity, bias=b, scale=r), inc="sa1")
            st["apply"] = S.val("sa1")
            if j + 2 < NSLOT:
                a1_load(j + 2)

        a1_load(0)
        a1_load(1)

        g = 0
        group_emitters = []
        for ccc in range(CH):
            for hc in range(2):
                def mk_m(c=ccc, h=hc):
                    def em():
                        global_bank = emctx["g"] % 2

                        def mbody(e, s, b=global_bank, c2=c, h2=h):
                            inst = None
                            for er in range(CH):
                                inst = e.matmul(pSC[b][:, 0:512],
                                                wa_sb[:, er * D + c2 * 128: er * D + (c2 + 1) * 128],
                                                wb_sb[:, er * D + h2 * 512: er * D + h2 * 512 + 512],
                                                start=(er == 0), stop=(er == CH - 1))
                            return inst
                        need = [("dwh0", 32)] if (c < 4 and h == 0) else [("dwh0", 32), ("dw", 32)]
                        S.add("tensor", need + gwait("sc%d" % global_bank), mbody, inc="tm0")
                        tv = S.val("tm0")
                        S.add("scalar", [("tm0", tv)],
                              lambda e, s, b=global_bank, c2=c, h2=h:
                              e.copy(m_sb[:, c2 * D + h2 * 512: c2 * D + h2 * 512 + 512],
                                     pSC[b][:, 0:512]), inc="se0")
                        gate["sc%d" % global_bank] = ("se0", S.val("se0"))
                        emctx["g"] += 1
                    return em
                group_emitters.append(mk_m())
        wvT_marker = len(group_emitters)
        for ec in range(CH):
            for oh in range(2):
                def mk_v(c=ec, h=oh):
                    def em():
                        global_bank = emctx["g"] % 2

                        def vbody(e, s, b=global_bank, c2=c, h2=h):
                            inst = None
                            for vr in range(CH):
                                inst = e.matmul(pSC[b][:, 0:512],
                                                wc_sb[:, vr * D + c2 * 128: vr * D + (c2 + 1) * 128],
                                                wd_sb[:, vr * D + h2 * 512: vr * D + h2 * 512 + 512],
                                                start=(vr == 0), stop=(vr == CH - 1))
                            return inst
                        vneed = [("dwv0", 32)] if (c < 4 and h == 0) else \
                            [("dwv0", 32), ("dwv1", 32)]
                        S.add("tensor", vneed + gwait("sc%d" % global_bank),
                              vbody, inc="tm0")
                        tv = S.val("tm0")
                        wt = [("tm0", tv)]
                        if c >= 2 and h == 0:
                            wt.append(("dvs", 16 * c))
                        S.add("scalar", wt,
                              lambda e, s, b=global_bank, sl=c % 2, h2=h:
                              e.copy(wvpe[:, sl * D + h2 * 512: sl * D + h2 * 512 + 512],
                                     pSC[b][:, 0:512]), inc="se0")
                        gate["sc%d" % global_bank] = ("se0", S.val("se0"))
                        emctx["g"] += 1
                        if h == 1:
                            S.add("gpsimd", [("se0", S.val("se0"))],
                                  lambda e, s, c2=c, sl=c % 2:
                                  e.dma_start(wvp_d[:, c2 * D:(c2 + 1) * D],
                                              wvpe[:, sl * D:(sl + 1) * D]),
                                  inc="dvs", incby=16)
                    return em
                group_emitters.append(mk_v())

        def emit_a1_transposes(j):
            st = ast[j]
            for tau in range(2):
                i = 2 * j + tau
                apply_done = ("sa1", st["apply"] - 1 + tau)
                dst_ap = xof[:, 2048 * j:2048 * (j + 1)].rearrange(
                    "p (c q) -> p c q", q=256)[:, :, tau * 128:tau * 128 + 128]
                transpose_tile(xot[:, i * D:(i + 1) * D], dst_ap, i % 2,
                               apply_done, "tt1", "ve1")
            st["evac"] = ("ve1", S.val("ve1"))
            S.add("sync", [st["evac"]],
                  lambda e, s, jj=j:
                  e.dma_start(xof_d[:, 2048 * jj:2048 * (jj + 1)],
                              xof[:, 2048 * jj:2048 * (jj + 1)]),
                  inc="dxf", incby=16)

        def emit_a1_tensor(j):
            st = ast[j]
            evac_done = st["evac"]
            for ccp in range(CH):
                bank = ccp % 2

                def tqb(e, s, b=bank, c=ccp, jj=j):
                    inst = None
                    for cr in range(CH):
                        inst = e.matmul(pSC[b][:, 0:256],
                                        m_sb[:, cr * D + c * 128: cr * D + (c + 1) * 128],
                                        xof[:, 2048 * jj + cr * 256: 2048 * jj + (cr + 1) * 256],
                                        start=(cr == 0), stop=(cr == CH - 1))
                    return inst
                S.add("tensor", [evac_done] + gwait("sc%d" % bank), tqb, inc="tq1")
                tv = S.val("tq1")
                S.add("scalar", [("tq1", tv)],
                      lambda e, s, b=bank, jj=j, c=ccp:
                      e.copy(t_all[:, jj * 2048 + c * 256: jj * 2048 + (c + 1) * 256],
                             pSC[b][:, 0:256]), inc="st1")
                gate["sc%d" % bank] = ("st1", S.val("st1"))

        a1_pieces(0)
        a1_pieces(1)
        emit_a1_transposes(0)
        emit_a1_transposes(1)
        emctx = {"g": 0, "dw_a0": 0}
        for idx, em in enumerate(group_emitters):
            em()
            if idx % 4 == 3 and idx // 4 + 2 < NSLOT:
                a1_pieces(idx // 4 + 2)

        for _j in range(2, NSLOT):
            emit_a1_transposes(_j)
            emit_a1_tensor(_j - 2)
        emit_a1_tensor(NSLOT - 2)
        emit_a1_tensor(NSLOT - 1)
        a1_t_done = ("tq1", S.val("tq1"))
        a1_s_done = [("sa1", S.val("sa1")), ("st1", S.val("st1"))]
        S.add("gpsimd", a1_s_done + [a1_t_done, ("dvs", S.val("dvs")), ("dxf", 16 * NSLOT)],
              lambda e, s: e.dma_start(wv_p[:, :], wvp_d[:, :]), inc="dvs", incby=16)
        wvp_ready = ("dvs", S.val("dvs"))

        exp_of_block = {}
        av_of_block = {}
        pt_gate = [None, None, None]

        ln_state = {}
        sa2_apply = {}

        def emit_load(gi):
            if gi == 0:
                wt = a1_s_done + [a1_t_done, ("vl1", 4 * NSLOT), ("dxf", 16 * NSLOT)]
            elif gi == 1:
                wt = [("dxf", 16 * NSLOT)]
            else:
                st = ln_state[gi - 2]
                wt = [("sa2", st["apply"]), ("vl2", st["reduce"])]
            S.add("sync", wt,
                  lambda e, s, g=gi:
                  e.dma_start(x_grp[g % 2][:, :].rearrange("p (t k) -> p t k", k=D),
                              x_d[256 * g:256 * (g + 1), :].rearrange("(t p) k -> p t k", p=128)),
                  inc="dxa" if gi % 2 == 0 else "dxb", incby=16)
            ln_state[gi] = {"xg": ("dxa" if gi % 2 == 0 else "dxb",
                                   S.val("dxa" if gi % 2 == 0 else "dxb"))}

        def emit_ln_vec(gi):
            st = ln_state[gi]
            buf = x_grp[gi % 2]
            k0 = 2 * gi
            cs = lambda n: col2[n][:, k0:k0 + 2]
            vL0 = S.val("vl2")
            S.add("vector", [st["xg"]],
                  lambda e, s, o=cs("s1"), i=buf[:, :].rearrange("p (t k) -> p t k", k=D):
                  e.reduce_sum(o, i, axis=AX.X), inc="vl2")
            st["reduce"] = S.val("vl2")
            S.add("vector", [("vl2", vL0 + 1)],
                  lambda e, s, o=cs("c6"), i=cs("s1"): e.tensor_tensor(o, i, i, ALU.mult),
                  inc="vl2")
            S.add("vector", [("sa2", st["sq"]), ("vl2", vL0 + 2)],
                  lambda e, s, o=cs("cv"), i=cs("c6"), j2=cs("s2"):
                  e.scalar_tensor_tensor(o, i, -1.0 / 1024, j2, ALU.mult, ALU.add), inc="vl2")
            st["cvar"] = S.val("vl2")

        def emit_ln_sq(gi):
            st = ln_state[gi]
            buf = x_grp[gi % 2]
            k0 = 2 * gi
            for g in range(2):
                i = k0 + g
                S.add("scalar", [st["xg"], ("tt2", S.val("tt2"))] if g == 0 else [],
                      lambda e, s, o=xn_tm[:, i * D:(i + 1) * D], ii=buf[:, g * D:(g + 1) * D],
                      a=col2["s2"][:, i:i + 1]:
                      e.activation(o, ii, AF.Square, accum_out=a), inc="sa2")
            st["sq"] = S.val("sa2")

        def emit_ln_s_ln(gi):
            st = ln_state[gi]
            k0 = 2 * gi
            cs = lambda n: col2[n][:, k0:k0 + 2]
            S.add("scalar", [("vl2", st["cvar"])],
                  lambda e, s, o=cs("u"), i=cs("cv"):
                  e.activation(o, i, AF.Ln, bias=f32(1e-5), scale=f32(1.0 / 1024)), inc="sa2")
            st["ln"] = S.val("sa2")

        def emit_ln_s_exp(gi):
            st = ln_state[gi]
            k0 = 2 * gi
            cs = lambda n: col2[n][:, k0:k0 + 2]
            S.add("scalar", [("sa2", st["ln"])],
                  lambda e, s, o=cs("rs"), i=cs("u"): e.activation(o, i, AF.Exp, scale=f32(-0.5)),
                  inc="sa2")

        def emit_ln_negm(gi):
            st = ln_state[gi]
            k0 = 2 * gi
            cs = lambda n: col2[n][:, k0:k0 + 2]
            S.add("vector", [("sa2", st["exp"])],
                  lambda e, s, o=cs("nm"), i=cs("s1"), r=cs("rs"):
                  e.scalar_tensor_tensor(o, i, -1.0 / 1024, r, ALU.mult, ALU.mult), inc="vl2")
            st["negm"] = S.val("vl2")

        def emit_ln_apply(gi):
            st = ln_state[gi]
            buf = x_grp[gi % 2]
            k0 = 2 * gi
            for g in range(2):
                i = k0 + g
                S.add("scalar", [("vl2", st["negm"])] if g == 0 else [],
                      lambda e, s, o=xn_tm[:, i * D:(i + 1) * D], ii=buf[:, g * D:(g + 1) * D],
                      b=col2["nm"][:, i:i + 1], r=col2["rs"][:, i:i + 1]:
                      e.activation(o, ii, AF.Identity, bias=b, scale=r), inc="sa2")
            st["apply"] = S.val("sa2")
            sa2_apply[gi] = S.val("sa2")

        def emit_transposes(j):
            gate["sc4"] = ("vav", S.val("vav"))
            gate["sc5"] = ("vav", S.val("vav"))
            for i in range(4 * j, 4 * j + 4):
                gi = i // 2
                apply_done = ("sa2", sa2_apply[gi] - 1 + (i % 2))
                dst_ap = xn_fm[:, :].rearrange("p (c t) -> p c t", t=4096)[:, :, i * 128:(i + 1) * 128]
                transpose_tile(xn_tm[:, i * D:(i + 1) * D], dst_ap, 4 + i % 2,
                               apply_done, "tt2", "ve2")

        def emit_B(j, inject):
            E = 4 * j + 4

            def emit_scores(b):
                bank = b % 2

                def scb(e, s, bk=bank, jj=j, bb=b):
                    inst = None
                    for cc in range(CH):
                        inst = e.matmul(pSC[bk][:, 0:256],
                                        xn_fm[:, cc * 4096 + 128 * bb: cc * 4096 + 128 * (bb + 1)],
                                        t_all[:, jj * 2048 + cc * 256: jj * 2048 + (cc + 1) * 256],
                                        start=(cc == 0), stop=(cc == CH - 1))
                    return inst
                S.add("tensor", [("ve2", ve2_of[j])] + gwait("sc%d" % bank), scb, inc="tsc")
                tscv = S.val("tsc")
                k = b - 4 * j
                if k >= 0:
                    S.add("vector", [("tsc", tscv), ("dmk", 16)],
                          lambda e, s, bk=bank, kk=k:
                          e.tensor_tensor(pSC[bk][:, 0:256], pSC[bk][:, 0:256],
                                          maskt[:, 384 - 128 * kk: 640 - 128 * kk], ALU.add),
                          inc="vmk")
                    exp_w = [("vmk", S.val("vmk"))]
                else:
                    exp_w = [("tsc", tscv)]
                pslot = b % 3
                if pt_gate[pslot] is not None:
                    exp_w = exp_w + [pt_gate[pslot]]
                S.add("scalar", exp_w,
                      lambda e, s, bk=bank, ps=pslot:
                      e.activation(ptbuf[:, ps * 256:(ps + 1) * 256], pSC[bk][:, 0:256],
                                   AF.Exp, bias=EBIAS, scale=SCALE), inc="sex")
                exp_of_block[(j, b)] = S.val("sex")
                gate["sc%d" % bank] = ("sex", S.val("sex"))

            def emit_sums_avxt(b):
                pslot = b % 3
                w0 = [("sex", exp_of_block[(j, b)])]
                if b == 0 and j > 0:
                    w0 += [("soe", 2 * j), ("vav", 2 * j)]

                def sums_avxt(e, s, jj=j, bb=b, ps=pslot, first=(b == 0), last=(b == E - 1)):
                    for qt in range(2):
                        e.matmul(pOS[:, qt:qt + 1],
                                 ptbuf[:, ps * 256 + qt * 128: ps * 256 + (qt + 1) * 128],
                                 ones_bf[:, :],
                                 start=(first and qt == 0), stop=(last and qt == 1))
                    inst = None
                    for cc in range(CH):
                        inst = e.matmul(pAV[:, cc * 256:(cc + 1) * 256],
                                        xn_tm[:, bb * D + cc * 128: bb * D + (cc + 1) * 128],
                                        ptbuf[:, ps * 256:(ps + 1) * 256],
                                        start=(first and (cc % 2 == 0)),
                                        stop=(last and (cc % 2 == 1)))
                    return inst
                S.add("tensor", w0, sums_avxt, inc="tav")
                av_of_block[(j, b)] = S.val("tav")
                pt_gate[pslot] = ("tav", S.val("tav"))

            for b in range(E):
                emit_scores(b)
                if b < len(inject):
                    inject[b]()
                if b >= 2:
                    emit_sums_avxt(b - 2)
            for fn in inject[E:]:
                fn()
            emit_sums_avxt(E - 2)
            emit_sums_avxt(E - 1)

            S.add("tensor", [], lambda e, s: e.nop(), inc="tsu")
            for qt in range(2):
                S.add("vector", [("tav", av_of_block[(j, E - 1)])] if qt == 0 else [],
                      lambda e, s, q=qt:
                      e.tensor_copy(
                          avxt[:, :].rearrange("p (c q) -> p c q", q=256)[:, :, q * 128:(q + 1) * 128],
                          pAV[:, :].rearrange("p (c q) -> p c q", q=256)[:, :, q * 128:(q + 1) * 128]),
                      inc="vav")
            vav_q = S.val("vav")
            S.add("vector", [("tsu", S.val("tsu"))],
                  lambda e, s, jj=j: e.reciprocal(recip[:, 2 * jj:2 * jj + 2], pOS[:, 0:2]),
                  inc="vrc")
            for qt in range(2):
                tgt = pOS if qt == 0 else pAVlo
                w = [("vav", vav_q - (1 - qt)), ("vrc", S.val("vrc"))]
                if j == 0 and qt == 0:
                    w.append(wvp_ready)

                def ob(e, s, t=tgt, q=qt):
                    inst = None
                    for oh in range(2):
                        for cc in range(CH):
                            inst = e.matmul(t[:, oh * 512:(oh + 1) * 512],
                                            avxt[:, cc * 256 + q * 128: cc * 256 + (q + 1) * 128],
                                            wv_p[:, cc * D + oh * 512: cc * D + oh * 512 + 512],
                                            start=(cc == 0), stop=(cc == CH - 1))
                    return inst
                S.add("tensor", w, ob, inc="to")
                tov = S.val("to")
                wse = [("to", tov), ("vrc", S.val("vrc")), ("dob", 16 * j)]
                S.add("scalar", wse,
                      lambda e, s, t=tgt, q=qt, jj=j:
                      e.activation(outb[:, q * D:(q + 1) * D],
                                   t[:, 0:D], AF.Identity, scale=recip[:, 2 * jj + q: 2 * jj + q + 1]),
                      inc="soe")
            S.add("gpsimd", [("soe", S.val("soe"))],
                  lambda e, s, jj=j:
                  e.dma_start(out_d[256 * jj:256 * (jj + 1), :].rearrange("(t p) k -> p t k", p=128),
                              outb[:, :].rearrange("p (t k) -> p t k", k=D)),
                  inc="dob", incby=16)

        ve2_of = {}
        emit_load(0)
        emit_load(1)
        for j in range(NSLOT):
            ga, gb = 2 * j, 2 * j + 1
            emit_ln_sq(ga)
            emit_ln_sq(gb)
            ln_state[ga]["exp"] = S.val("sa2") + 2
            ln_state[gb]["exp"] = S.val("sa2") + 4
            emit_ln_vec(ga)
            emit_ln_vec(gb)
            if j + 1 < NSLOT:
                pass
            inject = [lambda a=ga: emit_ln_s_ln(a),
                      lambda a=ga: emit_ln_s_exp(a),
                      lambda a=ga: emit_ln_negm(a),
                      lambda a=gb: emit_ln_s_ln(a),
                      lambda a=gb: emit_ln_s_exp(a),
                      lambda a=gb: emit_ln_negm(a),
                      lambda a=ga: emit_ln_apply(a),
                      lambda a=gb: emit_ln_apply(a)]
            if j == 0:
                for fn in inject:
                    fn()
                emit_transposes(0)
                ve2_of[0] = S.val("ve2")
                if NSLOT > 1:
                    emit_load(2)
                    emit_load(3)
                continue
            emit_B(j - 1, inject)
            if j == 1:
                for (sd, rd) in ((wrs_d, wrT_d), (wks_d, wkTf_d), (wvs_d, wvTf_d)):
                    S.add("gpsimd", [], lambda e, s, d=sd, r=rd: e.dma_start(d[:, :], r[:, :]),
                          inc="dtx", incby=16)
            emit_transposes(j)
            ve2_of[j] = S.val("ve2")
            if j + 1 < NSLOT:
                emit_load(2 * (j + 1))
                emit_load(2 * (j + 1) + 1)
        emit_B(NSLOT - 1, [])

        sc_all = ("tsc", S.val("tsc"))
        av_all = ("tav", S.val("tav"))
        S.add("tensor", [], lambda e, s: e.nop(), inc="tdn")
        b_t_done = ("tdn", S.val("tdn"))
        b_s_done = ("soe", S.val("soe"))
        b_v_done = ("vav", S.val("vav"))

        for t in taps:
            src = {"xnfm": xn_fm, "xntm": xn_tm, "tall": t_all, "m": m_sb, "wvp": wv_p}[t]
            S.add("gpsimd", [b_t_done, b_s_done],
                  lambda e, s, d=tap_d[t], i=src: e.dma_start(d[:, :], i[:, :]),
                  inc="dw", incby=16)

        tap_dw = ("dw", S.val("dw"))
        S.add("gpsimd", [sc_all, tap_dw], lambda e, s: e.dma_start(mur[:, :], mur_d[:, :]),
              inc="dmu", incby=16)
        S.add("gpsimd", [sc_all, tap_dw], lambda e, s: e.dma_start(muk[:, :], muk_d[:, :]),
              inc="dmu", incby=16)
        cmu = ("dmu", 32)
        cwk = []
        for hh in range(2):
            S.add("gpsimd", [sc_all, tap_dw, ("dtx", 48)],
                  lambda e, s, h=hh:
                  e.dma_start(wark[:, :].rearrange("p (c k) -> p c k", k=D)[:, :, h * 512:(h + 1) * 512],
                              wks_d[:, :].rearrange("(c p) k -> p c k", p=128)[:, :, h * 512:(h + 1) * 512]),
                  inc="dk%d" % hh, incby=16)
            cwk.append(("dk%d" % hh, 16))
        for (dst, dr, sm) in ((warr, wrs_d, "dwr2"), (warv, wvs_d, "dvv")):
            S.add("gpsimd", [sc_all, tap_dw, ("dtx", 48)],
                  lambda e, s, d=dst, r=dr:
                  e.dma_start(d[:, :].rearrange("p (c k) -> p c k", k=D),
                              r[:, :].rearrange("(c p) k -> p c k", p=128)),
                  inc=sm, incby=16)
        cwr = ("dwr2", 16)
        cwv = ("dvv", 16)
        S.add("sync", [sc_all, tap_dw],
              lambda e, s: e.dma_start(xprev[:, :], xp_d[:, :]), inc="dx", incby=16)
        cw_done = ("dw", S.val("dw"))

        xpg = ("dx", S.val("dx"))
        ln_group([xprev[:, :]], [xprevt[:, :]], xprev[:, :].rearrange("p (t k) -> p t k", k=D), colc, 16, xpg, "slc", "vlc", pd=8)
        prev_ap = xprevf[:, :].rearrange("p (c q) -> p c q", q=8)

        def prevtb(e, s):
            inst = None
            for cc in range(CH):
                inst = e.matmul(pSCb[0][:, cc * 8:(cc + 1) * 8],
                                xprevt[:, cc * 128:(cc + 1) * 128], ident[0:8, 0:8],
                                is_transpose=True, start=(cc == 0), stop=(cc == CH - 1))
            return inst
        S.add("tensor", [("slc", S.val("slc"))] + gwait("sc0"), prevtb, inc="ttc")
        S.add("vector", [("ttc", S.val("ttc"))],
              lambda e, s: e.tensor_copy(xprevf[:, :], pSCb[0][:, 0:CH * 8]), inc="vec")
        prevf_done = ("vec", S.val("vec"))
        gate["sc0"] = ("vec", S.val("vec"))

        vfn_after = {}
        tf_after = {}
        vka_of = {}
        xpart_of = {}
        cst = {}
        mix_done_of = {}
        ld_of = {}
        vec_after = {-1: ("vec", S.val("vec"))}
        ttc_after = {-1: ("ttc", S.val("ttc"))}

        def c_load(j):
            buf = j % 3
            if j < 3:
                wl = [av_all, tap_dw]
            else:
                wl = [("vfn", xpart_of[j - 3])]
            S.add("sync", wl,
                  lambda e, s, jj=j, b=buf:
                  e.dma_start(xoc[b][:, :].rearrange("p (t k) -> p t k", k=D),
                              xo_d[256 * jj:256 * (jj + 1), :].rearrange("(t p) k -> p t k", p=128)),
                  inc="dci%d" % (j % 3), incby=16)
            ld_of[j] = ("dci%d" % (j % 3), S.val("dci%d" % (j % 3)))
            cst[j] = {"xg": ld_of[j]}

        def c_feed(j):
            st = cst[j]
            buf = j % 2
            if j < 2:
                fw = [av_all, tap_dw]
            else:
                fw = [mix_done_of[j - 2]]
            S.add("sync", fw,
                  lambda e, s, jj=j, b=buf:
                  e.dma_start(
                      xnopf_b[b][:, :].rearrange("p (c q) -> p c q", q=257)[:, :, 1:257],
                      xof_d[:, 2048 * jj:2048 * (jj + 1)].rearrange("p (c q) -> p c q", q=256)),
                  inc="dnf%d" % buf, incby=16)
            st["feed"] = ("dnf%d" % buf, S.val("dnf%d" % buf))
            S.add("vector", [prevf_done],
                  lambda e, s, jj=j, b=buf:
                  e.tensor_copy(xnopf_b[b][:, :].rearrange("p (c q) -> p c q", q=257)[:, :, 0:1],
                                prev_ap[:, :, jj:jj + 1]), inc="vec")
            st["pcol"] = ("vec", S.val("vec"))

        def c_mixes(j):
            st = cst[j]
            buf = j % 2
            xnopf = xnopf_b[buf]
            mix_w = [cmu, st["feed"], st["pcol"]]
            if j >= 1 and mix_done_of.get(j - 1):
                mix_w.append(mix_done_of[j - 1])
            if j >= 2 and (j - 2) in tf_after:
                mix_w.append(tf_after[j - 2])
            cur = xnopf[:, :].rearrange("p (c q) -> p c q", q=257)[:, :, 1:257]
            prv = xnopf[:, :].rearrange("p (c q) -> p c q", q=257)[:, :, 0:256]
            d3 = diffb[:, :].rearrange("p (c q) -> p c q", q=256)
            xk3 = xk_sb[buf][:, :].rearrange("p (c q) -> p c q", q=256)
            xr3 = xr_sb[buf][:, :].rearrange("p (c q) -> p c q", q=256)
            mukb = muk[:, :].rearrange("p (c one) -> p c one", one=1).broadcast_to([128, CH, 256])
            murb = mur[:, :].rearrange("p (c one) -> p c one", one=1).broadcast_to([128, CH, 256])
            S.add("vector", mix_w,
                  lambda e, s: e.tensor_tensor(d3, cur, prv, ALU.subtract), inc="vmx")
            S.add("vector", [("vmx", S.val("vmx"))],
                  lambda e, s: e.tensor_tensor(xk3, d3, mukb, ALU.mult), inc="vmx")
            S.add("vector", [("vmx", S.val("vmx"))],
                  lambda e, s: e.tensor_tensor(xk3, xk3, prv, ALU.add), inc="vmx")
            xk_done_of = ("vmx", S.val("vmx"))
            S.add("vector", [("vmx", S.val("vmx") - 1)],
                  lambda e, s: e.tensor_tensor(xr3, d3, murb, ALU.mult), inc="vmx")
            S.add("vector", [("vmx", S.val("vmx"))],
                  lambda e, s: e.tensor_tensor(xr3, xr3, prv, ALU.add), inc="vmx")
            mix_done_of[j] = ("vmx", S.val("vmx"))
            cst[j]["xk_done"] = xk_done_of

        def c_stage_b(j, inject, midfns=()):
            buf = j % 2
            mix_done = mix_done_of[j]
            ld = ld_of[j]
            pw = ([("vfn", xpart_of[j - 1])] if j >= 1 else [av_all, tap_dw]) + [("dob", 16 * (j + 1))]
            S.add("sync", pw,
                  lambda e, s, jj=j:
                  e.dma_start(parts[:, :].rearrange("p (t k) -> p t k", k=D),
                              out_d[256 * jj:256 * (jj + 1), :].rearrange("(t p) k -> p t k", p=128)),
                  inc="dcp", incby=16)
            pld = ("dcp", S.val("dcp"))
            ip = 0

            def inj():
                nonlocal ip
                if ip < len(inject):
                    inject[ip]()
                    ip += 1

            if j == 0:
                gate["sc2"] = b_s_done
                gate["sc3"] = b_s_done
            for dc in range(CH):
                bank = dc % 4
                pb = pool[:, 512 * bank:512 * bank + 512]

                def kb(e, s, b=pb, d=dc, bu=buf):
                    inst = None
                    for cr in range(CH):
                        inst = e.matmul(b[:, 0:256],
                                        wark[:, cr * D + d * 128: cr * D + (d + 1) * 128],
                                        xk_sb[bu][:, cr * 256:(cr + 1) * 256],
                                        start=(cr == 0), stop=(cr == CH - 1))
                    return inst
                S.add("tensor", [cst[j]["xk_done"], cwk[0] if dc < 4 else cwk[1]] + gwait("sc%d" % bank), kb, inc="tk")
                S.add("scalar", [("tk", S.val("tk"))],
                      lambda e, s, b=pb, d=dc:
                      e.activation(rk_sb[:, d * 256:(d + 1) * 256], b[:, 0:256], AF.Relu),
                      inc="srk")
                gate["sc%d" % bank] = ("srk", S.val("srk"))
                inj()
            sq_w = [("srk", S.val("srk"))]
            if j >= 2:
                sq_w.append(tf_after[j - 2])
            S.add("scalar", sq_w,
                  lambda e, s, bu=buf: e.activation(kact[bu][:, :], rk_sb[:, :], AF.Square),
                  inc="srk")
            kact_done = ("srk", S.val("srk"))
            for qt in range(2):
                tgt = pOS if qt == 0 else pAVlo
                w = [mix_done, cwr]
                if j > 0:
                    w.append(("vfn", vfn_after[j - 1] - 1) if qt == 0 else ("ssg", 2 * j))
                elif qt == 0:
                    w.append(b_s_done)
                if qt == 1:
                    w += gwait("sc2") + gwait("sc3")

                def rb(e, s, t=tgt, q=qt, bu=buf):
                    inst = None
                    for oh in range(2):
                        for cr in range(CH):
                            inst = e.matmul(t[:, oh * 512:(oh + 1) * 512],
                                            xr_sb[bu][:, cr * 256 + q * 128: cr * 256 + (q + 1) * 128],
                                            warr[:, cr * D + oh * 512: cr * D + oh * 512 + 512],
                                            start=(cr == 0), stop=(cr == CH - 1))
                    return inst
                S.add("tensor", w, rb, inc="tr")
                if qt == 0:
                    while ip < len(inject):
                        inj()
                    for fn in midfns:
                        fn()
                sg_w = [("tr", S.val("tr"))]
                if j >= 1:
                    sg_w.append(("vfn", vfn_after[j - 1] - 1))
                S.add("scalar", sg_w,
                      lambda e, s, t=tgt, q=qt:
                      e.activation(r_sb[:, q * D:(q + 1) * D], t[:, 0:D], AF.Sigmoid), inc="ssg")
                if qt == 1:
                    gate["sc2"] = ("ssg", S.val("ssg"))
                    gate["sc3"] = ("ssg", S.val("ssg"))
            for qt in range(2):
                tgt = pAVhi if qt == 0 else pOS
                w = [kact_done, cwv]
                if qt == 0 and j > 0:
                    w.append(("vfn", vfn_after[j - 1] - 3))
                if qt == 0 and j == 0:
                    w.append(b_v_done)
                if qt == 1:
                    w.append(("ssg", S.val("ssg") - 1))

                def fb3(e, s, t=tgt, q=qt, bu=buf):
                    inst = None
                    for oh in range(2):
                        for dc in range(CH):
                            inst = e.matmul(t[:, oh * 512:(oh + 1) * 512],
                                            kact[bu][:, dc * 256 + q * 128: dc * 256 + (q + 1) * 128],
                                            warv[:, dc * D + oh * 512: dc * D + oh * 512 + 512],
                                            start=(dc == 0), stop=(dc == CH - 1))
                    return inst
                S.add("tensor", w, fb3, inc="tf")
            tf_q0 = S.val("tf") - 1
            tf_q1 = S.val("tf")
            tf_after[j] = ("tf", tf_q1)
            S.add("vector", [ld, pld, ("vfn", S.val("vfn"))],
                  lambda e, s, b=j % 3:
                  e.tensor_tensor(xpart[:, :], parts[:, :], xoc[b][:, :], ALU.add),
                  inc="vfn")
            xpart_of[j] = S.val("vfn")
            for qt in range(2):
                src = pAVhi if qt == 0 else pOS
                w = [("tf", tf_q0 if qt == 0 else tf_q1), ("ssg", S.val("ssg") - (1 - qt)),
                     ("vfn", S.val("vfn"))]
                if j >= 2:
                    w.append(("doc%d" % (j % 2), 16 * (j // 2)))
                S.add("vector", w,
                      lambda e, s, t=src, q=qt:
                      e.tensor_tensor(fbuf[:, :], t[:, 0:D], r_sb[:, q * D:(q + 1) * D],
                                      ALU.mult), inc="vfn")
                S.add("vector", [("vfn", S.val("vfn"))],
                      lambda e, s, q=qt, jj=j:
                      e.tensor_tensor(outc[jj % 2][:, q * D:(q + 1) * D], fbuf[:, :],
                                      xpart[:, q * D:(q + 1) * D], ALU.add), inc="vfn")
            vfn_after[j] = S.val("vfn")
            S.add("gpsimd", [("vfn", S.val("vfn"))],
                  lambda e, s, jj=j:
                  e.dma_start(out_d[256 * jj:256 * (jj + 1), :].rearrange("(t p) k -> p t k", p=128),
                              outc[jj % 2][:, :].rearrange("p (t k) -> p t k", k=D)),
                  inc="doc%d" % (j % 2), incby=16)

        c_load(0)
        c_load(1)
        c_feed(0)
        c_feed(1)
        c_mixes(0)
        c_mixes(1)
        for j in range(NSLOT):
            nx = j + 2
            if nx < NSLOT:
                c_load(nx)
                c_feed(nx)
            if j + 1 < NSLOT and j + 1 >= 2:
                midfns = [lambda a=j + 1: c_mixes(a)]
            else:
                midfns = []
            c_stage_b(j, [], midfns)

        S.add("sync", [("doc0", 64), ("doc1", 64), ("dob", 16 * NSLOT)],
              lambda e, s: e.nop(), inc=None)

        with nc.Block() as block:
            def runner(oplist):
                def f(eng):
                    for op in oplist:
                        op(eng, sems)
                return f
            block.sync(runner(S.ops["sync"]))
            block.gpsimd(runner(S.ops["gpsimd"]))
            block.scalar(runner(S.ops["scalar"]))
            block.vector(runner(S.ops["vector"]))
            block.tensor(runner(S.ops["tensor"]))

    return nc


def _host_inputs(x, Wq, Wk, Wv, Wo, W_r, W_k, W_v, mu_r, mu_k):
    x = np.asarray(x, dtype=np.float32)
    c = np.ascontiguousarray
    shared = {
        "wq": c(np.asarray(Wq, np.float32).T),
        "wk": c(np.asarray(Wk, np.float32).T),
        "wvT": c(np.asarray(Wv, np.float32).T),
        "wo": c(np.asarray(Wo, np.float32)),
        "wrT": c(np.asarray(W_r, np.float32).T),
        "wkTf": c(np.asarray(W_k, np.float32).T),
        "wvTf": c(np.asarray(W_v, np.float32).T),
        "mur": c(np.asarray(mu_r, np.float32).reshape(8, 128).T),
        "muk": c(np.asarray(mu_k, np.float32).reshape(8, 128).T),
    }
    in_maps = []
    own_rows_all = []
    for core in range(8):
        b, par = core // 2, core % 2
        xb = c(x[b])
        rows = np.concatenate([np.arange(512 * j + 256 * par, 512 * j + 256 * par + 256)
                               for j in range(8)])
        own_rows_all.append(rows)
        x_own = c(xb[rows])
        xprev = np.zeros((8, 1024), np.float32)
        for j in range(8):
            t = 512 * j + 256 * par - 1
            if t >= 0:
                xprev[j] = xb[t]
        pp = np.arange(128)[:, None]
        cc_ = np.arange(640)[None, :]
        mask = np.where(cc_ < pp + 384 - 256 * par, -1e9, 0.0).astype(np.float32)
        im = dict(shared)
        im["x"] = xb
        im["x_own"] = x_own
        im["x_prev"] = c(xprev)
        im["maskt"] = c(mask)
        in_maps.append(im)
    return in_maps, own_rows_all


def kernel(L0_attn_x, L0_attn_Wq, L0_attn_Wk, L0_attn_Wv, L0_attn_Wo,
           L0_ffn_W_r, L0_ffn_W_k, L0_ffn_W_v, L0_ffn_mu_r, L0_ffn_mu_k):
    in_maps, own_rows = _host_inputs(L0_attn_x, L0_attn_Wq, L0_attn_Wk, L0_attn_Wv,
                                     L0_attn_Wo, L0_ffn_W_r, L0_ffn_W_k, L0_ffn_W_v,
                                     L0_ffn_mu_r, L0_ffn_mu_k)
    nc = build()
    res = run_bass_kernel_spmd(nc, in_maps, core_ids=list(range(8)))
    out = np.empty((4, 4096, 1024), np.float32)
    for core in range(8):
        b = core // 2
        out[b][own_rows[core]] = res.results[core]["out"]
    return out


# version 32
# speedup vs baseline: 1.1731x; 1.0300x over previous
import numpy as np
import ml_dtypes
import concourse.bass as bass
import concourse.mybir as mybir
from concourse.bass_utils import run_bass_kernel_spmd
from contextlib import ExitStack

F32 = mybir.dt.float32
BF16 = mybir.dt.bfloat16
AF = mybir.ActivationFunctionType
ALU = mybir.AluOpType
AX = mybir.AxisListType

D = 1024
CH = 8
NTILE = 32
NSLOT = 8
SCALE = np.float32(1.0 / 32.0)
EBIAS = np.float32(-44.0)
f32 = np.float32


class Sched:

    def __init__(self):
        self.ops = {"sync": [], "gpsimd": [], "scalar": [], "vector": [], "tensor": []}
        self.cnt = {}

    def inc(self, name, by=1):
        self.cnt[name] = self.cnt.get(name, 0) + by
        return self.cnt[name]

    def val(self, name):
        return self.cnt.get(name, 0)

    def add(self, eng, waits, body, inc=None, incby=1):
        waits = [w for w in waits if w is not None]
        post = None
        if inc is not None:
            post = (inc, self.inc(inc, incby))

        def fn(e, sems, _w=waits, _b=body, _p=post, _by=incby):
            for s, v in _w:
                if v > 0:
                    e.wait_ge(sems[s], v)
            inst = _b(e, sems)
            if _p is not None:
                assert inst is not None
                inst.then_inc(sems[_p[0]], _by)

        self.ops[eng].append(fn)


def build(taps=()):
    nc = bass.Bass("TRN2", target_bir_lowering=False)

    x_d = nc.dram_tensor("x", [4096, D], F32, kind="ExternalInput")
    xo_d = nc.dram_tensor("x_own", [2048, D], F32, kind="ExternalInput")
    xp_d = nc.dram_tensor("x_prev", [8, D], F32, kind="ExternalInput")
    wq_d = nc.dram_tensor("wq", [D, D], F32, kind="ExternalInput")
    wk_d = nc.dram_tensor("wk", [D, D], F32, kind="ExternalInput")
    wvT_d = nc.dram_tensor("wvT", [D, D], F32, kind="ExternalInput")
    wo_d = nc.dram_tensor("wo", [D, D], F32, kind="ExternalInput")
    wrT_d = nc.dram_tensor("wrT", [D, D], F32, kind="ExternalInput")
    wkTf_d = nc.dram_tensor("wkTf", [D, D], F32, kind="ExternalInput")
    wvTf_d = nc.dram_tensor("wvTf", [D, D], F32, kind="ExternalInput")
    mur_d = nc.dram_tensor("mur", [128, CH], F32, kind="ExternalInput")
    muk_d = nc.dram_tensor("muk", [128, CH], F32, kind="ExternalInput")
    maskt_d = nc.dram_tensor("maskt", [128, 640], F32, kind="ExternalInput")
    out_d = nc.dram_tensor("out", [2048, D], F32, kind="ExternalOutput")
    wvp_d = nc.dram_tensor("wvp_stage", [128, CH * D], BF16, kind="Internal")
    xof_d = nc.dram_tensor("xof_stage", [128, NSLOT * CH * 256], BF16, kind="Internal")
    wrs_d = nc.dram_tensor("wrs_stage", [D, D], BF16, kind="Internal")
    wks_d = nc.dram_tensor("wks_stage", [D, D], BF16, kind="Internal")
    wvs_d = nc.dram_tensor("wvs_stage", [D, D], BF16, kind="Internal")

    tap_d = {}
    for t in taps:
        shp = {"xnfm": [128, CH * 4096], "xntm": [128, NTILE * D],
               "tall": [128, NSLOT * CH * 256], "m": [128, CH * D],
               "wvp": [128, CH * D]}[t]
        tap_d[t] = nc.dram_tensor("tap_" + t, shp, F32, kind="ExternalOutput")

    S = Sched()
    sem_names = ["dx", "dw", "dvs", "dob", "dci0", "dci1", "doc0", "doc1", "dcp", "dxa", "dxb", "dwh0", "dmk", "dci2", "dtx", "vka", "gfn", "dxf", "dnf0", "dnf1", "dwv0", "dmu", "dk0", "dk1", "dwr2", "dvv", "dwv1",
                 "tm0", "tt1", "tq1", "tt2", "tsc", "tav", "tsu", "to", "tdn",
                 "ttc", "tk", "tr", "tf",
                 "se0", "st1", "sa1", "sa2", "sex", "soe", "slc", "srk", "ssg",
                 "ve1", "vl1", "ve2", "vl2", "vmk", "vav", "vrc", "vcst",
                 "vec", "vlc", "vmx", "vfn", "gio"]

    with ExitStack() as ctx:
        sb = lambda name, shape, dt: ctx.enter_context(nc.sbuf_tensor(name, shape, dt))
        ident = sb("ident", [128, 128], BF16)
        pidx = sb("pidx", [128, 1], F32)
        ones_bf = sb("onesbf", [128, 1], BF16)
        maskt = sb("maskt_s", [128, 640], BF16)
        t_all = sb("t_all", [128, NSLOT * CH * 256], BF16)

        e_m = ExitStack()
        m_sb = e_m.enter_context(nc.sbuf_tensor("m_sb", [128, CH * D], BF16))
        xo1 = [e_m.enter_context(nc.sbuf_tensor(f"xo1_{i}", [128, 2 * D], F32)) for i in range(2)]
        col1 = {n: e_m.enter_context(nc.sbuf_tensor("c1" + n, [128, 16], F32))
                for n in ("s1", "s2", "c6", "cv", "u", "rs", "nm")}
        xot = e_m.enter_context(nc.sbuf_tensor("xot", [128, 16 * D], BF16))
        xof = e_m.enter_context(nc.sbuf_tensor("xof", [128, NSLOT * CH * 256], BF16))
        e_a0 = ExitStack()
        wa_sb = e_a0.enter_context(nc.sbuf_tensor("wa", [128, CH * D], BF16))
        wb_sb = e_a0.enter_context(nc.sbuf_tensor("wb", [128, CH * D], BF16))
        wvpe = e_a0.enter_context(nc.sbuf_tensor("wvpe", [128, 2 * D], BF16))
        wc_sb = e_a0.enter_context(nc.sbuf_tensor("wc", [128, CH * D], BF16))
        wd_sb = e_a0.enter_context(nc.sbuf_tensor("wd", [128, CH * D], BF16))
        iota_f = e_a0.enter_context(nc.sbuf_tensor("iotaf", [128, 128], F32))
        e_a0.close()
        e_m.close()

        e_b = ExitStack()
        xn_fm = e_b.enter_context(nc.sbuf_tensor("xn_fm", [128, CH * 4096], BF16))
        xn_tm = e_b.enter_context(nc.sbuf_tensor("xn_tm", [128, NTILE * D], BF16))
        wv_p = e_b.enter_context(nc.sbuf_tensor("wv_p", [128, CH * D], BF16))
        x_grp = [e_b.enter_context(nc.sbuf_tensor(f"x_grp{i}", [128, 2 * D], F32)) for i in range(2)]
        col2 = {n: e_b.enter_context(nc.sbuf_tensor("c2" + n, [128, NTILE], F32))
                for n in ("s1", "s2", "cv", "u", "rs")}
        col2["c6"] = col2["u"]
        col2["nm"] = col2["cv"]
        ptbuf = e_b.enter_context(nc.sbuf_tensor("ptbuf", [128, 3 * 256], BF16))
        avxt = e_b.enter_context(nc.sbuf_tensor("avxt", [128, CH * 256], BF16))
        outb = e_b.enter_context(nc.sbuf_tensor("outb", [128, 2 * D], F32))
        recip = e_b.enter_context(nc.sbuf_tensor("recip", [128, 16], F32))
        e_b.close()

        e_c = ExitStack()
        warr = e_c.enter_context(nc.sbuf_tensor("warr", [128, CH * D], BF16))
        wark = e_c.enter_context(nc.sbuf_tensor("wark", [128, CH * D], BF16))
        warv = e_c.enter_context(nc.sbuf_tensor("warv", [128, CH * D], BF16))
        mur = e_c.enter_context(nc.sbuf_tensor("mur_s", [128, CH], F32))
        muk = e_c.enter_context(nc.sbuf_tensor("muk_s", [128, CH], F32))
        xprev = e_c.enter_context(nc.sbuf_tensor("xprev", [8, D], F32))
        xprevt = e_c.enter_context(nc.sbuf_tensor("xprevt", [8, D], BF16))
        xprevf = e_c.enter_context(nc.sbuf_tensor("xprevf", [128, CH * 8], BF16))
        xoc = [e_c.enter_context(nc.sbuf_tensor(f"xoc{i}", [128, 2 * D], F32)) for i in range(3)]
        parts = e_c.enter_context(nc.sbuf_tensor("parts", [128, 2 * D], F32))
        colc = {n: e_c.enter_context(nc.sbuf_tensor("cc" + n, [128, 20], F32))
                for n in ("s1", "s2", "c6", "cv", "u", "rs", "nm")}
        xnopf_b = [e_c.enter_context(nc.sbuf_tensor(f"xnopf{i}", [128, CH * 257], BF16))
                   for i in range(2)]
        diffb = e_c.enter_context(nc.sbuf_tensor("diffb", [128, CH * 256], BF16))
        xr_sb = [e_c.enter_context(nc.sbuf_tensor(f"xr_sb{i}", [128, CH * 256], BF16)) for i in range(2)]
        xk_sb = [e_c.enter_context(nc.sbuf_tensor(f"xk_sb{i}", [128, CH * 256], BF16)) for i in range(2)]
        rk_sb = e_c.enter_context(nc.sbuf_tensor("rk_sb", [128, CH * 256], BF16))
        kact = [e_c.enter_context(nc.sbuf_tensor(f"kact{i}", [128, CH * 256], BF16)) for i in range(2)]
        r_sb = e_c.enter_context(nc.sbuf_tensor("r_sb", [128, 2 * D], F32))
        xpart = e_c.enter_context(nc.sbuf_tensor("xpart", [128, 2 * D], F32))
        fbuf = e_c.enter_context(nc.sbuf_tensor("fbuf", [128, D], F32))
        outc = [e_c.enter_context(nc.sbuf_tensor(f"outc{i}", [128, 2 * D], F32)) for i in range(2)]
        e_c.close()

        pool = ctx.enter_context(nc.psum_tensor("pool", [128, 4096], F32))
        pSC = [pool[:, 0:512], pool[:, 512:1024]]
        pSCb = [pool[:, 512 * i:512 * (i + 1)].bitcast(BF16) for i in range(6)]
        pAV = pool[:, 1024:3072]
        pAVlo = pool[:, 1024:2048]
        pAVhi = pool[:, 2048:3072]
        pOS = pool[:, 3072:4096]

        sems = {}
        for n in sem_names:
            sems[n] = ctx.enter_context(nc.semaphore("sem_" + n))

        gate = {"sc0": None, "sc1": None, "sc2": None, "sc3": None, "sc4": None, "sc5": None}

        def gwait(key):
            g = gate[key]
            return [g] if g else []

        def ln_group(xslices, xnslices, x3d, cols, k0, xgate, sA, vL, pd=128,
                     apply_extra=None):
            G = len(xslices)
            sA0 = S.val(sA)
            vL0 = S.val(vL)
            cs = lambda n: cols[n][0:pd, k0:k0 + G]
            n = 1024
            for g in range(G):
                S.add("scalar",
                      ([xgate] + ([apply_extra] if apply_extra else [])) if g == 0 else [],
                      lambda e, s, o=xnslices[g], i=xslices[g],
                      a=cols["s2"][0:pd, k0 + g:k0 + g + 1]:
                      e.activation(o, i, AF.Square, accum_out=a), inc=sA)
            S.add("vector", [xgate],
                  lambda e, s, o=cs("s1"), i=x3d: e.reduce_sum(o, i, axis=AX.X), inc=vL)
            S.add("vector", [(vL, vL0 + 1)],
                  lambda e, s, o=cs("c6"), i=cs("s1"): e.tensor_tensor(o, i, i, ALU.mult),
                  inc=vL)
            S.add("vector", [(sA, sA0 + G), (vL, vL0 + 2)],
                  lambda e, s, o=cs("cv"), i=cs("c6"), j2=cs("s2"), sc=-1.0 / n:
                  e.scalar_tensor_tensor(o, i, sc, j2, ALU.mult, ALU.add), inc=vL)
            S.add("scalar", [(vL, vL0 + 3)],
                  lambda e, s, o=cs("u"), i=cs("cv"), sc=f32(1.0 / n):
                  e.activation(o, i, AF.Ln, bias=f32(1e-5), scale=sc), inc=sA)
            S.add("scalar", [(sA, sA0 + G + 1)],
                  lambda e, s, o=cs("rs"), i=cs("u"): e.activation(o, i, AF.Exp, scale=f32(-0.5)),
                  inc=sA)
            S.add("vector", [(sA, sA0 + G + 2)],
                  lambda e, s, o=cs("nm"), i=cs("s1"), r=cs("rs"), sc=-1.0 / n:
                  e.scalar_tensor_tensor(o, i, sc, r, ALU.mult, ALU.mult), inc=vL)
            for g in range(G):
                S.add("scalar", [(vL, vL0 + 4)] if g == 0 else [],
                      lambda e, s, o=xnslices[g], i=xslices[g],
                      b=cols["nm"][0:pd, k0 + g:k0 + g + 1],
                      r=cols["rs"][0:pd, k0 + g:k0 + g + 1]:
                      e.activation(o, i, AF.Identity, bias=b, scale=r), inc=sA)

        def transpose_tile(src, dst_ap, bank, readygate, tsem, vsem, evac_extra=None):
            def tbody(e, s, b=bank, src_=src):
                inst = None
                for cc in range(CH):
                    inst = e.matmul(pSCb[b][:, cc * 128:(cc + 1) * 128],
                                    src_[:, cc * 128:(cc + 1) * 128], ident[:, :],
                                    is_transpose=True, start=(cc == 0), stop=(cc == CH - 1))
                return inst
            S.add("tensor", [readygate, ("vcst", 1)] + gwait("sc%d" % bank), tbody, inc=tsem)
            tv = S.val(tsem)
            S.add("vector", [(tsem, tv)] + ([evac_extra] if evac_extra else []),
                  lambda e, s, o=dst_ap, b=bank:
                  e.tensor_copy(o, pSCb[b][:, 0:CH * 128]), inc=vsem)
            gate["sc%d" % bank] = (vsem, S.val(vsem))

        for hh in range(2):
            for (dst, dr) in ((wa_sb, wq_d), (wb_sb, wk_d)):
                S.add("gpsimd", [],
                      lambda e, s, d=dst, r=dr, h=hh:
                      e.dma_start(d[:, :].rearrange("p (c k) -> p c k", k=D)[:, :, h * 512:(h + 1) * 512],
                                  r[:, :].rearrange("(c p) k -> p c k", p=128)[:, :, h * 512:(h + 1) * 512]),
                      inc="dwh0" if hh == 0 else "dw", incby=16)
        for hh in range(2):
            for (dst, dr) in ((wc_sb, wvT_d), (wd_sb, wo_d)):
                S.add("gpsimd", [],
                      lambda e, s, d=dst, r=dr, h=hh:
                      e.dma_start(d[:, :].rearrange("p (c k) -> p c k", k=D)[:, :, h * 512:(h + 1) * 512],
                                  r[:, :].rearrange("(c p) k -> p c k", p=128)[:, :, h * 512:(h + 1) * 512]),
                      inc="dwv0" if hh == 0 else "dwv1", incby=16)
        S.add("gpsimd", [], lambda e, s: e.dma_start(maskt[:, :], maskt_d[:, :]),
              inc="dmk", incby=16)
        S.add("gpsimd", [], lambda e, s: e.iota(
            iota_f[:, :], [[1, 128]], channel_multiplier=0,
            allow_small_or_imprecise_dtypes=True), inc=None)
        S.add("gpsimd", [], lambda e, s: e.iota(
            pidx[:, :], [[0, 1]], channel_multiplier=1,
            allow_small_or_imprecise_dtypes=True), inc="gio")
        S.add("vector", [("gio", 1)],
              lambda e, s: e.tensor_scalar(ident[:, :], iota_f[:, :], pidx[:, :],
                                           None, ALU.is_equal), inc="vcst")
        S.add("vector", [], lambda e, s: e.memset(ones_bf[:, :], 1.0), inc="vcst")

        ast = {}
        tt1_after = {}

        def a1_load(j):
            sem = "dxa" if j % 2 == 0 else "dxb"
            wt = []
            if j >= 2:
                wt = [("sa1", ast[j - 2]["apply"]), ("vl1", ast[j - 2]["reduce"])]
            S.add("sync", wt,
                  lambda e, s, jj=j:
                  e.dma_start(xo1[jj % 2][:, :].rearrange("p (t k) -> p t k", k=D),
                              xo_d[256 * jj:256 * (jj + 1), :].rearrange("(t p) k -> p t k", p=128)),
                  inc=sem, incby=16)
            ast[j] = {"xg": (sem, S.val(sem))}

        def a1_pieces(j):
            st = ast[j]
            k0 = 2 * j
            cs = lambda n: col1[n][:, k0:k0 + 2]
            bufx = xo1[j % 2]
            for g in range(2):
                S.add("scalar",
                      [st["xg"]] if g == 0 else [],
                      lambda e, s, o=xot[:, (k0 + g) * D:(k0 + g + 1) * D],
                      ii=bufx[:, g * D:(g + 1) * D],
                      a=col1["s2"][:, k0 + g:k0 + g + 1]:
                      e.activation(o, ii, AF.Square, accum_out=a), inc="sa1")
            st["sq"] = S.val("sa1")
            vL0 = S.val("vl1")
            S.add("vector", [st["xg"]],
                  lambda e, s, o=cs("s1"), i=bufx[:, :].rearrange("p (t k) -> p t k", k=D):
                  e.reduce_sum(o, i, axis=AX.X), inc="vl1")
            st["reduce"] = S.val("vl1")
            S.add("vector", [("vl1", vL0 + 1)],
                  lambda e, s, o=cs("c6"), i=cs("s1"): e.tensor_tensor(o, i, i, ALU.mult),
                  inc="vl1")
            S.add("vector", [("sa1", st["sq"]), ("vl1", vL0 + 2)],
                  lambda e, s, o=cs("cv"), i=cs("c6"), j2=cs("s2"):
                  e.scalar_tensor_tensor(o, i, -1.0 / 1024, j2, ALU.mult, ALU.add), inc="vl1")
            st["cvar"] = S.val("vl1")
            S.add("scalar", [("vl1", st["cvar"])],
                  lambda e, s, o=cs("u"), i=cs("cv"):
                  e.activation(o, i, AF.Ln, bias=f32(1e-5), scale=f32(1.0 / 1024)), inc="sa1")
            st["ln"] = S.val("sa1")
            S.add("scalar", [("sa1", st["ln"])],
                  lambda e, s, o=cs("rs"), i=cs("u"): e.activation(o, i, AF.Exp, scale=f32(-0.5)),
                  inc="sa1")
            st["exp"] = S.val("sa1")
            S.add("vector", [("sa1", st["exp"])],
                  lambda e, s, o=cs("nm"), i=cs("s1"), r=cs("rs"):
                  e.scalar_tensor_tensor(o, i, -1.0 / 1024, r, ALU.mult, ALU.mult), inc="vl1")
            st["negm"] = S.val("vl1")
            for g in range(2):
                S.add("scalar", [("vl1", st["negm"])] if g == 0 else [],
                      lambda e, s, o=xot[:, (k0 + g) * D:(k0 + g + 1) * D],
                      ii=bufx[:, g * D:(g + 1) * D],
                      b=col1["nm"][:, k0 + g:k0 + g + 1], r=col1["rs"][:, k0 + g:k0 + g + 1]:
                      e.activation(o, ii, AF.Identity, bias=b, scale=r), inc="sa1")
            st["apply"] = S.val("sa1")
            if j + 2 < NSLOT:
                a1_load(j + 2)

        a1_load(0)
        a1_load(1)

        g = 0
        group_emitters = []
        for ccc in range(CH):
            for hc in range(2):
                def mk_m(c=ccc, h=hc):
                    def em():
                        global_bank = emctx["g"] % 2

                        def mbody(e, s, b=global_bank, c2=c, h2=h):
                            inst = None
                            for er in range(CH):
                                inst = e.matmul(pSC[b][:, 0:512],
                                                wa_sb[:, er * D + c2 * 128: er * D + (c2 + 1) * 128],
                                                wb_sb[:, er * D + h2 * 512: er * D + h2 * 512 + 512],
                                                start=(er == 0), stop=(er == CH - 1))
                            return inst
                        need = [("dwh0", 32)] if (c < 4 and h == 0) else [("dwh0", 32), ("dw", 32)]
                        S.add("tensor", need + gwait("sc%d" % global_bank), mbody, inc="tm0")
                        tv = S.val("tm0")
                        S.add("scalar", [("tm0", tv)],
                              lambda e, s, b=global_bank, c2=c, h2=h:
                              e.copy(m_sb[:, c2 * D + h2 * 512: c2 * D + h2 * 512 + 512],
                                     pSC[b][:, 0:512]), inc="se0")
                        gate["sc%d" % global_bank] = ("se0", S.val("se0"))
                        emctx["g"] += 1
                    return em
                group_emitters.append(mk_m())
        wvT_marker = len(group_emitters)
        for ec in range(CH):
            for oh in range(2):
                def mk_v(c=ec, h=oh):
                    def em():
                        global_bank = emctx["g"] % 2

                        def vbody(e, s, b=global_bank, c2=c, h2=h):
                            inst = None
                            for vr in range(CH):
                                inst = e.matmul(pSC[b][:, 0:512],
                                                wc_sb[:, vr * D + c2 * 128: vr * D + (c2 + 1) * 128],
                                                wd_sb[:, vr * D + h2 * 512: vr * D + h2 * 512 + 512],
                                                start=(vr == 0), stop=(vr == CH - 1))
                            return inst
                        vneed = [("dwv0", 32)] if (c < 4 and h == 0) else \
                            [("dwv0", 32), ("dwv1", 32)]
                        S.add("tensor", vneed + gwait("sc%d" % global_bank),
                              vbody, inc="tm0")
                        tv = S.val("tm0")
                        wt = [("tm0", tv)]
                        if c >= 2 and h == 0:
                            wt.append(("dvs", 16 * c))
                        S.add("scalar", wt,
                              lambda e, s, b=global_bank, sl=c % 2, h2=h:
                              e.copy(wvpe[:, sl * D + h2 * 512: sl * D + h2 * 512 + 512],
                                     pSC[b][:, 0:512]), inc="se0")
                        gate["sc%d" % global_bank] = ("se0", S.val("se0"))
                        emctx["g"] += 1
                        if h == 1:
                            S.add("gpsimd", [("se0", S.val("se0"))],
                                  lambda e, s, c2=c, sl=c % 2:
                                  e.dma_start(wvp_d[:, c2 * D:(c2 + 1) * D],
                                              wvpe[:, sl * D:(sl + 1) * D]),
                                  inc="dvs", incby=16)
                    return em
                group_emitters.append(mk_v())

        def emit_a1_transposes(j):
            st = ast[j]
            for tau in range(2):
                i = 2 * j + tau
                apply_done = ("sa1", st["apply"] - 1 + tau)
                dst_ap = xof[:, 2048 * j:2048 * (j + 1)].rearrange(
                    "p (c q) -> p c q", q=256)[:, :, tau * 128:tau * 128 + 128]
                transpose_tile(xot[:, i * D:(i + 1) * D], dst_ap, i % 2,
                               apply_done, "tt1", "ve1")
            st["evac"] = ("ve1", S.val("ve1"))
            S.add("sync", [st["evac"]],
                  lambda e, s, jj=j:
                  e.dma_start(xof_d[:, 2048 * jj:2048 * (jj + 1)],
                              xof[:, 2048 * jj:2048 * (jj + 1)]),
                  inc="dxf", incby=16)

        def emit_a1_tensor(j):
            st = ast[j]
            evac_done = st["evac"]
            for ccp in range(CH):
                bank = ccp % 2

                def tqb(e, s, b=bank, c=ccp, jj=j):
                    inst = None
                    for cr in range(CH):
                        inst = e.matmul(pSC[b][:, 0:256],
                                        m_sb[:, cr * D + c * 128: cr * D + (c + 1) * 128],
                                        xof[:, 2048 * jj + cr * 256: 2048 * jj + (cr + 1) * 256],
                                        start=(cr == 0), stop=(cr == CH - 1))
                    return inst
                S.add("tensor", [evac_done] + gwait("sc%d" % bank), tqb, inc="tq1")
                tv = S.val("tq1")
                S.add("scalar", [("tq1", tv)],
                      lambda e, s, b=bank, jj=j, c=ccp:
                      e.copy(t_all[:, jj * 2048 + c * 256: jj * 2048 + (c + 1) * 256],
                             pSC[b][:, 0:256]), inc="st1")
                gate["sc%d" % bank] = ("st1", S.val("st1"))

        a1_pieces(0)
        a1_pieces(1)
        emit_a1_transposes(0)
        emit_a1_transposes(1)
        emctx = {"g": 0, "dw_a0": 0}
        for idx, em in enumerate(group_emitters):
            em()
            if idx % 4 == 3 and idx // 4 + 2 < NSLOT:
                a1_pieces(idx // 4 + 2)

        for _j in range(2, NSLOT):
            emit_a1_transposes(_j)
            emit_a1_tensor(_j - 2)
        emit_a1_tensor(NSLOT - 2)
        emit_a1_tensor(NSLOT - 1)
        a1_t_done = ("tq1", S.val("tq1"))
        a1_s_done = [("sa1", S.val("sa1")), ("st1", S.val("st1"))]
        S.add("gpsimd", a1_s_done + [a1_t_done, ("dvs", S.val("dvs")), ("dxf", 16 * NSLOT)],
              lambda e, s: e.dma_start(wv_p[:, :], wvp_d[:, :]), inc="dvs", incby=16)
        wvp_ready = ("dvs", S.val("dvs"))

        exp_of_block = {}
        av_of_block = {}
        pt_gate = [None, None, None]

        ln_state = {}
        sa2_apply = {}

        def emit_load(gi):
            if gi == 0:
                wt = a1_s_done + [a1_t_done, ("vl1", 4 * NSLOT), ("dxf", 16 * NSLOT)]
            elif gi == 1:
                wt = [("dxf", 16 * NSLOT)]
            else:
                st = ln_state[gi - 2]
                wt = [("sa2", st["apply"]), ("vl2", st["reduce"])]
            S.add("sync", wt,
                  lambda e, s, g=gi:
                  e.dma_start(x_grp[g % 2][:, :].rearrange("p (t k) -> p t k", k=D),
                              x_d[256 * g:256 * (g + 1), :].rearrange("(t p) k -> p t k", p=128)),
                  inc="dxa" if gi % 2 == 0 else "dxb", incby=16)
            ln_state[gi] = {"xg": ("dxa" if gi % 2 == 0 else "dxb",
                                   S.val("dxa" if gi % 2 == 0 else "dxb"))}

        def emit_ln_vec(gi):
            st = ln_state[gi]
            buf = x_grp[gi % 2]
            k0 = 2 * gi
            cs = lambda n: col2[n][:, k0:k0 + 2]
            vL0 = S.val("vl2")
            S.add("vector", [st["xg"]],
                  lambda e, s, o=cs("s1"), i=buf[:, :].rearrange("p (t k) -> p t k", k=D):
                  e.reduce_sum(o, i, axis=AX.X), inc="vl2")
            st["reduce"] = S.val("vl2")
            S.add("vector", [("vl2", vL0 + 1)],
                  lambda e, s, o=cs("c6"), i=cs("s1"): e.tensor_tensor(o, i, i, ALU.mult),
                  inc="vl2")
            S.add("vector", [("sa2", st["sq"]), ("vl2", vL0 + 2)],
                  lambda e, s, o=cs("cv"), i=cs("c6"), j2=cs("s2"):
                  e.scalar_tensor_tensor(o, i, -1.0 / 1024, j2, ALU.mult, ALU.add), inc="vl2")
            st["cvar"] = S.val("vl2")

        def emit_ln_sq(gi):
            st = ln_state[gi]
            buf = x_grp[gi % 2]
            k0 = 2 * gi
            for g in range(2):
                i = k0 + g
                S.add("scalar", [st["xg"], ("tt2", S.val("tt2"))] if g == 0 else [],
                      lambda e, s, o=xn_tm[:, i * D:(i + 1) * D], ii=buf[:, g * D:(g + 1) * D],
                      a=col2["s2"][:, i:i + 1]:
                      e.activation(o, ii, AF.Square, accum_out=a), inc="sa2")
            st["sq"] = S.val("sa2")

        def emit_ln_s_ln(gi):
            st = ln_state[gi]
            k0 = 2 * gi
            cs = lambda n: col2[n][:, k0:k0 + 2]
            S.add("scalar", [("vl2", st["cvar"])],
                  lambda e, s, o=cs("u"), i=cs("cv"):
                  e.activation(o, i, AF.Ln, bias=f32(1e-5), scale=f32(1.0 / 1024)), inc="sa2")
            st["ln"] = S.val("sa2")

        def emit_ln_s_exp(gi):
            st = ln_state[gi]
            k0 = 2 * gi
            cs = lambda n: col2[n][:, k0:k0 + 2]
            S.add("scalar", [("sa2", st["ln"])],
                  lambda e, s, o=cs("rs"), i=cs("u"): e.activation(o, i, AF.Exp, scale=f32(-0.5)),
                  inc="sa2")

        def emit_ln_negm(gi):
            st = ln_state[gi]
            k0 = 2 * gi
            cs = lambda n: col2[n][:, k0:k0 + 2]
            S.add("vector", [("sa2", st["exp"])],
                  lambda e, s, o=cs("nm"), i=cs("s1"), r=cs("rs"):
                  e.scalar_tensor_tensor(o, i, -1.0 / 1024, r, ALU.mult, ALU.mult), inc="vl2")
            st["negm"] = S.val("vl2")

        def emit_ln_apply(gi):
            st = ln_state[gi]
            buf = x_grp[gi % 2]
            k0 = 2 * gi
            for g in range(2):
                i = k0 + g
                S.add("scalar", [("vl2", st["negm"])] if g == 0 else [],
                      lambda e, s, o=xn_tm[:, i * D:(i + 1) * D], ii=buf[:, g * D:(g + 1) * D],
                      b=col2["nm"][:, i:i + 1], r=col2["rs"][:, i:i + 1]:
                      e.activation(o, ii, AF.Identity, bias=b, scale=r), inc="sa2")
            st["apply"] = S.val("sa2")
            sa2_apply[gi] = S.val("sa2")

        def emit_transposes(j):
            gate["sc4"] = ("vav", S.val("vav"))
            gate["sc5"] = ("vav", S.val("vav"))
            for i in range(4 * j, 4 * j + 4):
                gi = i // 2
                apply_done = ("sa2", sa2_apply[gi] - 1 + (i % 2))
                dst_ap = xn_fm[:, :].rearrange("p (c t) -> p c t", t=4096)[:, :, i * 128:(i + 1) * 128]
                transpose_tile(xn_tm[:, i * D:(i + 1) * D], dst_ap, 4 + i % 2,
                               apply_done, "tt2", "ve2")

        def emit_B(j, inject):
            E = 4 * j + 4

            def emit_scores(b):
                bank = b % 2

                def scb(e, s, bk=bank, jj=j, bb=b):
                    inst = None
                    for cc in range(CH):
                        inst = e.matmul(pSC[bk][:, 0:256],
                                        xn_fm[:, cc * 4096 + 128 * bb: cc * 4096 + 128 * (bb + 1)],
                                        t_all[:, jj * 2048 + cc * 256: jj * 2048 + (cc + 1) * 256],
                                        start=(cc == 0), stop=(cc == CH - 1))
                    return inst
                S.add("tensor", [("ve2", ve2_of[j])] + gwait("sc%d" % bank), scb, inc="tsc")
                tscv = S.val("tsc")
                k = b - 4 * j
                if k >= 0:
                    S.add("vector", [("tsc", tscv), ("dmk", 16)],
                          lambda e, s, bk=bank, kk=k:
                          e.tensor_tensor(pSC[bk][:, 0:256], pSC[bk][:, 0:256],
                                          maskt[:, 384 - 128 * kk: 640 - 128 * kk], ALU.add),
                          inc="vmk")
                    exp_w = [("vmk", S.val("vmk"))]
                else:
                    exp_w = [("tsc", tscv)]
                pslot = b % 3
                if pt_gate[pslot] is not None:
                    exp_w = exp_w + [pt_gate[pslot]]
                S.add("scalar", exp_w,
                      lambda e, s, bk=bank, ps=pslot:
                      e.activation(ptbuf[:, ps * 256:(ps + 1) * 256], pSC[bk][:, 0:256],
                                   AF.Exp, bias=EBIAS, scale=SCALE), inc="sex")
                exp_of_block[(j, b)] = S.val("sex")
                gate["sc%d" % bank] = ("sex", S.val("sex"))

            def emit_sums_avxt(b):
                pslot = b % 3
                w0 = [("sex", exp_of_block[(j, b)])]
                if b == 0 and j > 0:
                    w0 += [("soe", 2 * j), ("vav", 8 * j)]

                def sums_avxt(e, s, jj=j, bb=b, ps=pslot, first=(b == 0), last=(b == E - 1)):
                    for qt in range(2):
                        e.matmul(pOS[:, qt:qt + 1],
                                 ptbuf[:, ps * 256 + qt * 128: ps * 256 + (qt + 1) * 128],
                                 ones_bf[:, :],
                                 start=(first and qt == 0), stop=(last and qt == 1))
                    inst = None
                    for cc in range(CH):
                        inst = e.matmul(pAV[:, cc * 256:(cc + 1) * 256],
                                        xn_tm[:, bb * D + cc * 128: bb * D + (cc + 1) * 128],
                                        ptbuf[:, ps * 256:(ps + 1) * 256],
                                        start=(first and (cc % 2 == 0)),
                                        stop=(last and (cc % 2 == 1)))
                    return inst
                S.add("tensor", w0, sums_avxt, inc="tav")
                av_of_block[(j, b)] = S.val("tav")
                pt_gate[pslot] = ("tav", S.val("tav"))

            for b in range(E):
                emit_scores(b)
                if b < len(inject):
                    inject[b]()
                if b >= 2:
                    emit_sums_avxt(b - 2)
            for fn in inject[E:]:
                fn()
            emit_sums_avxt(E - 2)
            emit_sums_avxt(E - 1)

            S.add("tensor", [], lambda e, s: e.nop(), inc="tsu")
            vav0 = S.val("vav")
            for cc in range(CH):
                S.add("vector", [("tav", av_of_block[(j, E - 1)])] if cc == 0 else [],
                      lambda e, s, c=cc:
                      e.tensor_copy(avxt[:, c * 256:(c + 1) * 256],
                                    pAV[:, c * 256:(c + 1) * 256]),
                      inc="vav")
            S.add("vector", [("tsu", S.val("tsu"))],
                  lambda e, s, jj=j: e.reciprocal(recip[:, 2 * jj:2 * jj + 2], pOS[:, 0:2]),
                  inc="vrc")
            for qt in range(2):
                tgt = pOS if qt == 0 else pAVlo
                for cc in range(CH):
                    need = cc + 1 if qt == 0 else max(cc + 1, 4)
                    w = [("vav", vav0 + need)]
                    if cc == 0:
                        w.append(("vrc", S.val("vrc")))
                        if j == 0 and qt == 0:
                            w.append(wvp_ready)

                    def ob(e, s, t=tgt, q=qt, c=cc):
                        inst = None
                        for oh in range(2):
                            inst = e.matmul(t[:, oh * 512:(oh + 1) * 512],
                                            avxt[:, c * 256 + q * 128: c * 256 + (q + 1) * 128],
                                            wv_p[:, c * D + oh * 512: c * D + oh * 512 + 512],
                                            start=(c == 0), stop=(c == CH - 1))
                        return inst
                    S.add("tensor", w, ob, inc="to")
                tov = S.val("to")
                wse = [("to", tov), ("vrc", S.val("vrc")), ("dob", 16 * j)]
                S.add("scalar", wse,
                      lambda e, s, t=tgt, q=qt, jj=j:
                      e.activation(outb[:, q * D:(q + 1) * D],
                                   t[:, 0:D], AF.Identity, scale=recip[:, 2 * jj + q: 2 * jj + q + 1]),
                      inc="soe")
            S.add("gpsimd", [("soe", S.val("soe"))],
                  lambda e, s, jj=j:
                  e.dma_start(out_d[256 * jj:256 * (jj + 1), :].rearrange("(t p) k -> p t k", p=128),
                              outb[:, :].rearrange("p (t k) -> p t k", k=D)),
                  inc="dob", incby=16)

        ve2_of = {}
        emit_load(0)
        emit_load(1)
        for j in range(NSLOT):
            ga, gb = 2 * j, 2 * j + 1
            emit_ln_sq(ga)
            emit_ln_sq(gb)
            ln_state[ga]["exp"] = S.val("sa2") + 2
            ln_state[gb]["exp"] = S.val("sa2") + 4
            emit_ln_vec(ga)
            emit_ln_vec(gb)
            if j + 1 < NSLOT:
                pass
            inject = [lambda a=ga: emit_ln_s_ln(a),
                      lambda a=ga: emit_ln_s_exp(a),
                      lambda a=ga: emit_ln_negm(a),
                      lambda a=gb: emit_ln_s_ln(a),
                      lambda a=gb: emit_ln_s_exp(a),
                      lambda a=gb: emit_ln_negm(a),
                      lambda a=ga: emit_ln_apply(a),
                      lambda a=gb: emit_ln_apply(a)]
            if j == 0:
                for fn in inject:
                    fn()
                emit_transposes(0)
                ve2_of[0] = S.val("ve2")
                if NSLOT > 1:
                    emit_load(2)
                    emit_load(3)
                continue
            emit_B(j - 1, inject)
            if j == 1:
                for (sd, rd) in ((wrs_d, wrT_d), (wks_d, wkTf_d), (wvs_d, wvTf_d)):
                    S.add("gpsimd", [], lambda e, s, d=sd, r=rd: e.dma_start(d[:, :], r[:, :]),
                          inc="dtx", incby=16)
            emit_transposes(j)
            ve2_of[j] = S.val("ve2")
            if j + 1 < NSLOT:
                emit_load(2 * (j + 1))
                emit_load(2 * (j + 1) + 1)
        emit_B(NSLOT - 1, [])

        sc_all = ("tsc", S.val("tsc"))
        av_all = ("tav", S.val("tav"))
        S.add("tensor", [], lambda e, s: e.nop(), inc="tdn")
        b_t_done = ("tdn", S.val("tdn"))
        b_s_done = ("soe", S.val("soe"))
        b_v_done = ("vav", S.val("vav"))

        for t in taps:
            src = {"xnfm": xn_fm, "xntm": xn_tm, "tall": t_all, "m": m_sb, "wvp": wv_p}[t]
            S.add("gpsimd", [b_t_done, b_s_done],
                  lambda e, s, d=tap_d[t], i=src: e.dma_start(d[:, :], i[:, :]),
                  inc="dw", incby=16)

        tap_dw = ("dw", S.val("dw"))
        S.add("gpsimd", [sc_all, tap_dw], lambda e, s: e.dma_start(mur[:, :], mur_d[:, :]),
              inc="dmu", incby=16)
        S.add("gpsimd", [sc_all, tap_dw], lambda e, s: e.dma_start(muk[:, :], muk_d[:, :]),
              inc="dmu", incby=16)
        cmu = ("dmu", 32)
        cwk = []
        for hh in range(2):
            S.add("gpsimd", [sc_all, tap_dw, ("dtx", 48)],
                  lambda e, s, h=hh:
                  e.dma_start(wark[:, :].rearrange("p (c k) -> p c k", k=D)[:, :, h * 512:(h + 1) * 512],
                              wks_d[:, :].rearrange("(c p) k -> p c k", p=128)[:, :, h * 512:(h + 1) * 512]),
                  inc="dk%d" % hh, incby=16)
            cwk.append(("dk%d" % hh, 16))
        for (dst, dr, sm) in ((warr, wrs_d, "dwr2"), (warv, wvs_d, "dvv")):
            S.add("gpsimd", [sc_all, tap_dw, ("dtx", 48)],
                  lambda e, s, d=dst, r=dr:
                  e.dma_start(d[:, :].rearrange("p (c k) -> p c k", k=D),
                              r[:, :].rearrange("(c p) k -> p c k", p=128)),
                  inc=sm, incby=16)
        cwr = ("dwr2", 16)
        cwv = ("dvv", 16)
        S.add("sync", [sc_all, tap_dw],
              lambda e, s: e.dma_start(xprev[:, :], xp_d[:, :]), inc="dx", incby=16)
        cw_done = ("dw", S.val("dw"))

        xpg = ("dx", S.val("dx"))
        ln_group([xprev[:, :]], [xprevt[:, :]], xprev[:, :].rearrange("p (t k) -> p t k", k=D), colc, 16, xpg, "slc", "vlc", pd=8)
        prev_ap = xprevf[:, :].rearrange("p (c q) -> p c q", q=8)

        def prevtb(e, s):
            inst = None
            for cc in range(CH):
                inst = e.matmul(pSCb[0][:, cc * 8:(cc + 1) * 8],
                                xprevt[:, cc * 128:(cc + 1) * 128], ident[0:8, 0:8],
                                is_transpose=True, start=(cc == 0), stop=(cc == CH - 1))
            return inst
        S.add("tensor", [("slc", S.val("slc"))] + gwait("sc0"), prevtb, inc="ttc")
        S.add("vector", [("ttc", S.val("ttc"))],
              lambda e, s: e.tensor_copy(xprevf[:, :], pSCb[0][:, 0:CH * 8]), inc="vec")
        prevf_done = ("vec", S.val("vec"))
        gate["sc0"] = ("vec", S.val("vec"))

        vfn_after = {}
        tf_after = {}
        vka_of = {}
        xpart_of = {}
        cst = {}
        mix_done_of = {}
        ld_of = {}
        vec_after = {-1: ("vec", S.val("vec"))}
        ttc_after = {-1: ("ttc", S.val("ttc"))}

        def c_load(j):
            buf = j % 3
            if j < 3:
                wl = [av_all, tap_dw]
            else:
                wl = [("vfn", xpart_of[j - 3])]
            S.add("sync", wl,
                  lambda e, s, jj=j, b=buf:
                  e.dma_start(xoc[b][:, :].rearrange("p (t k) -> p t k", k=D),
                              xo_d[256 * jj:256 * (jj + 1), :].rearrange("(t p) k -> p t k", p=128)),
                  inc="dci%d" % (j % 3), incby=16)
            ld_of[j] = ("dci%d" % (j % 3), S.val("dci%d" % (j % 3)))
            cst[j] = {"xg": ld_of[j]}

        def c_feed(j):
            st = cst[j]
            buf = j % 2
            if j < 2:
                fw = [av_all, tap_dw]
            else:
                fw = [mix_done_of[j - 2]]
            S.add("sync", fw,
                  lambda e, s, jj=j, b=buf:
                  e.dma_start(
                      xnopf_b[b][:, :].rearrange("p (c q) -> p c q", q=257)[:, :, 1:257],
                      xof_d[:, 2048 * jj:2048 * (jj + 1)].rearrange("p (c q) -> p c q", q=256)),
                  inc="dnf%d" % buf, incby=16)
            st["feed"] = ("dnf%d" % buf, S.val("dnf%d" % buf))
            S.add("vector", [prevf_done],
                  lambda e, s, jj=j, b=buf:
                  e.tensor_copy(xnopf_b[b][:, :].rearrange("p (c q) -> p c q", q=257)[:, :, 0:1],
                                prev_ap[:, :, jj:jj + 1]), inc="vec")
            st["pcol"] = ("vec", S.val("vec"))

        def c_mixes(j):
            st = cst[j]
            buf = j % 2
            xnopf = xnopf_b[buf]
            mix_w = [cmu, st["feed"], st["pcol"]]
            if j >= 1 and mix_done_of.get(j - 1):
                mix_w.append(mix_done_of[j - 1])
            if j >= 2 and (j - 2) in tf_after:
                mix_w.append(tf_after[j - 2])
            cur = xnopf[:, :].rearrange("p (c q) -> p c q", q=257)[:, :, 1:257]
            prv = xnopf[:, :].rearrange("p (c q) -> p c q", q=257)[:, :, 0:256]
            d3 = diffb[:, :].rearrange("p (c q) -> p c q", q=256)
            xk3 = xk_sb[buf][:, :].rearrange("p (c q) -> p c q", q=256)
            xr3 = xr_sb[buf][:, :].rearrange("p (c q) -> p c q", q=256)
            mukb = muk[:, :].rearrange("p (c one) -> p c one", one=1).broadcast_to([128, CH, 256])
            murb = mur[:, :].rearrange("p (c one) -> p c one", one=1).broadcast_to([128, CH, 256])
            S.add("vector", mix_w,
                  lambda e, s: e.tensor_tensor(d3, cur, prv, ALU.subtract), inc="vmx")
            S.add("vector", [("vmx", S.val("vmx"))],
                  lambda e, s: e.tensor_tensor(xk3, d3, mukb, ALU.mult), inc="vmx")
            S.add("vector", [("vmx", S.val("vmx"))],
                  lambda e, s: e.tensor_tensor(xk3, xk3, prv, ALU.add), inc="vmx")
            xk_done_of = ("vmx", S.val("vmx"))
            S.add("vector", [("vmx", S.val("vmx") - 1)],
                  lambda e, s: e.tensor_tensor(xr3, d3, murb, ALU.mult), inc="vmx")
            S.add("vector", [("vmx", S.val("vmx"))],
                  lambda e, s: e.tensor_tensor(xr3, xr3, prv, ALU.add), inc="vmx")
            mix_done_of[j] = ("vmx", S.val("vmx"))
            cst[j]["xk_done"] = xk_done_of

        def c_stage_b(j, inject, midfns=()):
            buf = j % 2
            mix_done = mix_done_of[j]
            ld = ld_of[j]
            pw = ([("vfn", xpart_of[j - 1])] if j >= 1 else [av_all, tap_dw]) + [("dob", 16 * (j + 1))]
            S.add("sync", pw,
                  lambda e, s, jj=j:
                  e.dma_start(parts[:, :].rearrange("p (t k) -> p t k", k=D),
                              out_d[256 * jj:256 * (jj + 1), :].rearrange("(t p) k -> p t k", p=128)),
                  inc="dcp", incby=16)
            pld = ("dcp", S.val("dcp"))
            ip = 0

            def inj():
                nonlocal ip
                if ip < len(inject):
                    inject[ip]()
                    ip += 1

            if j == 0:
                gate["sc2"] = b_s_done
                gate["sc3"] = b_s_done
            for dc in range(CH):
                bank = dc % 4
                pb = pool[:, 512 * bank:512 * bank + 512]

                def kb(e, s, b=pb, d=dc, bu=buf):
                    inst = None
                    for cr in range(CH):
                        inst = e.matmul(b[:, 0:256],
                                        wark[:, cr * D + d * 128: cr * D + (d + 1) * 128],
                                        xk_sb[bu][:, cr * 256:(cr + 1) * 256],
                                        start=(cr == 0), stop=(cr == CH - 1))
                    return inst
                S.add("tensor", [cst[j]["xk_done"], cwk[0] if dc < 4 else cwk[1]] + gwait("sc%d" % bank), kb, inc="tk")
                S.add("scalar", [("tk", S.val("tk"))],
                      lambda e, s, b=pb, d=dc:
                      e.activation(rk_sb[:, d * 256:(d + 1) * 256], b[:, 0:256], AF.Relu),
                      inc="srk")
                gate["sc%d" % bank] = ("srk", S.val("srk"))
                inj()
            sq_w = [("srk", S.val("srk"))]
            if j >= 2:
                sq_w.append(tf_after[j - 2])
            S.add("scalar", sq_w,
                  lambda e, s, bu=buf: e.activation(kact[bu][:, :], rk_sb[:, :], AF.Square),
                  inc="srk")
            kact_done = ("srk", S.val("srk"))
            for qt in range(2):
                tgt = pOS if qt == 0 else pAVlo
                w = [mix_done, cwr]
                if j > 0:
                    w.append(("vfn", vfn_after[j - 1] - 1) if qt == 0 else ("ssg", 2 * j))
                elif qt == 0:
                    w.append(b_s_done)
                if qt == 1:
                    w += gwait("sc2") + gwait("sc3")

                def rb(e, s, t=tgt, q=qt, bu=buf):
                    inst = None
                    for oh in range(2):
                        for cr in range(CH):
                            inst = e.matmul(t[:, oh * 512:(oh + 1) * 512],
                                            xr_sb[bu][:, cr * 256 + q * 128: cr * 256 + (q + 1) * 128],
                                            warr[:, cr * D + oh * 512: cr * D + oh * 512 + 512],
                                            start=(cr == 0), stop=(cr == CH - 1))
                    return inst
                S.add("tensor", w, rb, inc="tr")
                if qt == 0:
                    while ip < len(inject):
                        inj()
                    for fn in midfns:
                        fn()
                sg_w = [("tr", S.val("tr"))]
                if j >= 1:
                    sg_w.append(("vfn", vfn_after[j - 1] - 1))
                S.add("scalar", sg_w,
                      lambda e, s, t=tgt, q=qt:
                      e.activation(r_sb[:, q * D:(q + 1) * D], t[:, 0:D], AF.Sigmoid), inc="ssg")
                if qt == 1:
                    gate["sc2"] = ("ssg", S.val("ssg"))
                    gate["sc3"] = ("ssg", S.val("ssg"))
            for qt in range(2):
                tgt = pAVhi if qt == 0 else pOS
                w = [kact_done, cwv]
                if qt == 0 and j > 0:
                    w.append(("vfn", vfn_after[j - 1] - 3))
                if qt == 0 and j == 0:
                    w.append(b_v_done)
                if qt == 1:
                    w.append(("ssg", S.val("ssg") - 1))

                def fb3(e, s, t=tgt, q=qt, bu=buf):
                    inst = None
                    for oh in range(2):
                        for dc in range(CH):
                            inst = e.matmul(t[:, oh * 512:(oh + 1) * 512],
                                            kact[bu][:, dc * 256 + q * 128: dc * 256 + (q + 1) * 128],
                                            warv[:, dc * D + oh * 512: dc * D + oh * 512 + 512],
                                            start=(dc == 0), stop=(dc == CH - 1))
                    return inst
                S.add("tensor", w, fb3, inc="tf")
            tf_q0 = S.val("tf") - 1
            tf_q1 = S.val("tf")
            tf_after[j] = ("tf", tf_q1)
            S.add("vector", [ld, pld, ("vfn", S.val("vfn"))],
                  lambda e, s, b=j % 3:
                  e.tensor_tensor(xpart[:, :], parts[:, :], xoc[b][:, :], ALU.add),
                  inc="vfn")
            xpart_of[j] = S.val("vfn")
            for qt in range(2):
                src = pAVhi if qt == 0 else pOS
                w = [("tf", tf_q0 if qt == 0 else tf_q1), ("ssg", S.val("ssg") - (1 - qt)),
                     ("vfn", S.val("vfn"))]
                if j >= 2:
                    w.append(("doc%d" % (j % 2), 16 * (j // 2)))
                S.add("vector", w,
                      lambda e, s, t=src, q=qt:
                      e.tensor_tensor(fbuf[:, :], t[:, 0:D], r_sb[:, q * D:(q + 1) * D],
                                      ALU.mult), inc="vfn")
                S.add("vector", [("vfn", S.val("vfn"))],
                      lambda e, s, q=qt, jj=j:
                      e.tensor_tensor(outc[jj % 2][:, q * D:(q + 1) * D], fbuf[:, :],
                                      xpart[:, q * D:(q + 1) * D], ALU.add), inc="vfn")
            vfn_after[j] = S.val("vfn")
            S.add("gpsimd", [("vfn", S.val("vfn"))],
                  lambda e, s, jj=j:
                  e.dma_start(out_d[256 * jj:256 * (jj + 1), :].rearrange("(t p) k -> p t k", p=128),
                              outc[jj % 2][:, :].rearrange("p (t k) -> p t k", k=D)),
                  inc="doc%d" % (j % 2), incby=16)

        c_load(0)
        c_load(1)
        c_feed(0)
        c_feed(1)
        c_mixes(0)
        c_mixes(1)
        for j in range(NSLOT):
            nx = j + 2
            if nx < NSLOT:
                c_load(nx)
                c_feed(nx)
            if j + 1 < NSLOT and j + 1 >= 2:
                midfns = [lambda a=j + 1: c_mixes(a)]
            else:
                midfns = []
            c_stage_b(j, [], midfns)

        S.add("sync", [("doc0", 64), ("doc1", 64), ("dob", 16 * NSLOT)],
              lambda e, s: e.nop(), inc=None)

        with nc.Block() as block:
            def runner(oplist):
                def f(eng):
                    for op in oplist:
                        op(eng, sems)
                return f
            block.sync(runner(S.ops["sync"]))
            block.gpsimd(runner(S.ops["gpsimd"]))
            block.scalar(runner(S.ops["scalar"]))
            block.vector(runner(S.ops["vector"]))
            block.tensor(runner(S.ops["tensor"]))

    return nc


def _host_inputs(x, Wq, Wk, Wv, Wo, W_r, W_k, W_v, mu_r, mu_k):
    x = np.asarray(x, dtype=np.float32)
    c = np.ascontiguousarray
    shared = {
        "wq": c(np.asarray(Wq, np.float32).T),
        "wk": c(np.asarray(Wk, np.float32).T),
        "wvT": c(np.asarray(Wv, np.float32).T),
        "wo": c(np.asarray(Wo, np.float32)),
        "wrT": c(np.asarray(W_r, np.float32).T),
        "wkTf": c(np.asarray(W_k, np.float32).T),
        "wvTf": c(np.asarray(W_v, np.float32).T),
        "mur": c(np.asarray(mu_r, np.float32).reshape(8, 128).T),
        "muk": c(np.asarray(mu_k, np.float32).reshape(8, 128).T),
    }
    in_maps = []
    own_rows_all = []
    for core in range(8):
        b, par = core // 2, core % 2
        xb = c(x[b])
        rows = np.concatenate([np.arange(512 * j + 256 * par, 512 * j + 256 * par + 256)
                               for j in range(8)])
        own_rows_all.append(rows)
        x_own = c(xb[rows])
        xprev = np.zeros((8, 1024), np.float32)
        for j in range(8):
            t = 512 * j + 256 * par - 1
            if t >= 0:
                xprev[j] = xb[t]
        pp = np.arange(128)[:, None]
        cc_ = np.arange(640)[None, :]
        mask = np.where(cc_ < pp + 384 - 256 * par, -1e9, 0.0).astype(np.float32)
        im = dict(shared)
        im["x"] = xb
        im["x_own"] = x_own
        im["x_prev"] = c(xprev)
        im["maskt"] = c(mask)
        in_maps.append(im)
    return in_maps, own_rows_all


def kernel(L0_attn_x, L0_attn_Wq, L0_attn_Wk, L0_attn_Wv, L0_attn_Wo,
           L0_ffn_W_r, L0_ffn_W_k, L0_ffn_W_v, L0_ffn_mu_r, L0_ffn_mu_k):
    in_maps, own_rows = _host_inputs(L0_attn_x, L0_attn_Wq, L0_attn_Wk, L0_attn_Wv,
                                     L0_attn_Wo, L0_ffn_W_r, L0_ffn_W_k, L0_ffn_W_v,
                                     L0_ffn_mu_r, L0_ffn_mu_k)
    nc = build()
    res = run_bass_kernel_spmd(nc, in_maps, core_ids=list(range(8)))
    out = np.empty((4, 4096, 1024), np.float32)
    for core in range(8):
        b = core // 2
        out[b][own_rows[core]] = res.results[core]["out"]
    return out


# version 33
# speedup vs baseline: 1.1944x; 1.0181x over previous
import numpy as np
import ml_dtypes
import concourse.bass as bass
import concourse.mybir as mybir
from concourse.bass_utils import run_bass_kernel_spmd
from contextlib import ExitStack

F32 = mybir.dt.float32
BF16 = mybir.dt.bfloat16
AF = mybir.ActivationFunctionType
ALU = mybir.AluOpType
AX = mybir.AxisListType

D = 1024
CH = 8
NTILE = 32
NSLOT = 8
SCALE = np.float32(1.0 / 32.0)
EBIAS = np.float32(-44.0)
f32 = np.float32


class Sched:

    def __init__(self):
        self.ops = {"sync": [], "gpsimd": [], "scalar": [], "vector": [], "tensor": []}
        self.cnt = {}

    def inc(self, name, by=1):
        self.cnt[name] = self.cnt.get(name, 0) + by
        return self.cnt[name]

    def val(self, name):
        return self.cnt.get(name, 0)

    def add(self, eng, waits, body, inc=None, incby=1):
        waits = [w for w in waits if w is not None]
        post = None
        if inc is not None:
            post = (inc, self.inc(inc, incby))

        def fn(e, sems, _w=waits, _b=body, _p=post, _by=incby):
            for s, v in _w:
                if v > 0:
                    e.wait_ge(sems[s], v)
            inst = _b(e, sems)
            if _p is not None:
                assert inst is not None
                inst.then_inc(sems[_p[0]], _by)

        self.ops[eng].append(fn)


def build(taps=()):
    nc = bass.Bass("TRN2", target_bir_lowering=False)

    x_d = nc.dram_tensor("x", [4096, D], F32, kind="ExternalInput")
    xo_d = nc.dram_tensor("x_own", [2048, D], F32, kind="ExternalInput")
    xp_d = nc.dram_tensor("x_prev", [8, D], F32, kind="ExternalInput")
    wq_d = nc.dram_tensor("wq", [D, D], F32, kind="ExternalInput")
    wk_d = nc.dram_tensor("wk", [D, D], F32, kind="ExternalInput")
    wvT_d = nc.dram_tensor("wvT", [D, D], F32, kind="ExternalInput")
    wo_d = nc.dram_tensor("wo", [D, D], F32, kind="ExternalInput")
    wrT_d = nc.dram_tensor("wrT", [D, D], F32, kind="ExternalInput")
    wkTf_d = nc.dram_tensor("wkTf", [D, D], F32, kind="ExternalInput")
    wvTf_d = nc.dram_tensor("wvTf", [D, D], F32, kind="ExternalInput")
    mur_d = nc.dram_tensor("mur", [128, CH], F32, kind="ExternalInput")
    muk_d = nc.dram_tensor("muk", [128, CH], F32, kind="ExternalInput")
    maskt_d = nc.dram_tensor("maskt", [128, 640], F32, kind="ExternalInput")
    out_d = nc.dram_tensor("out", [2048, D], F32, kind="ExternalOutput")
    wvp_d = nc.dram_tensor("wvp_stage", [128, CH * D], BF16, kind="Internal")
    xof_d = nc.dram_tensor("xof_stage", [128, NSLOT * CH * 256], BF16, kind="Internal")
    wrs_d = nc.dram_tensor("wrs_stage", [D, D], BF16, kind="Internal")
    wks_d = nc.dram_tensor("wks_stage", [D, D], BF16, kind="Internal")
    wvs_d = nc.dram_tensor("wvs_stage", [D, D], BF16, kind="Internal")

    tap_d = {}
    for t in taps:
        shp = {"xnfm": [128, CH * 4096], "xntm": [128, NTILE * D],
               "tall": [128, NSLOT * CH * 256], "m": [128, CH * D],
               "wvp": [128, CH * D]}[t]
        tap_d[t] = nc.dram_tensor("tap_" + t, shp, F32, kind="ExternalOutput")

    S = Sched()
    sem_names = ["dx", "dw", "dvs", "dob", "dci0", "dci1", "doc0", "doc1", "dcp", "dxa", "dxb", "dwh0", "dmk", "dci2", "dtx", "vka", "gfn", "dxf", "dnf0", "dnf1", "dwv0", "dmu", "dk0", "dk1", "dwr2", "dvv", "dwv1",
                 "tm0", "tt1", "tq1", "tt2", "tsc", "tav", "tsu", "to", "tdn",
                 "ttc", "tk", "tr", "tf",
                 "se0", "st1", "sa1", "sa2", "sex", "soe", "slc", "srk", "ssg",
                 "ve1", "vl1", "ve2", "vl2", "vmk", "vav", "vrc", "vcst",
                 "vec", "vlc", "vmx", "vfn", "gio"]

    with ExitStack() as ctx:
        sb = lambda name, shape, dt: ctx.enter_context(nc.sbuf_tensor(name, shape, dt))
        ident = sb("ident", [128, 128], BF16)
        pidx = sb("pidx", [128, 1], F32)
        ones_bf = sb("onesbf", [128, 1], BF16)
        maskt = sb("maskt_s", [128, 640], BF16)
        t_all = sb("t_all", [128, NSLOT * CH * 256], BF16)

        e_m = ExitStack()
        m_sb = e_m.enter_context(nc.sbuf_tensor("m_sb", [128, CH * D], BF16))
        xo1 = [e_m.enter_context(nc.sbuf_tensor(f"xo1_{i}", [128, 2 * D], F32)) for i in range(2)]
        col1 = {n: e_m.enter_context(nc.sbuf_tensor("c1" + n, [128, 16], F32))
                for n in ("s1", "s2", "c6", "cv", "u", "rs", "nm")}
        xot = e_m.enter_context(nc.sbuf_tensor("xot", [128, 16 * D], BF16))
        xof = e_m.enter_context(nc.sbuf_tensor("xof", [128, NSLOT * CH * 256], BF16))
        e_a0 = ExitStack()
        wa_sb = e_a0.enter_context(nc.sbuf_tensor("wa", [128, CH * D], BF16))
        wb_sb = e_a0.enter_context(nc.sbuf_tensor("wb", [128, CH * D], BF16))
        wvpe = e_a0.enter_context(nc.sbuf_tensor("wvpe", [128, 2 * D], BF16))
        wc_sb = e_a0.enter_context(nc.sbuf_tensor("wc", [128, CH * D], BF16))
        wd_sb = e_a0.enter_context(nc.sbuf_tensor("wd", [128, CH * D], BF16))
        iota_f = e_a0.enter_context(nc.sbuf_tensor("iotaf", [128, 128], F32))
        e_a0.close()
        e_m.close()

        e_b = ExitStack()
        xn_fm = e_b.enter_context(nc.sbuf_tensor("xn_fm", [128, CH * 4096], BF16))
        xn_tm = e_b.enter_context(nc.sbuf_tensor("xn_tm", [128, NTILE * D], BF16))
        wv_p = e_b.enter_context(nc.sbuf_tensor("wv_p", [128, CH * D], BF16))
        x_grp = [e_b.enter_context(nc.sbuf_tensor(f"x_grp{i}", [128, 2 * D], F32)) for i in range(2)]
        col2 = {n: e_b.enter_context(nc.sbuf_tensor("c2" + n, [128, NTILE], F32))
                for n in ("s1", "s2", "cv", "u", "rs")}
        col2["c6"] = col2["u"]
        col2["nm"] = col2["cv"]
        ptbuf = e_b.enter_context(nc.sbuf_tensor("ptbuf", [128, 3 * 256], BF16))
        avxt = e_b.enter_context(nc.sbuf_tensor("avxt", [128, CH * 256], BF16))
        outb = e_b.enter_context(nc.sbuf_tensor("outb", [128, 2 * D], F32))
        recip = e_b.enter_context(nc.sbuf_tensor("recip", [128, 16], F32))
        e_b.close()

        e_c = ExitStack()
        warr = e_c.enter_context(nc.sbuf_tensor("warr", [128, CH * D], BF16))
        wark = e_c.enter_context(nc.sbuf_tensor("wark", [128, CH * D], BF16))
        warv = e_c.enter_context(nc.sbuf_tensor("warv", [128, CH * D], BF16))
        mur = e_c.enter_context(nc.sbuf_tensor("mur_s", [128, CH], F32))
        muk = e_c.enter_context(nc.sbuf_tensor("muk_s", [128, CH], F32))
        xprev = e_c.enter_context(nc.sbuf_tensor("xprev", [8, D], F32))
        xprevt = e_c.enter_context(nc.sbuf_tensor("xprevt", [8, D], BF16))
        xprevf = e_c.enter_context(nc.sbuf_tensor("xprevf", [128, CH * 8], BF16))
        xoc = [e_c.enter_context(nc.sbuf_tensor(f"xoc{i}", [128, 2 * D], F32)) for i in range(3)]
        parts = e_c.enter_context(nc.sbuf_tensor("parts", [128, 2 * D], F32))
        colc = {n: e_c.enter_context(nc.sbuf_tensor("cc" + n, [128, 20], F32))
                for n in ("s1", "s2", "c6", "cv", "u", "rs", "nm")}
        xnopf_b = [e_c.enter_context(nc.sbuf_tensor(f"xnopf{i}", [128, CH * 257], BF16))
                   for i in range(2)]
        diffb = e_c.enter_context(nc.sbuf_tensor("diffb", [128, CH * 256], BF16))
        xr_sb = [e_c.enter_context(nc.sbuf_tensor(f"xr_sb{i}", [128, CH * 256], BF16)) for i in range(2)]
        xk_sb = [e_c.enter_context(nc.sbuf_tensor(f"xk_sb{i}", [128, CH * 256], BF16)) for i in range(2)]
        rk_sb = e_c.enter_context(nc.sbuf_tensor("rk_sb", [128, CH * 256], BF16))
        kact = [e_c.enter_context(nc.sbuf_tensor(f"kact{i}", [128, CH * 256], BF16)) for i in range(2)]
        r_sb = e_c.enter_context(nc.sbuf_tensor("r_sb", [128, 2 * D], F32))
        xpart = e_c.enter_context(nc.sbuf_tensor("xpart", [128, 2 * D], F32))
        fbuf = e_c.enter_context(nc.sbuf_tensor("fbuf", [128, D], F32))
        outc = [e_c.enter_context(nc.sbuf_tensor(f"outc{i}", [128, 2 * D], F32)) for i in range(2)]
        e_c.close()

        pool = ctx.enter_context(nc.psum_tensor("pool", [128, 4096], F32))
        pSC = [pool[:, 0:512], pool[:, 512:1024]]
        pSCb = [pool[:, 512 * i:512 * (i + 1)].bitcast(BF16) for i in range(6)]
        pAV = pool[:, 1024:3072]
        pAVlo = pool[:, 1024:2048]
        pAVhi = pool[:, 2048:3072]
        pOS = pool[:, 3072:4096]

        sems = {}
        for n in sem_names:
            sems[n] = ctx.enter_context(nc.semaphore("sem_" + n))

        gate = {"sc0": None, "sc1": None, "sc2": None, "sc3": None, "sc4": None, "sc5": None}

        def gwait(key):
            g = gate[key]
            return [g] if g else []

        def ln_group(xslices, xnslices, x3d, cols, k0, xgate, sA, vL, pd=128,
                     apply_extra=None):
            G = len(xslices)
            sA0 = S.val(sA)
            vL0 = S.val(vL)
            cs = lambda n: cols[n][0:pd, k0:k0 + G]
            n = 1024
            for g in range(G):
                S.add("scalar",
                      ([xgate] + ([apply_extra] if apply_extra else [])) if g == 0 else [],
                      lambda e, s, o=xnslices[g], i=xslices[g],
                      a=cols["s2"][0:pd, k0 + g:k0 + g + 1]:
                      e.activation(o, i, AF.Square, accum_out=a), inc=sA)
            S.add("vector", [xgate],
                  lambda e, s, o=cs("s1"), i=x3d: e.reduce_sum(o, i, axis=AX.X), inc=vL)
            S.add("vector", [(vL, vL0 + 1)],
                  lambda e, s, o=cs("c6"), i=cs("s1"): e.tensor_tensor(o, i, i, ALU.mult),
                  inc=vL)
            S.add("vector", [(sA, sA0 + G), (vL, vL0 + 2)],
                  lambda e, s, o=cs("cv"), i=cs("c6"), j2=cs("s2"), sc=-1.0 / n:
                  e.scalar_tensor_tensor(o, i, sc, j2, ALU.mult, ALU.add), inc=vL)
            S.add("scalar", [(vL, vL0 + 3)],
                  lambda e, s, o=cs("u"), i=cs("cv"), sc=f32(1.0 / n):
                  e.activation(o, i, AF.Ln, bias=f32(1e-5), scale=sc), inc=sA)
            S.add("scalar", [(sA, sA0 + G + 1)],
                  lambda e, s, o=cs("rs"), i=cs("u"): e.activation(o, i, AF.Exp, scale=f32(-0.5)),
                  inc=sA)
            S.add("vector", [(sA, sA0 + G + 2)],
                  lambda e, s, o=cs("nm"), i=cs("s1"), r=cs("rs"), sc=-1.0 / n:
                  e.scalar_tensor_tensor(o, i, sc, r, ALU.mult, ALU.mult), inc=vL)
            for g in range(G):
                S.add("scalar", [(vL, vL0 + 4)] if g == 0 else [],
                      lambda e, s, o=xnslices[g], i=xslices[g],
                      b=cols["nm"][0:pd, k0 + g:k0 + g + 1],
                      r=cols["rs"][0:pd, k0 + g:k0 + g + 1]:
                      e.activation(o, i, AF.Identity, bias=b, scale=r), inc=sA)

        def transpose_tile(src, dst_ap, bank, readygate, tsem, vsem, evac_extra=None):
            def tbody(e, s, b=bank, src_=src):
                inst = None
                for cc in range(CH):
                    inst = e.matmul(pSCb[b][:, cc * 128:(cc + 1) * 128],
                                    src_[:, cc * 128:(cc + 1) * 128], ident[:, :],
                                    is_transpose=True, start=(cc == 0), stop=(cc == CH - 1))
                return inst
            S.add("tensor", [readygate, ("vcst", 1)] + gwait("sc%d" % bank), tbody, inc=tsem)
            tv = S.val(tsem)
            S.add("vector", [(tsem, tv)] + ([evac_extra] if evac_extra else []),
                  lambda e, s, o=dst_ap, b=bank:
                  e.tensor_copy(o, pSCb[b][:, 0:CH * 128]), inc=vsem)
            gate["sc%d" % bank] = (vsem, S.val(vsem))

        for hh in range(2):
            for (dst, dr) in ((wa_sb, wq_d), (wb_sb, wk_d)):
                S.add("gpsimd", [],
                      lambda e, s, d=dst, r=dr, h=hh:
                      e.dma_start(d[:, :].rearrange("p (c k) -> p c k", k=D)[:, :, h * 512:(h + 1) * 512],
                                  r[:, :].rearrange("(c p) k -> p c k", p=128)[:, :, h * 512:(h + 1) * 512]),
                      inc="dwh0" if hh == 0 else "dw", incby=16)
        for hh in range(2):
            for (dst, dr) in ((wc_sb, wvT_d), (wd_sb, wo_d)):
                S.add("gpsimd", [],
                      lambda e, s, d=dst, r=dr, h=hh:
                      e.dma_start(d[:, :].rearrange("p (c k) -> p c k", k=D)[:, :, h * 512:(h + 1) * 512],
                                  r[:, :].rearrange("(c p) k -> p c k", p=128)[:, :, h * 512:(h + 1) * 512]),
                      inc="dwv0" if hh == 0 else "dwv1", incby=16)
        S.add("gpsimd", [], lambda e, s: e.dma_start(maskt[:, :], maskt_d[:, :]),
              inc="dmk", incby=16)
        S.add("gpsimd", [], lambda e, s: e.iota(
            iota_f[:, :], [[1, 128]], channel_multiplier=0,
            allow_small_or_imprecise_dtypes=True), inc=None)
        S.add("gpsimd", [], lambda e, s: e.iota(
            pidx[:, :], [[0, 1]], channel_multiplier=1,
            allow_small_or_imprecise_dtypes=True), inc="gio")
        S.add("vector", [("gio", 1)],
              lambda e, s: e.tensor_scalar(ident[:, :], iota_f[:, :], pidx[:, :],
                                           None, ALU.is_equal), inc="vcst")
        S.add("vector", [], lambda e, s: e.memset(ones_bf[:, :], 1.0), inc="vcst")

        ast = {}
        tt1_after = {}

        def a1_load(j):
            sem = "dxa" if j % 2 == 0 else "dxb"
            wt = []
            if j >= 2:
                wt = [("sa1", ast[j - 2]["apply"]), ("vl1", ast[j - 2]["reduce"])]
            S.add("sync", wt,
                  lambda e, s, jj=j:
                  e.dma_start(xo1[jj % 2][:, :].rearrange("p (t k) -> p t k", k=D),
                              xo_d[256 * jj:256 * (jj + 1), :].rearrange("(t p) k -> p t k", p=128)),
                  inc=sem, incby=16)
            ast[j] = {"xg": (sem, S.val(sem))}

        def a1_pieces(j):
            st = ast[j]
            k0 = 2 * j
            cs = lambda n: col1[n][:, k0:k0 + 2]
            bufx = xo1[j % 2]
            for g in range(2):
                S.add("scalar",
                      [st["xg"]] if g == 0 else [],
                      lambda e, s, o=xot[:, (k0 + g) * D:(k0 + g + 1) * D],
                      ii=bufx[:, g * D:(g + 1) * D],
                      a=col1["s2"][:, k0 + g:k0 + g + 1]:
                      e.activation(o, ii, AF.Square, accum_out=a), inc="sa1")
            st["sq"] = S.val("sa1")
            vL0 = S.val("vl1")
            S.add("vector", [st["xg"]],
                  lambda e, s, o=cs("s1"), i=bufx[:, :].rearrange("p (t k) -> p t k", k=D):
                  e.reduce_sum(o, i, axis=AX.X), inc="vl1")
            st["reduce"] = S.val("vl1")
            S.add("vector", [("vl1", vL0 + 1)],
                  lambda e, s, o=cs("c6"), i=cs("s1"): e.tensor_tensor(o, i, i, ALU.mult),
                  inc="vl1")
            S.add("vector", [("sa1", st["sq"]), ("vl1", vL0 + 2)],
                  lambda e, s, o=cs("cv"), i=cs("c6"), j2=cs("s2"):
                  e.scalar_tensor_tensor(o, i, -1.0 / 1024, j2, ALU.mult, ALU.add), inc="vl1")
            st["cvar"] = S.val("vl1")
            S.add("scalar", [("vl1", st["cvar"])],
                  lambda e, s, o=cs("u"), i=cs("cv"):
                  e.activation(o, i, AF.Ln, bias=f32(1e-5), scale=f32(1.0 / 1024)), inc="sa1")
            st["ln"] = S.val("sa1")
            S.add("scalar", [("sa1", st["ln"])],
                  lambda e, s, o=cs("rs"), i=cs("u"): e.activation(o, i, AF.Exp, scale=f32(-0.5)),
                  inc="sa1")
            st["exp"] = S.val("sa1")
            S.add("vector", [("sa1", st["exp"])],
                  lambda e, s, o=cs("nm"), i=cs("s1"), r=cs("rs"):
                  e.scalar_tensor_tensor(o, i, -1.0 / 1024, r, ALU.mult, ALU.mult), inc="vl1")
            st["negm"] = S.val("vl1")
            for g in range(2):
                S.add("scalar", [("vl1", st["negm"])] if g == 0 else [],
                      lambda e, s, o=xot[:, (k0 + g) * D:(k0 + g + 1) * D],
                      ii=bufx[:, g * D:(g + 1) * D],
                      b=col1["nm"][:, k0 + g:k0 + g + 1], r=col1["rs"][:, k0 + g:k0 + g + 1]:
                      e.activation(o, ii, AF.Identity, bias=b, scale=r), inc="sa1")
            st["apply"] = S.val("sa1")
            if j + 2 < NSLOT:
                a1_load(j + 2)

        a1_load(0)
        a1_load(1)

        g = 0
        group_emitters = []
        for ccc in range(CH):
            for hc in range(2):
                def mk_m(c=ccc, h=hc):
                    def em():
                        global_bank = emctx["g"] % 2

                        def mbody(e, s, b=global_bank, c2=c, h2=h):
                            inst = None
                            for er in range(CH):
                                inst = e.matmul(pSC[b][:, 0:512],
                                                wa_sb[:, er * D + c2 * 128: er * D + (c2 + 1) * 128],
                                                wb_sb[:, er * D + h2 * 512: er * D + h2 * 512 + 512],
                                                start=(er == 0), stop=(er == CH - 1))
                            return inst
                        need = [("dwh0", 32)] if (c < 4 and h == 0) else [("dwh0", 32), ("dw", 32)]
                        S.add("tensor", need + gwait("sc%d" % global_bank), mbody, inc="tm0")
                        tv = S.val("tm0")
                        S.add("scalar", [("tm0", tv)],
                              lambda e, s, b=global_bank, c2=c, h2=h:
                              e.copy(m_sb[:, c2 * D + h2 * 512: c2 * D + h2 * 512 + 512],
                                     pSC[b][:, 0:512]), inc="se0")
                        gate["sc%d" % global_bank] = ("se0", S.val("se0"))
                        emctx["g"] += 1
                    return em
                group_emitters.append(mk_m())
        wvT_marker = len(group_emitters)
        for ec in range(CH):
            for oh in range(2):
                def mk_v(c=ec, h=oh):
                    def em():
                        global_bank = emctx["g"] % 2

                        def vbody(e, s, b=global_bank, c2=c, h2=h):
                            inst = None
                            for vr in range(CH):
                                inst = e.matmul(pSC[b][:, 0:512],
                                                wc_sb[:, vr * D + c2 * 128: vr * D + (c2 + 1) * 128],
                                                wd_sb[:, vr * D + h2 * 512: vr * D + h2 * 512 + 512],
                                                start=(vr == 0), stop=(vr == CH - 1))
                            return inst
                        vneed = [("dwv0", 32)] if (c < 4 and h == 0) else \
                            [("dwv0", 32), ("dwv1", 32)]
                        S.add("tensor", vneed + gwait("sc%d" % global_bank),
                              vbody, inc="tm0")
                        tv = S.val("tm0")
                        wt = [("tm0", tv)]
                        if c >= 2 and h == 0:
                            wt.append(("dvs", 16 * c))
                        S.add("scalar", wt,
                              lambda e, s, b=global_bank, sl=c % 2, h2=h:
                              e.copy(wvpe[:, sl * D + h2 * 512: sl * D + h2 * 512 + 512],
                                     pSC[b][:, 0:512]), inc="se0")
                        gate["sc%d" % global_bank] = ("se0", S.val("se0"))
                        emctx["g"] += 1
                        if h == 1:
                            S.add("gpsimd", [("se0", S.val("se0"))],
                                  lambda e, s, c2=c, sl=c % 2:
                                  e.dma_start(wvp_d[:, c2 * D:(c2 + 1) * D],
                                              wvpe[:, sl * D:(sl + 1) * D]),
                                  inc="dvs", incby=16)
                    return em
                group_emitters.append(mk_v())

        def emit_a1_transposes(j):
            st = ast[j]
            for tau in range(2):
                i = 2 * j + tau
                apply_done = ("sa1", st["apply"] - 1 + tau)
                dst_ap = xof[:, 2048 * j:2048 * (j + 1)].rearrange(
                    "p (c q) -> p c q", q=256)[:, :, tau * 128:tau * 128 + 128]
                transpose_tile(xot[:, i * D:(i + 1) * D], dst_ap, i % 2,
                               apply_done, "tt1", "ve1")
            st["evac"] = ("ve1", S.val("ve1"))
            S.add("sync", [st["evac"]],
                  lambda e, s, jj=j:
                  e.dma_start(xof_d[:, 2048 * jj:2048 * (jj + 1)],
                              xof[:, 2048 * jj:2048 * (jj + 1)]),
                  inc="dxf", incby=16)

        def emit_a1_tensor(j):
            st = ast[j]
            evac_done = st["evac"]
            for ccp in range(CH):
                bank = ccp % 2

                def tqb(e, s, b=bank, c=ccp, jj=j):
                    inst = None
                    for cr in range(CH):
                        inst = e.matmul(pSC[b][:, 0:256],
                                        m_sb[:, cr * D + c * 128: cr * D + (c + 1) * 128],
                                        xof[:, 2048 * jj + cr * 256: 2048 * jj + (cr + 1) * 256],
                                        start=(cr == 0), stop=(cr == CH - 1))
                    return inst
                S.add("tensor", [evac_done] + gwait("sc%d" % bank), tqb, inc="tq1")
                tv = S.val("tq1")
                S.add("scalar", [("tq1", tv)],
                      lambda e, s, b=bank, jj=j, c=ccp:
                      e.copy(t_all[:, jj * 2048 + c * 256: jj * 2048 + (c + 1) * 256],
                             pSC[b][:, 0:256]), inc="st1")
                gate["sc%d" % bank] = ("st1", S.val("st1"))

        a1_pieces(0)
        a1_pieces(1)
        emit_a1_transposes(0)
        emit_a1_transposes(1)
        emctx = {"g": 0, "dw_a0": 0}
        for idx, em in enumerate(group_emitters):
            em()
            if idx % 4 == 3 and idx // 4 + 2 < NSLOT:
                a1_pieces(idx // 4 + 2)

        for _j in range(2, NSLOT):
            emit_a1_transposes(_j)
            emit_a1_tensor(_j - 2)
        emit_a1_tensor(NSLOT - 2)
        emit_a1_tensor(NSLOT - 1)
        a1_t_done = ("tq1", S.val("tq1"))
        a1_s_done = [("sa1", S.val("sa1")), ("st1", S.val("st1"))]
        S.add("gpsimd", a1_s_done + [a1_t_done, ("dvs", S.val("dvs")), ("dxf", 16 * NSLOT)],
              lambda e, s: e.dma_start(wv_p[:, :], wvp_d[:, :]), inc="dvs", incby=16)
        wvp_ready = ("dvs", S.val("dvs"))

        exp_of_block = {}
        av_of_block = {}
        pt_gate = [None, None, None]

        ln_state = {}
        sa2_apply = {}

        def emit_load(gi):
            if gi == 0:
                wt = a1_s_done + [a1_t_done, ("vl1", 4 * NSLOT), ("dxf", 16 * NSLOT)]
            elif gi == 1:
                wt = [("dxf", 16 * NSLOT)]
            else:
                st = ln_state[gi - 2]
                wt = [("sa2", st["apply"]), ("vl2", st["reduce"])]
            S.add("sync", wt,
                  lambda e, s, g=gi:
                  e.dma_start(x_grp[g % 2][:, :].rearrange("p (t k) -> p t k", k=D),
                              x_d[256 * g:256 * (g + 1), :].rearrange("(t p) k -> p t k", p=128)),
                  inc="dxa" if gi % 2 == 0 else "dxb", incby=16)
            ln_state[gi] = {"xg": ("dxa" if gi % 2 == 0 else "dxb",
                                   S.val("dxa" if gi % 2 == 0 else "dxb"))}

        def emit_ln_vec(gi):
            st = ln_state[gi]
            buf = x_grp[gi % 2]
            k0 = 2 * gi
            cs = lambda n: col2[n][:, k0:k0 + 2]
            vL0 = S.val("vl2")
            S.add("vector", [st["xg"]],
                  lambda e, s, o=cs("s1"), i=buf[:, :].rearrange("p (t k) -> p t k", k=D):
                  e.reduce_sum(o, i, axis=AX.X), inc="vl2")
            st["reduce"] = S.val("vl2")
            S.add("vector", [("vl2", vL0 + 1)],
                  lambda e, s, o=cs("c6"), i=cs("s1"): e.tensor_tensor(o, i, i, ALU.mult),
                  inc="vl2")
            S.add("vector", [("sa2", st["sq"]), ("vl2", vL0 + 2)],
                  lambda e, s, o=cs("cv"), i=cs("c6"), j2=cs("s2"):
                  e.scalar_tensor_tensor(o, i, -1.0 / 1024, j2, ALU.mult, ALU.add), inc="vl2")
            st["cvar"] = S.val("vl2")

        def emit_ln_sq(gi):
            st = ln_state[gi]
            buf = x_grp[gi % 2]
            k0 = 2 * gi
            for g in range(2):
                i = k0 + g
                S.add("scalar", [st["xg"], ("tt2", S.val("tt2"))] if g == 0 else [],
                      lambda e, s, o=xn_tm[:, i * D:(i + 1) * D], ii=buf[:, g * D:(g + 1) * D],
                      a=col2["s2"][:, i:i + 1]:
                      e.activation(o, ii, AF.Square, accum_out=a), inc="sa2")
            st["sq"] = S.val("sa2")

        def emit_ln_s_ln(gi):
            st = ln_state[gi]
            k0 = 2 * gi
            cs = lambda n: col2[n][:, k0:k0 + 2]
            S.add("scalar", [("vl2", st["cvar"])],
                  lambda e, s, o=cs("u"), i=cs("cv"):
                  e.activation(o, i, AF.Ln, bias=f32(1e-5), scale=f32(1.0 / 1024)), inc="sa2")
            st["ln"] = S.val("sa2")

        def emit_ln_s_exp(gi):
            st = ln_state[gi]
            k0 = 2 * gi
            cs = lambda n: col2[n][:, k0:k0 + 2]
            S.add("scalar", [("sa2", st["ln"])],
                  lambda e, s, o=cs("rs"), i=cs("u"): e.activation(o, i, AF.Exp, scale=f32(-0.5)),
                  inc="sa2")

        def emit_ln_negm(gi):
            st = ln_state[gi]
            k0 = 2 * gi
            cs = lambda n: col2[n][:, k0:k0 + 2]
            S.add("vector", [("sa2", st["exp"])],
                  lambda e, s, o=cs("nm"), i=cs("s1"), r=cs("rs"):
                  e.scalar_tensor_tensor(o, i, -1.0 / 1024, r, ALU.mult, ALU.mult), inc="vl2")
            st["negm"] = S.val("vl2")

        def emit_ln_apply(gi):
            st = ln_state[gi]
            buf = x_grp[gi % 2]
            k0 = 2 * gi
            for g in range(2):
                i = k0 + g
                S.add("scalar", [("vl2", st["negm"])] if g == 0 else [],
                      lambda e, s, o=xn_tm[:, i * D:(i + 1) * D], ii=buf[:, g * D:(g + 1) * D],
                      b=col2["nm"][:, i:i + 1], r=col2["rs"][:, i:i + 1]:
                      e.activation(o, ii, AF.Identity, bias=b, scale=r), inc="sa2")
            st["apply"] = S.val("sa2")
            sa2_apply[gi] = S.val("sa2")

        def emit_transposes(j):
            gate["sc4"] = ("vav", S.val("vav"))
            gate["sc5"] = ("vav", S.val("vav"))
            for i in range(4 * j, 4 * j + 4):
                gi = i // 2
                apply_done = ("sa2", sa2_apply[gi] - 1 + (i % 2))
                dst_ap = xn_fm[:, :].rearrange("p (c t) -> p c t", t=4096)[:, :, i * 128:(i + 1) * 128]
                transpose_tile(xn_tm[:, i * D:(i + 1) * D], dst_ap, 4 + i % 2,
                               apply_done, "tt2", "ve2")

        def emit_B(j, inject):
            E = 4 * j + 4

            def emit_scores(b):
                bank = b % 2

                def scb(e, s, bk=bank, jj=j, bb=b):
                    inst = None
                    for cc in range(CH):
                        inst = e.matmul(pSC[bk][:, 0:256],
                                        xn_fm[:, cc * 4096 + 128 * bb: cc * 4096 + 128 * (bb + 1)],
                                        t_all[:, jj * 2048 + cc * 256: jj * 2048 + (cc + 1) * 256],
                                        start=(cc == 0), stop=(cc == CH - 1))
                    return inst
                S.add("tensor", [("ve2", ve2_of[j])] + gwait("sc%d" % bank), scb, inc="tsc")
                tscv = S.val("tsc")
                k = b - 4 * j
                if k >= 0:
                    S.add("vector", [("tsc", tscv), ("dmk", 16)],
                          lambda e, s, bk=bank, kk=k:
                          e.tensor_tensor(pSC[bk][:, 0:256], pSC[bk][:, 0:256],
                                          maskt[:, 384 - 128 * kk: 640 - 128 * kk], ALU.add),
                          inc="vmk")
                    exp_w = [("vmk", S.val("vmk"))]
                else:
                    exp_w = [("tsc", tscv)]
                pslot = b % 3
                if pt_gate[pslot] is not None:
                    exp_w = exp_w + [pt_gate[pslot]]
                S.add("scalar", exp_w,
                      lambda e, s, bk=bank, ps=pslot:
                      e.activation(ptbuf[:, ps * 256:(ps + 1) * 256], pSC[bk][:, 0:256],
                                   AF.Exp, bias=EBIAS, scale=SCALE), inc="sex")
                exp_of_block[(j, b)] = S.val("sex")
                gate["sc%d" % bank] = ("sex", S.val("sex"))

            def emit_sums_avxt(b):
                pslot = b % 3
                w0 = [("sex", exp_of_block[(j, b)])]
                if b == 0 and j > 0:
                    w0 += [("soe", 2 * j), ("vav", 2 * j)]

                def sums_avxt(e, s, jj=j, bb=b, ps=pslot, first=(b == 0), last=(b == E - 1)):
                    for qt in range(2):
                        e.matmul(pOS[:, qt:qt + 1],
                                 ptbuf[:, ps * 256 + qt * 128: ps * 256 + (qt + 1) * 128],
                                 ones_bf[:, :],
                                 start=(first and qt == 0), stop=(last and qt == 1))
                    inst = None
                    for cc in range(CH):
                        inst = e.matmul(pAV[:, cc * 256:(cc + 1) * 256],
                                        xn_tm[:, bb * D + cc * 128: bb * D + (cc + 1) * 128],
                                        ptbuf[:, ps * 256:(ps + 1) * 256],
                                        start=(first and (cc % 2 == 0)),
                                        stop=(last and (cc % 2 == 1)))
                    return inst
                S.add("tensor", w0, sums_avxt, inc="tav")
                av_of_block[(j, b)] = S.val("tav")
                pt_gate[pslot] = ("tav", S.val("tav"))

            for b in range(E):
                emit_scores(b)
                if b < len(inject):
                    inject[b]()
                if b >= 2:
                    emit_sums_avxt(b - 2)
            for fn in inject[E:]:
                fn()
            emit_sums_avxt(E - 2)
            emit_sums_avxt(E - 1)

            S.add("tensor", [], lambda e, s: e.nop(), inc="tsu")
            for qt in range(2):
                S.add("vector", [("tav", av_of_block[(j, E - 1)])] if qt == 0 else [],
                      lambda e, s, q=qt:
                      e.tensor_copy(
                          avxt[:, :].rearrange("p (c q) -> p c q", q=256)[:, :, q * 128:(q + 1) * 128],
                          pAV[:, :].rearrange("p (c q) -> p c q", q=256)[:, :, q * 128:(q + 1) * 128]),
                      inc="vav")
            vav_q = S.val("vav")
            S.add("vector", [("tsu", S.val("tsu"))],
                  lambda e, s, jj=j: e.reciprocal(recip[:, 2 * jj:2 * jj + 2], pOS[:, 0:2]),
                  inc="vrc")
            for qt in range(2):
                tgt = pOS if qt == 0 else pAVlo
                w = [("vav", vav_q - (1 - qt)), ("vrc", S.val("vrc"))]
                if j == 0 and qt == 0:
                    w.append(wvp_ready)

                def ob(e, s, t=tgt, q=qt):
                    inst = None
                    for oh in range(2):
                        for cc in range(CH):
                            inst = e.matmul(t[:, oh * 512:(oh + 1) * 512],
                                            avxt[:, cc * 256 + q * 128: cc * 256 + (q + 1) * 128],
                                            wv_p[:, cc * D + oh * 512: cc * D + oh * 512 + 512],
                                            start=(cc == 0), stop=(cc == CH - 1))
                    return inst
                S.add("tensor", w, ob, inc="to")
                tov = S.val("to")
                wse = [("to", tov), ("vrc", S.val("vrc")), ("dob", 16 * j)]
                S.add("scalar", wse,
                      lambda e, s, t=tgt, q=qt, jj=j:
                      e.activation(outb[:, q * D:(q + 1) * D],
                                   t[:, 0:D], AF.Identity, scale=recip[:, 2 * jj + q: 2 * jj + q + 1]),
                      inc="soe")
            S.add("gpsimd", [("soe", S.val("soe"))],
                  lambda e, s, jj=j:
                  e.dma_start(out_d[256 * jj:256 * (jj + 1), :].rearrange("(t p) k -> p t k", p=128),
                              outb[:, :].rearrange("p (t k) -> p t k", k=D)),
                  inc="dob", incby=16)

        ve2_of = {}
        emit_load(0)
        emit_load(1)
        for j in range(NSLOT):
            ga, gb = 2 * j, 2 * j + 1
            emit_ln_sq(ga)
            emit_ln_sq(gb)
            ln_state[ga]["exp"] = S.val("sa2") + 2
            ln_state[gb]["exp"] = S.val("sa2") + 4
            emit_ln_vec(ga)
            emit_ln_vec(gb)
            if j + 1 < NSLOT:
                pass
            inject = [lambda a=ga: emit_ln_s_ln(a),
                      lambda a=ga: emit_ln_s_exp(a),
                      lambda a=ga: emit_ln_negm(a),
                      lambda a=gb: emit_ln_s_ln(a),
                      lambda a=gb: emit_ln_s_exp(a),
                      lambda a=gb: emit_ln_negm(a),
                      lambda a=ga: emit_ln_apply(a),
                      lambda a=gb: emit_ln_apply(a)]
            if j == 0:
                for fn in inject:
                    fn()
                emit_transposes(0)
                ve2_of[0] = S.val("ve2")
                if NSLOT > 1:
                    emit_load(2)
                    emit_load(3)
                continue
            emit_B(j - 1, inject)
            if j == 1:
                for (sd, rd) in ((wrs_d, wrT_d), (wks_d, wkTf_d), (wvs_d, wvTf_d)):
                    S.add("gpsimd", [], lambda e, s, d=sd, r=rd: e.dma_start(d[:, :], r[:, :]),
                          inc="dtx", incby=16)
            emit_transposes(j)
            ve2_of[j] = S.val("ve2")
            if j + 1 < NSLOT:
                emit_load(2 * (j + 1))
                emit_load(2 * (j + 1) + 1)
        emit_B(NSLOT - 1, [])

        sc_all = ("tsc", S.val("tsc"))
        av_all = ("tav", S.val("tav"))
        S.add("tensor", [], lambda e, s: e.nop(), inc="tdn")
        b_t_done = ("tdn", S.val("tdn"))
        b_s_done = ("soe", S.val("soe"))
        b_v_done = ("vav", S.val("vav"))

        for t in taps:
            src = {"xnfm": xn_fm, "xntm": xn_tm, "tall": t_all, "m": m_sb, "wvp": wv_p}[t]
            S.add("gpsimd", [b_t_done, b_s_done],
                  lambda e, s, d=tap_d[t], i=src: e.dma_start(d[:, :], i[:, :]),
                  inc="dw", incby=16)

        tap_dw = ("dw", S.val("dw"))
        S.add("gpsimd", [sc_all, tap_dw], lambda e, s: e.dma_start(mur[:, :], mur_d[:, :]),
              inc="dmu", incby=16)
        S.add("gpsimd", [sc_all, tap_dw], lambda e, s: e.dma_start(muk[:, :], muk_d[:, :]),
              inc="dmu", incby=16)
        cmu = ("dmu", 32)
        cwk = []
        for hh in range(2):
            S.add("gpsimd", [sc_all, tap_dw, ("dtx", 48)],
                  lambda e, s, h=hh:
                  e.dma_start(wark[:, :].rearrange("p (c k) -> p c k", k=D)[:, :, h * 512:(h + 1) * 512],
                              wks_d[:, :].rearrange("(c p) k -> p c k", p=128)[:, :, h * 512:(h + 1) * 512]),
                  inc="dk%d" % hh, incby=16)
            cwk.append(("dk%d" % hh, 16))
        for (dst, dr, sm) in ((warr, wrs_d, "dwr2"), (warv, wvs_d, "dvv")):
            S.add("gpsimd", [sc_all, tap_dw, ("dtx", 48)],
                  lambda e, s, d=dst, r=dr:
                  e.dma_start(d[:, :].rearrange("p (c k) -> p c k", k=D),
                              r[:, :].rearrange("(c p) k -> p c k", p=128)),
                  inc=sm, incby=16)
        cwr = ("dwr2", 16)
        cwv = ("dvv", 16)
        S.add("sync", [sc_all, tap_dw],
              lambda e, s: e.dma_start(xprev[:, :], xp_d[:, :]), inc="dx", incby=16)
        cw_done = ("dw", S.val("dw"))

        xpg = ("dx", S.val("dx"))
        ln_group([xprev[:, :]], [xprevt[:, :]], xprev[:, :].rearrange("p (t k) -> p t k", k=D), colc, 16, xpg, "slc", "vlc", pd=8)
        prev_ap = xprevf[:, :].rearrange("p (c q) -> p c q", q=8)

        def prevtb(e, s):
            inst = None
            for cc in range(CH):
                inst = e.matmul(pSCb[0][:, cc * 8:(cc + 1) * 8],
                                xprevt[:, cc * 128:(cc + 1) * 128], ident[0:8, 0:8],
                                is_transpose=True, start=(cc == 0), stop=(cc == CH - 1))
            return inst
        S.add("tensor", [("slc", S.val("slc"))] + gwait("sc0"), prevtb, inc="ttc")
        S.add("vector", [("ttc", S.val("ttc"))],
              lambda e, s: e.tensor_copy(xprevf[:, :], pSCb[0][:, 0:CH * 8]), inc="vec")
        prevf_done = ("vec", S.val("vec"))
        gate["sc0"] = ("vec", S.val("vec"))

        vfn_after = {}
        tf_after = {}
        vka_of = {}
        xpart_of = {}
        cst = {}
        mix_done_of = {}
        ld_of = {}
        vec_after = {-1: ("vec", S.val("vec"))}
        ttc_after = {-1: ("ttc", S.val("ttc"))}

        def c_load(j):
            buf = j % 3
            if j < 3:
                wl = [av_all, tap_dw]
            else:
                wl = [("vfn", xpart_of[j - 3])]
            S.add("sync", wl,
                  lambda e, s, jj=j, b=buf:
                  e.dma_start(xoc[b][:, :].rearrange("p (t k) -> p t k", k=D),
                              xo_d[256 * jj:256 * (jj + 1), :].rearrange("(t p) k -> p t k", p=128)),
                  inc="dci%d" % (j % 3), incby=16)
            ld_of[j] = ("dci%d" % (j % 3), S.val("dci%d" % (j % 3)))
            cst[j] = {"xg": ld_of[j]}

        def c_feed(j):
            st = cst[j]
            buf = j % 2
            if j < 2:
                fw = [av_all, tap_dw]
            else:
                fw = [mix_done_of[j - 2]]
            S.add("sync", fw,
                  lambda e, s, jj=j, b=buf:
                  e.dma_start(
                      xnopf_b[b][:, :].rearrange("p (c q) -> p c q", q=257)[:, :, 1:257],
                      xof_d[:, 2048 * jj:2048 * (jj + 1)].rearrange("p (c q) -> p c q", q=256)),
                  inc="dnf%d" % buf, incby=16)
            st["feed"] = ("dnf%d" % buf, S.val("dnf%d" % buf))
            S.add("vector", [prevf_done],
                  lambda e, s, jj=j, b=buf:
                  e.tensor_copy(xnopf_b[b][:, :].rearrange("p (c q) -> p c q", q=257)[:, :, 0:1],
                                prev_ap[:, :, jj:jj + 1]), inc="vec")
            st["pcol"] = ("vec", S.val("vec"))

        def c_mixes(j):
            st = cst[j]
            buf = j % 2
            xnopf = xnopf_b[buf]
            mix_w = [cmu, st["feed"], st["pcol"]]
            if j >= 1 and mix_done_of.get(j - 1):
                mix_w.append(mix_done_of[j - 1])
            if j >= 2 and (j - 2) in tf_after:
                mix_w.append(tf_after[j - 2])
            cur = xnopf[:, :].rearrange("p (c q) -> p c q", q=257)[:, :, 1:257]
            prv = xnopf[:, :].rearrange("p (c q) -> p c q", q=257)[:, :, 0:256]
            d3 = diffb[:, :].rearrange("p (c q) -> p c q", q=256)
            xk3 = xk_sb[buf][:, :].rearrange("p (c q) -> p c q", q=256)
            xr3 = xr_sb[buf][:, :].rearrange("p (c q) -> p c q", q=256)
            mukb = muk[:, :].rearrange("p (c one) -> p c one", one=1).broadcast_to([128, CH, 256])
            murb = mur[:, :].rearrange("p (c one) -> p c one", one=1).broadcast_to([128, CH, 256])
            S.add("vector", mix_w,
                  lambda e, s: e.tensor_tensor(d3, cur, prv, ALU.subtract), inc="vmx")
            S.add("vector", [("vmx", S.val("vmx"))],
                  lambda e, s: e.tensor_tensor(xk3, d3, mukb, ALU.mult), inc="vmx")
            S.add("vector", [("vmx", S.val("vmx"))],
                  lambda e, s: e.tensor_tensor(xk3, xk3, prv, ALU.add), inc="vmx")
            xk_done_of = ("vmx", S.val("vmx"))
            S.add("vector", [("vmx", S.val("vmx") - 1)],
                  lambda e, s: e.tensor_tensor(xr3, d3, murb, ALU.mult), inc="vmx")
            S.add("vector", [("vmx", S.val("vmx"))],
                  lambda e, s: e.tensor_tensor(xr3, xr3, prv, ALU.add), inc="vmx")
            mix_done_of[j] = ("vmx", S.val("vmx"))
            cst[j]["xk_done"] = xk_done_of

        def c_stage_b(j, inject, midfns=()):
            buf = j % 2
            mix_done = mix_done_of[j]
            ld = ld_of[j]
            pw = ([("vfn", xpart_of[j - 1])] if j >= 1 else [av_all, tap_dw]) + [("dob", 16 * (j + 1))]
            S.add("sync", pw,
                  lambda e, s, jj=j:
                  e.dma_start(parts[:, :].rearrange("p (t k) -> p t k", k=D),
                              out_d[256 * jj:256 * (jj + 1), :].rearrange("(t p) k -> p t k", p=128)),
                  inc="dcp", incby=16)
            pld = ("dcp", S.val("dcp"))
            ip = 0

            def inj():
                nonlocal ip
                if ip < len(inject):
                    inject[ip]()
                    ip += 1

            if j == 0:
                gate["sc2"] = b_s_done
                gate["sc3"] = b_s_done
            for dc in range(CH):
                bank = dc % 4
                pb = pool[:, 512 * bank:512 * bank + 512]

                def kb(e, s, b=pb, d=dc, bu=buf):
                    inst = None
                    for cr in range(CH):
                        inst = e.matmul(b[:, 0:256],
                                        wark[:, cr * D + d * 128: cr * D + (d + 1) * 128],
                                        xk_sb[bu][:, cr * 256:(cr + 1) * 256],
                                        start=(cr == 0), stop=(cr == CH - 1))
                    return inst
                S.add("tensor", [cst[j]["xk_done"], cwk[0] if dc < 4 else cwk[1]] + gwait("sc%d" % bank), kb, inc="tk")
                S.add("scalar", [("tk", S.val("tk"))],
                      lambda e, s, b=pb, d=dc:
                      e.activation(rk_sb[:, d * 256:(d + 1) * 256], b[:, 0:256], AF.Relu),
                      inc="srk")
                gate["sc%d" % bank] = ("srk", S.val("srk"))
                if dc == 3:
                    sq_w = [("srk", S.val("srk"))]
                    if j >= 2:
                        sq_w.append(tf_after[j - 2])
                    S.add("scalar", sq_w,
                          lambda e, s, bu=buf:
                          e.activation(kact[bu][:, 0:4 * 256], rk_sb[:, 0:4 * 256],
                                       AF.Square), inc="srk")
                inj()
            S.add("scalar", [("srk", S.val("srk"))],
                  lambda e, s, bu=buf:
                  e.activation(kact[bu][:, 4 * 256:], rk_sb[:, 4 * 256:], AF.Square),
                  inc="srk")
            kact_done = ("srk", S.val("srk"))
            for qt in range(2):
                tgt = pOS if qt == 0 else pAVlo
                w = [mix_done, cwr]
                if j > 0:
                    w.append(("vfn", vfn_after[j - 1] - 1) if qt == 0 else ("ssg", 2 * j))
                elif qt == 0:
                    w.append(b_s_done)
                if qt == 1:
                    w += gwait("sc2") + gwait("sc3")

                def rb(e, s, t=tgt, q=qt, bu=buf):
                    inst = None
                    for oh in range(2):
                        for cr in range(CH):
                            inst = e.matmul(t[:, oh * 512:(oh + 1) * 512],
                                            xr_sb[bu][:, cr * 256 + q * 128: cr * 256 + (q + 1) * 128],
                                            warr[:, cr * D + oh * 512: cr * D + oh * 512 + 512],
                                            start=(cr == 0), stop=(cr == CH - 1))
                    return inst
                S.add("tensor", w, rb, inc="tr")
                if qt == 0:
                    while ip < len(inject):
                        inj()
                    for fn in midfns:
                        fn()
                sg_w = [("tr", S.val("tr"))]
                if j >= 1:
                    sg_w.append(("vfn", vfn_after[j - 1] - 1))
                S.add("scalar", sg_w,
                      lambda e, s, t=tgt, q=qt:
                      e.activation(r_sb[:, q * D:(q + 1) * D], t[:, 0:D], AF.Sigmoid), inc="ssg")
                if qt == 1:
                    gate["sc2"] = ("ssg", S.val("ssg"))
                    gate["sc3"] = ("ssg", S.val("ssg"))
            for qt in range(2):
                tgt = pAVhi if qt == 0 else pOS
                w = [kact_done, cwv]
                if qt == 0 and j > 0:
                    w.append(("vfn", vfn_after[j - 1] - 3))
                if qt == 0 and j == 0:
                    w.append(b_v_done)
                if qt == 1:
                    w.append(("ssg", S.val("ssg") - 1))

                def fb3(e, s, t=tgt, q=qt, bu=buf):
                    inst = None
                    for oh in range(2):
                        for dc in range(CH):
                            inst = e.matmul(t[:, oh * 512:(oh + 1) * 512],
                                            kact[bu][:, dc * 256 + q * 128: dc * 256 + (q + 1) * 128],
                                            warv[:, dc * D + oh * 512: dc * D + oh * 512 + 512],
                                            start=(dc == 0), stop=(dc == CH - 1))
                    return inst
                S.add("tensor", w, fb3, inc="tf")
            tf_q0 = S.val("tf") - 1
            tf_q1 = S.val("tf")
            tf_after[j] = ("tf", tf_q1)
            S.add("vector", [ld, pld, ("vfn", S.val("vfn"))],
                  lambda e, s, b=j % 3:
                  e.tensor_tensor(xpart[:, :], parts[:, :], xoc[b][:, :], ALU.add),
                  inc="vfn")
            xpart_of[j] = S.val("vfn")
            for qt in range(2):
                src = pAVhi if qt == 0 else pOS
                w = [("tf", tf_q0 if qt == 0 else tf_q1), ("ssg", S.val("ssg") - (1 - qt)),
                     ("vfn", S.val("vfn"))]
                if j >= 2:
                    w.append(("doc%d" % (j % 2), 16 * (j // 2)))
                S.add("vector", w,
                      lambda e, s, t=src, q=qt:
                      e.tensor_tensor(fbuf[:, :], t[:, 0:D], r_sb[:, q * D:(q + 1) * D],
                                      ALU.mult), inc="vfn")
                S.add("vector", [("vfn", S.val("vfn"))],
                      lambda e, s, q=qt, jj=j:
                      e.tensor_tensor(outc[jj % 2][:, q * D:(q + 1) * D], fbuf[:, :],
                                      xpart[:, q * D:(q + 1) * D], ALU.add), inc="vfn")
            vfn_after[j] = S.val("vfn")
            S.add("gpsimd", [("vfn", S.val("vfn"))],
                  lambda e, s, jj=j:
                  e.dma_start(out_d[256 * jj:256 * (jj + 1), :].rearrange("(t p) k -> p t k", p=128),
                              outc[jj % 2][:, :].rearrange("p (t k) -> p t k", k=D)),
                  inc="doc%d" % (j % 2), incby=16)

        c_load(0)
        c_load(1)
        c_feed(0)
        c_feed(1)
        c_mixes(0)
        c_mixes(1)
        for j in range(NSLOT):
            nx = j + 2
            if nx < NSLOT:
                c_load(nx)
                c_feed(nx)
            if j + 1 < NSLOT and j + 1 >= 2:
                midfns = [lambda a=j + 1: c_mixes(a)]
            else:
                midfns = []
            c_stage_b(j, [], midfns)

        S.add("sync", [("doc0", 64), ("doc1", 64), ("dob", 16 * NSLOT)],
              lambda e, s: e.nop(), inc=None)

        with nc.Block() as block:
            def runner(oplist):
                def f(eng):
                    for op in oplist:
                        op(eng, sems)
                return f
            block.sync(runner(S.ops["sync"]))
            block.gpsimd(runner(S.ops["gpsimd"]))
            block.scalar(runner(S.ops["scalar"]))
            block.vector(runner(S.ops["vector"]))
            block.tensor(runner(S.ops["tensor"]))

    return nc


def _host_inputs(x, Wq, Wk, Wv, Wo, W_r, W_k, W_v, mu_r, mu_k):
    x = np.asarray(x, dtype=np.float32)
    c = np.ascontiguousarray
    shared = {
        "wq": c(np.asarray(Wq, np.float32).T),
        "wk": c(np.asarray(Wk, np.float32).T),
        "wvT": c(np.asarray(Wv, np.float32).T),
        "wo": c(np.asarray(Wo, np.float32)),
        "wrT": c(np.asarray(W_r, np.float32).T),
        "wkTf": c(np.asarray(W_k, np.float32).T),
        "wvTf": c(np.asarray(W_v, np.float32).T),
        "mur": c(np.asarray(mu_r, np.float32).reshape(8, 128).T),
        "muk": c(np.asarray(mu_k, np.float32).reshape(8, 128).T),
    }
    in_maps = []
    own_rows_all = []
    for core in range(8):
        b, par = core // 2, core % 2
        xb = c(x[b])
        rows = np.concatenate([np.arange(512 * j + 256 * par, 512 * j + 256 * par + 256)
                               for j in range(8)])
        own_rows_all.append(rows)
        x_own = c(xb[rows])
        xprev = np.zeros((8, 1024), np.float32)
        for j in range(8):
            t = 512 * j + 256 * par - 1
            if t >= 0:
                xprev[j] = xb[t]
        pp = np.arange(128)[:, None]
        cc_ = np.arange(640)[None, :]
        mask = np.where(cc_ < pp + 384 - 256 * par, -1e9, 0.0).astype(np.float32)
        im = dict(shared)
        im["x"] = xb
        im["x_own"] = x_own
        im["x_prev"] = c(xprev)
        im["maskt"] = c(mask)
        in_maps.append(im)
    return in_maps, own_rows_all


def kernel(L0_attn_x, L0_attn_Wq, L0_attn_Wk, L0_attn_Wv, L0_attn_Wo,
           L0_ffn_W_r, L0_ffn_W_k, L0_ffn_W_v, L0_ffn_mu_r, L0_ffn_mu_k):
    in_maps, own_rows = _host_inputs(L0_attn_x, L0_attn_Wq, L0_attn_Wk, L0_attn_Wv,
                                     L0_attn_Wo, L0_ffn_W_r, L0_ffn_W_k, L0_ffn_W_v,
                                     L0_ffn_mu_r, L0_ffn_mu_k)
    nc = build()
    res = run_bass_kernel_spmd(nc, in_maps, core_ids=list(range(8)))
    out = np.empty((4, 4096, 1024), np.float32)
    for core in range(8):
        b = core // 2
        out[b][own_rows[core]] = res.results[core]["out"]
    return out
